# version 06
# baseline (speedup 1.0000x reference)
import numpy as np
import ml_dtypes
import concourse.bass as bass
import concourse.mybir as mybir
from concourse.bass_utils import run_bass_kernel_spmd

F32 = mybir.dt.float32
BF16 = mybir.dt.bfloat16
AF = mybir.ActivationFunctionType
ALU = mybir.AluOpType
BF = ml_dtypes.bfloat16

B, N, D = 4, 4096, 1024
H, HD = 16, 64
PF = 256
WIN = 256
DI = 2730
EPS_LN = 1e-5
EPS_ATTN = 1e-8
NCH = D // 128

L_TOK = 3584
S_L = 2304
ORG_H = N - L_TOK
D_L = L_TOK - S_L
W_BOUND = D_L // WIN
OUT0 = 244

ROT_PERM = np.arange(64).reshape(2, 32).T.reshape(-1)
SHUF_MASK = [i for p in range(16) for i in (2 * p + 1, 2 * p)]

TBUF = 1024
FFNP = 896


def ceil_div(a, b):
    return -(-a // b)


def even(x):
    return x - (x % 2)


def layer_ranges():
    out = []
    V = OUT0
    for i in range(6):
        A = V + 257
        wf = A // WIN
        attn_lo = min(D_L, wf * WIN)
        F = min(D_L, even(A + 1))
        out.append((attn_lo, F))
        V = A + 1
    assert V <= D_L + (N - ORG_H - S_L), ("pyramid chain too deep", V)
    return out


class Ker:

    ENGS = ("pe", "act", "dve", "pool", "sp")

    def __init__(self, nc):
        self.nc = nc
        self.eng = {
            "pe": nc.tensor, "act": nc.scalar, "dve": nc.vector,
            "pool": nc.gpsimd, "sp": nc.sync,
        }
        self._stack = []
        self.prog = {}
        self.count = {}
        for e in self.ENGS:
            self.prog[e] = self._enter(nc.semaphore(f"prog_{e}"))
            self.count[e] = 0
        self.dmasem = {}
        self.dmacnt = {}
        self.dmaout = {}
        self.waited = {e: {} for e in self.ENGS}

    def _enter(self, cm):
        v = cm.__enter__()
        self._stack.append(cm)
        return v

    def sbuf(self, name, shape, dtype):
        return Buf(self._enter(self.nc.sbuf_tensor(name, list(shape), dtype)))

    def psum(self, name, shape, dtype=F32):
        return Buf(self._enter(self.nc.psum_tensor(name, list(shape), dtype)))

    def wait(self, e, tok):
        if tok is None:
            return
        sem, val = tok
        if self.waited[e].get(sem.name, 0) >= val:
            return
        self.eng[e].wait_ge(sem, val)
        self.waited[e][sem.name] = val

    def _collect(self, e, deps, rd, wr):
        need = {}

        def add(tok):
            if tok is None:
                return
            sem, val = tok
            if self.waited[e].get(sem.name, 0) >= val:
                return
            if need.get(sem.name, (None, 0))[1] < val:
                need[sem.name] = (sem, val)
        for b in rd:
            add(b.w)
        for b in wr:
            add(b.w)
            for t in b.r:
                add(t)
        for t in deps:
            add(t)
        return list(need.values())

    def _pre(self, e, pend):
        for sem, val in pend[:-1]:
            self.eng[e].wait_ge(sem, val)
            self.waited[e][sem.name] = val
        if pend:
            sem, val = pend[-1]
            self.waited[e][sem.name] = val
            return (sem, val)
        return None

    def _post(self, tok, rd, wr):
        for b in rd:
            b.r.append(tok)
        for b in wr:
            b.w = tok
            b.r = []
        return tok

    def op(self, e, name, *args, deps=(), rd=(), wr=(), **kw):
        att = self._pre(e, self._collect(e, deps, rd, wr))
        inst = getattr(self.eng[e], name)(*args, **kw)
        if att is not None:
            inst._wait_ge(att[0], att[1])
        self.count[e] += 1
        tok = (self.prog[e], self.count[e])
        inst.then_inc(self.prog[e], 1)
        return self._post(tok, rd, wr)

    def dma(self, q, out, in_, key=None, deps=(), rd=(), wr=(), slow=False):
        e = q
        key = key if key is not None else q
        if key not in self.dmasem:
            self.dmasem[key] = self._enter(self.nc.semaphore(f"dma_{key}"))
            self.dmacnt[key] = 0
            self.dmaout[key] = None
        pend = self._collect(e, (self.dmaout[key],) + tuple(deps), rd, wr)
        att = self._pre(e, pend)
        inst = self.eng[e].dma_start(out=out, in_=in_,
                                     allow_slow_non_contiguous=slow)
        if att is not None:
            inst._wait_ge(att[0], att[1])
        self.dmacnt[key] += 16
        tok = (self.dmasem[key], self.dmacnt[key])
        inst.then_inc(self.dmasem[key], 16)
        self.dmaout[key] = tok
        return self._post(tok, rd, wr)

    def barrier_tokens(self):
        toks = [(self.prog[e], self.count[e]) for e in self.ENGS]
        toks += [(self.dmasem[kq], self.dmacnt[kq]) for kq in self.dmasem]
        return [t for t in toks if t[1] > 0]


class Buf:
    def __init__(self, h):
        self.h = h
        self.w = None
        self.r = []

    def __getitem__(self, idx):
        return self.h[idx]

    def ap(self):
        return self.h.ap()


class DBuf(Buf):
    def __init__(self, ap):
        self.a = ap
        self.w = None
        self.r = []

    def __getitem__(self, idx):
        return self.a[idx]

    def ap(self):
        return self.a


class PsumSlice:
    def __init__(self, t, col0):
        self.t = t
        self.col0 = col0
        self.w = None
        self.r = []

    def __getitem__(self, idx):
        p, c = idx
        c0 = self.col0 + (c.start or 0)
        c1 = self.col0 + (512 if c.stop is None else c.stop)
        return self.t[p, c0:c1]

    def ap(self):
        return self.t.ap()[:, self.col0: self.col0 + 512]


class PsumPool:
    def __init__(self, k):
        self.big = k.psum("psA", (128, 4096), F32)
        self.banks = [PsumSlice(self.big.h, i * 512) for i in range(8)]
        self.i = 0

    def take(self):
        b = self.banks[self.i % 8]
        self.i += 1
        return b

    def take_contig(self, n):
        s = self.i % 8
        if s + n > 8:
            self.i += 8 - s
            s = 0
        self.i += n
        return [self.banks[s + j] for j in range(n)]

    def span_ap(self, b0, nbanks, prow, ncols_per, stride=512):
        return self.big.h.ap()[prow[0]:prow[1],
                               b0.col0: b0.col0 + nbanks * 512].rearrange(
            "p (b c) -> p b c", b=nbanks)[:, :, 0:ncols_per]


def mm(k, pb, pslice, lh, rh, rd=(), start=False, stop=False):
    if start:
        att = k._pre("pe", k._collect("pe", (pb.w,) + tuple(pb.r), rd, ()))
    else:
        att = k._pre("pe", k._collect("pe", (), rd, ()))
    inst = k.eng["pe"].matmul(pslice, lh, rh, start=start, stop=stop)
    if att is not None:
        inst._wait_ge(att[0], att[1])
    k.count["pe"] += 1
    tok = (k.prog["pe"], k.count["pe"])
    inst.then_inc(k.prog["pe"], 1)
    for b in rd:
        b.r.append(tok)
    if start:
        pb.r = []
    pb.w = tok
    return tok


def rotary_tables_np(origin, L):
    half = np.arange(0, HD, 2, dtype=np.float64)
    inv_freq = 1.0 / 10000.0 ** (half / HD)
    t = np.arange(origin, origin + L, dtype=np.float64)
    freqs = np.concatenate([t[:, None] * inv_freq[None, :]] * 2, -1)
    base_scale = (half + 0.4 * HD) / (1.4 * HD)
    power = (t - N // 2) / (WIN // 2)
    scale = np.concatenate([base_scale[None, :] ** power[:, None]] * 2, -1)
    cos, sin = np.cos(freqs), np.sin(freqs)
    sign = np.where(np.arange(HD) % 2 == 0, -1.0, 1.0)
    tabs = []
    for sc in (scale, 1.0 / scale):
        tabs.append((cos * sc)[:, ROT_PERM])
        tabs.append((sin * sc)[:, ROT_PERM] * sign[None, :])
    return [np.ascontiguousarray(np.tile(a.T, (2, 1))).astype(np.float32)
            for a in tabs]


def perm_wqkv_np(wqkv):
    w = wqkv.copy()
    for h in range(H):
        w[:, h * HD: (h + 1) * HD] = wqkv[:, h * HD + ROT_PERM]
        w[:, D + h * HD: D + (h + 1) * HD] = wqkv[:, D + h * HD + ROT_PERM]
    return np.ascontiguousarray(w)


def scale_cols_np(qs, ks):
    pair = np.arange(HD).reshape(32, 2)[:, ::-1].reshape(-1)
    qi, ki = qs[ROT_PERM], ks[ROT_PERM]
    cols = np.stack([qi, qi[pair], ki, ki[pair]], 1)
    return np.ascontiguousarray(np.tile(cols, (2, 1))).astype(np.float32)


def band_masks_np(key_g0=None, blk_key0_g=None, blk_q0_g=None):
    q = np.arange(512)
    out = np.zeros((128, 6 * 512), np.float32)
    for kc in range(6):
        j = np.arange(128) + kc * 128
        m = (q[None, :] <= j[:, None]) & (j[:, None] <= q[None, :] + 256)
        if key_g0 is not None:
            keybad = (blk_key0_g + j[:, None]) < key_g0
            qreal = (blk_q0_g + q[None, :]) >= 0
            m = m & ~(keybad & qreal)
        out[:, kc * 512:(kc + 1) * 512] = np.where(m, 0.0, -1e5)
    return out


def chunk_cols_np(v):
    nch = len(v) // 128
    return np.ascontiguousarray(v.reshape(nch, 128).T).astype(np.float32)


def load_w_ff(k, wtile, wsrc, din, mc, mcols=128, key="w", base_col=0,
              q="sp"):
    kc_n = din // 128
    src = wsrc.rearrange("(kc p) m -> p kc m", p=128)
    src = src[:, :, base_col + mc * 128: base_col + mc * 128 + mcols]
    dst = wtile.h.ap()[:, 0:kc_n * mcols].rearrange(
        "p (kc m) -> p kc m", kc=kc_n)
    return k.dma(q, dst, src, key=key, wr=(wtile,))


def emit_ln(k, pp, st, xbf, tbuf, t0, tn, out_bf, C, gcol=None, bcol=None,
            eps=EPS_LN):
    sq0, sq1, row, tmp = st["sq0"], st["sq1"], st["row"], st["tmp"]
    ones2 = C["ones2"]
    nt = ceil_div(tn, 512)
    for j in range(nt):
        c0, cn = j * 512, min(512, tn - j * 512)
        ones_all = C["ones64"]
        pa = pp.take()
        for kc in range(NCH):
            mm(k, pa, pa[:1, :cn], ones_all[:, 0:1],
               xbf[:, kc * tbuf + t0 + c0: kc * tbuf + t0 + c0 + cn],
               rd=(ones_all, xbf), start=(kc == 0), stop=(kc == NCH - 1))
        pb = pp.take()
        for kc in range(NCH):
            sq = (sq0, sq1)[kc % 2]
            k.op("act", "activation", sq[:, :cn],
                 xbf[:, kc * tbuf + t0 + c0: kc * tbuf + t0 + c0 + cn],
                 AF.Square, rd=(xbf,), wr=(sq,))
            mm(k, pb, pb[:1, :cn], ones_all[:, 0:1], sq[:, :cn],
               rd=(ones_all, sq), start=(kc == 0), stop=(kc == NCH - 1))
        mu = row[:1, c0: c0 + cn]
        va = row[:1, tn + c0: tn + c0 + cn]
        t1 = tmp[:1, :cn]
        k.op("dve", "tensor_scalar_mul", mu, pa[:1, :cn], 1.0 / D,
             rd=(pa,), wr=(row,))
        k.op("dve", "tensor_scalar_mul", va, pb[:1, :cn], 1.0 / D,
             rd=(pb,), wr=(row,))
        k.op("dve", "tensor_mul", t1, mu, mu, rd=(row,), wr=(tmp,))
        k.op("dve", "tensor_sub", va, va, t1, rd=(row, tmp), wr=(row,))
        k.op("dve", "tensor_scalar_add", va, va, eps, rd=(row,), wr=(row,))
        k.op("act", "activation", va, va, AF.Sqrt, rd=(row,), wr=(row,))
        k.op("dve", "reciprocal", va, va, rd=(row,), wr=(row,))
        pm = pp.take()
        mm(k, pm, pm[:, :cn], C["onesf"][:1, 0:128], mu,
           rd=(C["onesf"], row), start=True, stop=True)
        pr = pp.take()
        mm(k, pr, pr[:, :cn], C["onesf"][:1, 0:128], va,
           rd=(C["onesf"], row), start=True, stop=True)
        for kc in range(NCH):
            xs = xbf[:, kc * tbuf + t0 + c0: kc * tbuf + t0 + c0 + cn]
            os_ = out_bf[:, kc * tbuf + t0 + c0: kc * tbuf + t0 + c0 + cn]
            tm = tmp[:, :cn]
            k.op("dve", "tensor_sub", tm, xs, pm[:, :cn], rd=(xbf, pm),
                 wr=(tmp,))
            if gcol is None:
                k.op("dve", "tensor_mul", os_, tm, pr[:, :cn], rd=(tmp, pr),
                     wr=(out_bf,))
            else:
                gb, gc = gcol
                bb, bc = bcol
                k.op("dve", "tensor_mul", tm, tm, pr[:, :cn], rd=(tmp, pr),
                     wr=(tmp,))
                k.op("act", "activation", os_, tm, AF.Identity,
                     bias=bb[:, bc + kc: bc + kc + 1],
                     scale=gb[:, gc + kc: gc + kc + 1],
                     rd=(tmp, gb, bb), wr=(out_bf,))


def emit_shift_cast(k, xs_bf, tbuf, dcol, src, src_tbuf, src_off, cn,
                    flag=None, fcol=None):
    dst3 = xs_bf.ap()[:, 0: NCH * tbuf].rearrange(
        "p (kc t) -> p kc t", kc=NCH)
    src3 = src.ap()[:, 0: NCH * src_tbuf].rearrange(
        "p (kc t) -> p kc t", kc=NCH)
    for half in range(2):
        sh = half
        s0 = src_off - sh
        if s0 >= 0:
            k.op("act", "copy",
                 dst3[:, 4 * half: 4 * half + 4, dcol: dcol + cn],
                 src3[:, 4 * half: 4 * half + 4, s0: s0 + cn],
                 rd=(src,), wr=(xs_bf,))
        else:
            k.op("act", "copy",
                 dst3[:, 4 * half: 4 * half + 4, dcol + 1: dcol + cn],
                 src3[:, 4 * half: 4 * half + 4, 0: cn - 1],
                 rd=(src,), wr=(xs_bf,))
            k.op("act", "copy",
                 dst3[:, 4 * half: 4 * half + 4, dcol: dcol + 1],
                 src3[:, 4 * half: 4 * half + 4, 0: 1],
                 rd=(src,), wr=(xs_bf,))
    if flag is not None and fcol is not None:
        for kc in range(4, NCH):
            cpos = kc * tbuf + dcol + fcol
            k.op("dve", "tensor_scalar_mul", xs_bf[:, cpos: cpos + 1],
                 xs_bf[:, cpos: cpos + 1], flag.ap()[:, 0:1],
                 rd=(xs_bf, flag), wr=(xs_bf,))


def emit_qkv(k, pp, Bf, C, xs_bf, xs_str, rn, t_dst, wq_src):
    q_bf, k_bf, v_tm = Bf["qb"], Bf["kb"], Bf["vt"]
    wt = [Bf["wt0"], Bf["wt1"]]
    stg = Bf["stg0"]
    srow, sq = Bf["srow"], Bf["sq"]
    rt1, rt2 = Bf["rt1"], Bf["rt2"]
    ones2, sel2 = C["ones2"], C["sel2"]
    tabs = Bf["tabs"]
    KBUF = 3 * WIN
    for qk, tbase in (("q", 0), ("k", 2)):
        base_mc = 0 if qk == "q" else NCH
        load_w_ff(k, wt[0], wq_src, D, base_mc, 128, key="wq0")
        for c in range(NCH):
            if c + 1 < NCH:
                load_w_ff(k, wt[(c + 1) % 2], wq_src, D, base_mc + c + 1,
                          128, key="wq" + str((c + 1) % 2))
            wtile = wt[c % 2]
            for j in range(ceil_div(rn, 512)):
                c0, cn = j * 512, min(512, rn - j * 512)
                pq = pp.take()
                for kc in range(NCH):
                    mm(k, pq, pq[:, :cn], wtile[:, kc * 128: kc * 128 + 128],
                       xs_bf[:, kc * xs_str + c0: kc * xs_str + c0 + cn],
                       rd=(wtile, xs_bf), start=(kc == 0),
                       stop=(kc == NCH - 1))
                ps = pp.take()
                k.op("act", "activation", sq[:, :cn], pq[:, :cn], AF.Square,
                     rd=(pq,), wr=(sq,))
                mm(k, ps, ps[:2, :cn], ones2[:, 0:2], sq[:, :cn],
                   rd=(ones2, sq), start=True, stop=True)
                rr = srow[:2, c0: c0 + cn]
                k.op("act", "activation", rr, ps[:2, :cn], AF.Sqrt,
                     rd=(ps,), wr=(srow,))
                k.op("dve", "tensor_scalar_max", rr, rr, 1e-12, rd=(srow,),
                     wr=(srow,))
                k.op("dve", "reciprocal", rr, rr, rd=(srow,), wr=(srow,))
                pbx = pp.take()
                mm(k, pbx, pbx[:, :cn], sel2[:2, :], rr, rd=(sel2, srow),
                   start=True, stop=True)
                tcs = tabs[:, tbase * 512 + c0: tbase * 512 + c0 + cn]
                tss = tabs[:, (tbase + 1) * 512 + c0:
                           (tbase + 1) * 512 + c0 + cn]
                k.op("dve", "tensor_mul", rt1[:, :cn], pq[:, :cn], tcs,
                     rd=(pq, tabs), wr=(rt1,))
                k.op("dve", "stream_shuffle", rt2[:, :cn], pq[:, :cn],
                     SHUF_MASK, rd=(pq,), wr=(rt2,))
                k.op("pool", "tensor_mul", rt2[:, :cn], rt2[:, :cn], tss,
                     rd=(rt2, tabs), wr=(rt2,))
                k.op("dve", "tensor_add", rt1[:, :cn], rt1[:, :cn],
                     rt2[:, :cn], rd=(rt1, rt2), wr=(rt1,))
                if qk == "q":
                    k.op("dve", "tensor_mul", q_bf[:, c * 512 + c0:
                                                   c * 512 + c0 + cn],
                         rt1[:, :cn], pbx[:, :cn], rd=(rt1, pbx), wr=(q_bf,))
                else:
                    w0 = (t_dst + c0) % KBUF
                    n1 = min(cn, KBUF - w0)
                    k.op("dve", "tensor_mul",
                         k_bf[:, c * KBUF + w0: c * KBUF + w0 + n1],
                         rt1[:, :n1], pbx[:, :n1], rd=(rt1, pbx), wr=(k_bf,))
                    if n1 < cn:
                        k.op("dve", "tensor_mul",
                             k_bf[:, c * KBUF: c * KBUF + cn - n1],
                             rt1[:, n1:cn], pbx[:, n1:cn], rd=(rt1, pbx),
                             wr=(k_bf,))
    wv = Bf["wv"]
    k.dma("sp", wv[:, 0:NCH * 1024].rearrange("p (kc m) -> p kc m", kc=NCH),
          wq_src.rearrange("(kc p) m -> p kc m", p=128)[:, :, 2 * D: 3 * D],
          key="wv", wr=(wv,))
    for tc in range(ceil_div(rn, 128)):
        cn = min(128, rn - tc * 128)
        slot = ((t_dst + tc * 128) // 128) % 6
        for half in range(2):
            pv = pp.take()
            for kc in range(NCH):
                mm(k, pv, pv[:cn, :512],
                   xs_bf[:, kc * xs_str + tc * 128: kc * xs_str + tc * 128 + cn],
                   wv[:, kc * 1024 + half * 512:
                      kc * 1024 + half * 512 + 512],
                   rd=(xs_bf, wv), start=(kc == 0), stop=(kc == NCH - 1))
            k.op("act", "copy",
                 v_tm[:cn, slot * 1024 + half * 512:
                      slot * 1024 + half * 512 + 512],
                 pv[:cn, :512], rd=(pv,), wr=(v_tm,))


def emit_block_attn(k, pp, Bf, C, wq0, nw, q_lo, q_hi, mask, mcol0):
    q_bf, k_bf, v_tm, attn_bf = Bf["qb"], Bf["kb"], Bf["vt"], Bf["at"]
    Pb, rc = Bf["p0"], Bf["rec0"]
    ones64 = C["ones64"]
    m8 = C["m8"]
    KBUF = 3 * WIN
    qa = max(q_lo, wq0 * WIN)
    qb_ = min(q_hi, (wq0 + nw) * WIN)
    qn = qb_ - qa
    qrel = qa - wq0 * WIN
    nkc = 2 * (nw + 1)
    for h in range(H):
        c, hp = h // 2, h % 2
        plo, phi = (0, 64) if hp == 0 else (64, 128)
        pss = pp.take_contig(nkc)
        pd = pp.take()
        pvp = pp.take()
        for kc in range(nkc):
            key0 = (wq0 - 1) * WIN + kc * 128
            kcol = (key0 % KBUF)
            mm(k, pss[kc], pss[kc][:, :qn],
               k_bf[plo:phi, c * KBUF + kcol: c * KBUF + kcol + 128],
               q_bf[plo:phi, c * 512 + qa - wq0 * WIN:
                    c * 512 + qa - wq0 * WIN + qn],
               rd=(k_bf, q_bf), start=True, stop=True)
        sspan = pp.span_ap(pss[0], nkc, (0, 128), qn)
        mspan = mask.ap()[:, mcol0: mcol0 + nkc * 512].rearrange(
            "p (b c) -> p b c", b=nkc)[:, :, qrel: qrel + qn]
        pbspan = Pb.ap()[:, 0: nkc * 512].rearrange(
            "p (b c) -> p b c", b=nkc)[:, :, 0:qn]
        k.op("dve", "tensor_add", sspan, sspan, mspan,
             rd=(mask,) + tuple(pss), wr=tuple(pss))
        k.op("act", "activation", pbspan, sspan, AF.Exp,
             bias=m8[:, 0:1], scale=m8[:, 1:2],
             rd=tuple(pss) + (m8,), wr=(Pb,))
        for kc in range(nkc):
            key0 = (wq0 - 1) * WIN + kc * 128
            mm(k, pd, pd[plo:phi, :qn], ones64[:, 0:64],
               Pb[:, kc * 512: kc * 512 + qn], rd=(ones64, Pb),
               start=(kc == 0), stop=(kc == nkc - 1))
            mm(k, pvp, pvp[plo:phi, :qn],
               v_tm[:, (key0 // 128 % 6) * 1024 + h * 64:
                    (key0 // 128 % 6) * 1024 + h * 64 + 64],
               Pb[:, kc * 512: kc * 512 + qn], rd=(v_tm, Pb),
               start=(kc == 0), stop=(kc == nkc - 1))
        k.op("dve", "reciprocal", rc[plo:phi, :qn], pd[plo:phi, :qn],
             rd=(pd,), wr=(rc,))
        k.op("dve", "tensor_mul",
             attn_bf[plo:phi, c * 512 + qa - wq0 * WIN:
                     c * 512 + qa - wq0 * WIN + qn],
             pvp[plo:phi, :qn], rc[plo:phi, :qn], rd=(pvp, rc),
             wr=(attn_bf,))


def emit_wo_residual(k, pp, Bf, tn, wo_t, x_load, x_out):
    attn_bf, wob = Bf["at"], Bf["xacc"]
    pbs = []
    for mc in range(NCH):
        pb = pp.take()
        for kc in range(NCH):
            mm(k, pb, pb[:, :tn],
               wo_t[:, kc * 1024 + mc * 128: kc * 1024 + mc * 128 + 128],
               attn_bf[:, kc * 512: kc * 512 + tn],
               rd=(wo_t, attn_bf), start=(kc == 0), stop=(kc == NCH - 1))
        pbs.append(pb)
    for j in range(ceil_div(tn, 256)):
        c0, cn = j * 256, min(256, tn - j * 256)
        xres, xsl = x_load(c0, cn)
        for mc in range(NCH):
            k.op("dve", "tensor_add",
                 wob[:, mc * 512 + c0: mc * 512 + c0 + cn],
                 pbs[mc][:, c0: c0 + cn], xsl(mc, cn),
                 rd=(pbs[mc], xres), wr=(wob,))
    x_out(wob, tn)


def emit_ffn(k, pp, Bf, C, xn_bf, tbuf, tn, w1src, w2src, b1t, b2col,
             x_src, x_out):
    JN = ceil_div(DI, 128)
    w1a = [Bf["wt0"], Bf["wt1"]]
    w1g = [Bf["wg0"], Bf["wg1"]]
    w2t, hg, xacc, xnew, gbf = (Bf["vt"], Bf["kb"], Bf["xacc"], Bf["xnew"],
                                Bf["gbf"])
    groups = [list(range(0, 11)), list(range(11, JN))]
    for gi, grp in enumerate(groups):
        for jj, j in enumerate(grp):
            rj = min(128, DI - 128 * j)
            wa, wg = w1a[jj % 2], w1g[jj % 2]
            srca = w1src.rearrange("(kc p) m -> p kc m", p=128)
            k.dma("sp", wa[:, 0:NCH * rj].rearrange(
                      "p (kc m) -> p kc m", kc=NCH),
                  srca[:, :, 128 * j: 128 * j + rj],
                  key="w1a" + str(jj % 2), wr=(wa,))
            k.dma("sp", wg[:, 0:NCH * rj].rearrange(
                      "p (kc m) -> p kc m", kc=NCH),
                  srca[:, :, DI + 128 * j: DI + 128 * j + rj],
                  key="w1g" + str(jj % 2), wr=(wg,))
            for j2 in range(ceil_div(tn, 512)):
                c0, cn = j2 * 512, min(512, tn - j2 * 512)
                pg = pp.take()
                pa = pp.take()
                for kc in range(NCH):
                    mm(k, pg, pg[:rj, :cn], wg[:, kc * rj: kc * rj + rj],
                       xn_bf[:, kc * tbuf + c0: kc * tbuf + c0 + cn],
                       rd=(wg, xn_bf), start=(kc == 0), stop=(kc == NCH - 1))
                for kc in range(NCH):
                    mm(k, pa, pa[:rj, :cn], wa[:, kc * rj: kc * rj + rj],
                       xn_bf[:, kc * tbuf + c0: kc * tbuf + c0 + cn],
                       rd=(wa, xn_bf), start=(kc == 0), stop=(kc == NCH - 1))
                k.op("act", "activation", gbf[:rj, :cn], pg[:rj, :cn],
                     AF.Gelu, bias=b1t[:rj, 22 + j: 23 + j], rd=(pg, b1t),
                     wr=(gbf,))
                k.op("dve", "scalar_tensor_tensor",
                     hg[:rj, jj * tn + c0: jj * tn + c0 + cn],
                     pa[:rj, :cn], b1t.ap()[0:rj, j: j + 1], gbf[:rj, :cn],
                     ALU.add, ALU.mult, rd=(pa, b1t, gbf), wr=(hg,))
        for jj, j in enumerate(grp):
            rj = min(128, DI - 128 * j)
            k.dma("sp", w2t[0:rj, jj * 1024: (jj + 1) * 1024],
                  w2src[128 * j: 128 * j + rj, :], key="w2", wr=(w2t,))
        for tt in range(ceil_div(tn, 512)):
            c0, cn = tt * 512, min(512, tn - tt * 512)
            for mh in range(2):
                pxs = pp.take_contig(4)
                for mi in range(4):
                    mc = mh * 4 + mi
                    px = pxs[mi]
                    for jj, j in enumerate(grp):
                        rj = min(128, DI - 128 * j)
                        mm(k, px, px[:, :cn],
                           w2t[0:rj, jj * 1024 + mc * 128:
                               jj * 1024 + mc * 128 + 128],
                           hg[0:rj, jj * tn + c0: jj * tn + c0 + cn],
                           rd=(w2t, hg), start=(jj == 0),
                           stop=(jj == len(grp) - 1))
                dst3 = xacc.ap()[:, mh * 4 * tn: mh * 4 * tn
                                 + 4 * tn].rearrange(
                    "p (m t) -> p m t", m=4)[:, :, c0: c0 + cn]
                src3 = pp.span_ap(pxs[0], 4, (0, 128), cn)
                if gi == 0:
                    k.op("dve", "tensor_copy", dst3, src3,
                         rd=tuple(pxs), wr=(xacc,))
                else:
                    k.op("dve", "tensor_add", dst3, dst3, src3,
                         rd=tuple(pxs) + (xacc,), wr=(xacc,))
    b2b, b2c = b2col
    for tt in range(ceil_div(tn, 256)):
        c0, cn = tt * 256, min(256, tn - tt * 256)
        xb, xsl = x_src(tt, cn)
        for mc in range(NCH):
            k.op("dve", "scalar_tensor_tensor",
                 xnew[:, mc * 256: mc * 256 + cn],
                 xacc[:, mc * tn + c0: mc * tn + c0 + cn],
                 b2b.ap()[:, b2c + mc: b2c + mc + 1], xsl(mc, cn),
                 ALU.add, ALU.add, rd=(xacc, b2b, xb), wr=(xnew,))
        x_out(tt, xnew, cn)


XS_STR = 544


def emit_inner_layer(k, pp, Bf, C, li, x_cur, x_nxt, wq_src, wo_src, w1_src,
                     w2_src, tab_hbm, bm_hbm, b1t, attn_lo, ffn_lo,
                     gflag_col, lbuf=L_TOK, x_wr=None):
    if x_wr is None:
        x_wr = x_cur
    """One inner layer on the local window. x_cur: DBuf [128, NCH*lbuf]
    fp32 (input; ffn writes back on [ffn_lo, lbuf)); x_nxt: DBuf scratch
    (post-attn). tab_hbm: DBuf [128, 4*lbuf] rotary base tables.
    bm_hbm: DBuf [128, 6*512] this layer's boundary-block mask (per-core
    content; equals the standard band mask on heavy cores)."""
    sc = C["scalecols"]

    def x_tile_load(xsrc, lo, ln_, which):
        xt = Bf["xf"][which % 2]
        k.dma("sp", xt[:, : NCH * ln_].rearrange("p (kc t) -> p kc t",
                                                 kc=NCH),
              xsrc.ap().rearrange("p (kc t) -> p kc t", kc=NCH)[
                  :, :, lo: lo + ln_],
              key="xf" + str(which % 2), rd=(xsrc,), wr=(xt,))
        return xt

    def load_tables(r0, rn):
        tabs = Bf["tabs"]
        k.dma("sp", tabs.ap()[:, 0: 4 * 512].rearrange(
                  "p (ti t) -> p ti t", ti=4)[:, :, 0:rn],
              tab_hbm.ap().rearrange("p (ti t) -> p ti t", ti=4)[
                  :, :, r0: r0 + rn],
              key="tab0", rd=(tab_hbm,), wr=(tabs,))
        for ti in range(4):
            k.op("dve", "tensor_scalar_mul", tabs[:, ti * 512: ti * 512 + rn],
                 tabs[:, ti * 512: ti * 512 + rn],
                 sc.ap()[:, 4 * li + ti: 4 * li + ti + 1], rd=(tabs, sc),
                 wr=(tabs,))

    def shift_cast_range(xsrc, t0, tn):
        xs_bf = Bf["xs"]
        for j in range(ceil_div(tn, 256)):
            c0, cn = j * 256, min(256, tn - j * 256)
            lo = max(0, t0 + c0 - 1)
            ln_ = t0 + c0 + cn - lo
            xt = x_tile_load(xsrc, lo, ln_, j)
            off = t0 + c0 - lo
            fcol = None
            if gflag_col is not None and c0 <= gflag_col - t0 < c0 + cn:
                fcol = gflag_col - t0 - c0
            emit_shift_cast(k, xs_bf, XS_STR, c0, xt, ln_, off, cn,
                            C["flag"] if fcol is not None else None, fcol)
        return xs_bf

    k.dma("sp", C["bmask"][:, :], bm_hbm.ap(), key="bml",
          rd=(bm_hbm,), wr=(C["bmask"],))

    w_lo = attn_lo // WIN
    nwin = (lbuf - w_lo * WIN) // WIN
    blocks = []
    w = w_lo
    if nwin % 2 == 1:
        blocks.append((w, 1))
        w += 1
    while w < lbuf // WIN:
        blocks.append((w, 2))
        w += 2
    for bi, (wq0, nw) in enumerate(blocks):
        first = bi == 0
        if first:
            lb0 = (wq0 - 1) * WIN
            load_tables(lb0, WIN)
            xs = shift_cast_range(x_cur, lb0, WIN)
            emit_qkv(k, pp, Bf, C, xs, XS_STR, WIN, lb0, wq_src)
        t0 = wq0 * WIN
        tn = nw * WIN
        load_tables(t0, tn)
        xs = shift_cast_range(x_cur, t0, tn)
        emit_qkv(k, pp, Bf, C, xs, XS_STR, tn, t0, wq_src)
        bm = (wq0 <= W_BOUND_DYN[0] < wq0 + nw)
        mask = C["bmask"] if bm else C["masks"]
        emit_block_attn(k, pp, Bf, C, wq0, nw, wq0 * WIN, (wq0 + nw) * WIN,
                        mask, 0)
        wo_t = Bf["wv"]
        k.dma("sp", wo_t[:, 0:NCH * 1024].rearrange(
                  "p (kc m) -> p kc m", kc=NCH),
              wo_src.rearrange("(kc p) m -> p kc m", p=128), key="wv",
              wr=(wo_t,))
        q_lo = wq0 * WIN

        def x_load(c0, cn, q_lo=q_lo):
            xt = x_tile_load(x_cur, q_lo + c0, cn, c0 // 256)

            def sl(mc, cnn, xt=xt, cn=cn):
                return xt[:, mc * cn: mc * cn + cnn]
            return xt, sl

        def x_out(wob, gn, q_lo=q_lo):
            k.dma("sp", x_nxt.ap().rearrange("p (kc t) -> p kc t", kc=NCH)[
                      :, :, q_lo: q_lo + gn],
                  wob.ap()[:, 0: NCH * 512].rearrange(
                      "p (kc t) -> p kc t", kc=NCH)[:, :, 0:gn],
                  key="xw", rd=(wob,), wr=(x_nxt,))

        emit_wo_residual(k, pp, Bf, tn, wo_t, x_load, x_out)
    total = lbuf - ffn_lo
    for pi in range(ceil_div(total, FFNP)):
        p0 = ffn_lo + pi * FFNP
        pn = min(FFNP, lbuf - p0)
        xs2, xn = Bf["xs"], Bf["qb"]
        for j in range(ceil_div(pn, 256)):
            c0, cn = j * 256, min(256, pn - j * 256)
            xt = x_tile_load(x_nxt, p0 + c0 - 1, cn + 1, j)
            fcol = None
            if gflag_col is not None and c0 <= gflag_col - p0 < c0 + cn:
                fcol = gflag_col - p0 - c0
            emit_shift_cast(k, xs2, FFNP, c0, xt, cn + 1, 1, cn,
                            C["flag"] if fcol is not None else None, fcol)
        emit_ln(k, pp, Bf["lnst"], xs2, FFNP, 0, pn, xn, C,
                gcol=(C["lncols"], 16 * li), bcol=(C["lncols"], 16 * li + 8))

        def fx_src(tt, cn, p0=p0):
            xt = x_tile_load(x_nxt, p0 + tt * 256, cn, tt)

            def sl(mc, cnn, xt=xt, cn=cn):
                return xt[:, mc * cn: mc * cn + cnn]
            return xt, sl

        def fx_out(tt, xnew, cn, p0=p0):
            k.dma("sp", x_wr.ap().rearrange("p (kc t) -> p kc t", kc=NCH)[
                      :, :, p0 + tt * 256: p0 + tt * 256 + cn],
                  xnew.ap()[:, 0: NCH * 256].rearrange(
                      "p (mc t) -> p mc t", mc=NCH)[:, :, 0:cn],
                  key="xw", rd=(xnew,), wr=(x_wr,))

        emit_ffn(k, pp, Bf, C, xn, FFNP, pn, w1_src, w2_src, b1t,
                 (C["lncols"], 96 + 8 * li), fx_src, fx_out)


W_BOUND_DYN = [W_BOUND]


def layer_blocks(attn_lo, lbuf=L_TOK):
    w_lo = attn_lo // WIN
    nwin = (lbuf - w_lo * WIN) // WIN
    blocks = []
    w = w_lo
    if nwin % 2 == 1:
        blocks.append((w, 1))
        w += 1
    while w < lbuf // WIN:
        blocks.append((w, 2))
        w += 2
    return blocks


def boundary_masks_np(attn_lo, light, d_l=D_L, w_bound=None, lbuf=L_TOK):
    wb = w_bound if w_bound is not None else (d_l // WIN)
    blk = [b for b in layer_blocks(attn_lo, lbuf) if b[0] <= wb < b[0] + b[1]]
    if not blk or not light:
        return band_masks_np()
    wq0, nw = blk[0]
    return band_masks_np(key_g0=0, blk_key0_g=(wq0 - 1) * WIN - d_l,
                         blk_q0_g=wq0 * WIN - d_l)


def make_bufs(k):
    Bf = {
        "xs": k.sbuf("xs", (128, NCH * FFNP), BF16),
        "qb": k.sbuf("qb", (128, NCH * FFNP), BF16),
        "kb": k.sbuf("kb", (128, 11 * FFNP), BF16),
        "vt": k.sbuf("vt", (128, 11 * 1024), BF16),
        "at": k.sbuf("at", (128, NCH * 512), BF16),
        "tabs": k.sbuf("tabs", (128, 4 * 512), BF16),
        "stg0": k.sbuf("stg0", (128, 512), F32),
        "rt1": k.sbuf("rt1", (128, 1024), F32),
        "rt2": k.sbuf("rt2", (128, 512), F32),
        "sq": k.sbuf("sqx", (128, 512), BF16),
        "srow": k.sbuf("srow", (2, 1024 + 2 * FFNP), F32),
        "wt0": k.sbuf("wt0", (128, 1024), BF16),
        "wt1": k.sbuf("wt1", (128, 1024), BF16),
        "wg0": k.sbuf("wg0", (128, 1024), BF16),
        "wg1": k.sbuf("wg1", (128, 1024), BF16),
        "wv": k.sbuf("wv", (128, NCH * 1024), BF16),
        "xnew": k.sbuf("xnew", (128, NCH * 256), F32),
        "p0": k.sbuf("p0", (128, 6 * 512), BF16),
        "rec0": k.sbuf("rec0", (128, 512), F32),
        "xacc": k.sbuf("xacc", (128, NCH * FFNP), F32),
        "gbf": k.sbuf("gbf", (128, 512), BF16),
        "xf": [k.sbuf("xf0", (128, NCH * 257), F32),
               k.sbuf("xf1", (128, NCH * 257), F32)],
        "lnst": None,
    }
    Bf["lnst"] = {
        "sq0": Bf["sq"],
        "sq1": Bf["gbf"],
        "row": RowView(Bf["srow"], 0, 1024),
        "tmp": Bf["rt1"],
    }
    return Bf


def make_consts(k, cbf_hbm, csel_hbm, conesf_hbm, cmisc_hbm, mask_hbm,
                sc_hbm, ln_hbm):
    C = {}
    C["ones2"] = k.sbuf("ones2", (128, 2), BF16)
    C["ones64"] = k.sbuf("ones64", (128, 64), BF16)
    C["onesf"] = k.sbuf("onesf", (1, 128), F32)
    C["sel2"] = k.sbuf("sel2", (2, 128), F32)
    C["m8"] = k.sbuf("m8", (128, 4), F32)
    C["masks"] = k.sbuf("masks", (128, 6 * 512), BF16)
    C["bmask"] = k.sbuf("bmask", (128, 6 * 512), BF16)
    C["scalecols"] = k.sbuf("sccols", (128, 24), F32)
    C["lncols"] = k.sbuf("lncols", (128, 96 + 48), F32)
    C["flag"] = RowColView(C["m8"], 2)
    cbf = k.sbuf("cbf", (128, 66), BF16)
    k.dma("pool", cbf[:, :], cbf_hbm, key="cst", wr=(cbf,))
    k.op("dve", "tensor_copy", C["ones2"][:, :], cbf[:, 0:2], rd=(cbf,),
         wr=(C["ones2"],))
    k.op("dve", "tensor_copy", C["ones64"][:, :], cbf[:, 2:66], rd=(cbf,),
         wr=(C["ones64"],))
    k.dma("pool", C["sel2"][:, :], csel_hbm, key="cst", wr=(C["sel2"],))
    k.dma("pool", C["onesf"][:, :], conesf_hbm, key="cst", wr=(C["onesf"],))
    k.dma("pool", C["m8"][:, :], cmisc_hbm, key="cst", wr=(C["m8"],))
    k.dma("pool", C["masks"][:, :], mask_hbm, key="cst", wr=(C["masks"],))
    k.dma("pool", C["scalecols"][:, :], sc_hbm, key="cst",
          wr=(C["scalecols"],))
    k.dma("pool", C["lncols"][:, :], ln_hbm, key="cst", wr=(C["lncols"],))
    return C


class RowView:
    def __init__(self, buf, row, col0=0):
        self.b = buf
        self.row = row
        self.col0 = col0

    def __getitem__(self, idx):
        p, c = idx
        c = slice((c.start or 0) + self.col0, (c.stop or 0) + self.col0)
        return self.b.ap()[self.row: self.row + 1, c]

    def ap(self):
        return self.b.ap()[self.row: self.row + 1, self.col0:]

    @property
    def w(self):
        return self.b.w

    @w.setter
    def w(self, v):
        self.b.w = v

    @property
    def r(self):
        return self.b.r

    @r.setter
    def r(self, v):
        self.b.r = v


class RowColView:
    def __init__(self, buf, col0):
        self.b = buf
        self.col0 = col0

    def ap(self):
        return self.b.ap()[:, self.col0:]

    @property
    def w(self):
        return self.b.w

    @w.setter
    def w(self, v):
        self.b.w = v

    @property
    def r(self):
        return self.b.r

    @r.setter
    def r(self, v):
        self.b.r = v


def host_const_arrays():
    cbf = np.zeros((128, 66), BF)
    cbf[:, 0] = 1
    cbf[64:, 0] = 0
    cbf[64:, 1] = 1
    cbf[:, 2:66] = 1
    csel = np.zeros((2, 128), np.float32)
    csel[0, 0:64] = 1
    csel[1, 64:128] = 1
    conesf = np.ones((1, 128), np.float32)
    return cbf, csel, conesf


OCOL_BOUT, OCOL_G, OCOL_B, OCOL_B2 = 0, 8, 16, 24


def emit_ln_stream(k, pp, Bf, C, x_src, src_lbuf, t0, tn, out_f32=None,
                   out_hbm=None, gcol=None, bcol=None):
    xs = Bf["xs"]
    for j in range(ceil_div(tn, 256)):
        c0, cn = j * 256, min(256, tn - j * 256)
        xt = Bf["xf"][j % 2]
        k.dma("sp", xt.h.ap()[:, : NCH * cn].rearrange(
                  "p (kc t) -> p kc t", kc=NCH),
              x_src.ap().rearrange("p (kc t) -> p kc t", kc=NCH)[
                  :, :, t0 + c0: t0 + c0 + cn],
              key="xf" + str(j % 2), rd=(x_src,), wr=(xt,))
        k.op("act", "copy",
             xs.ap()[:, 0: NCH * FFNP].rearrange(
                 "p (kc t) -> p kc t", kc=NCH)[:, :, c0: c0 + cn],
             xt.ap()[:, 0: NCH * cn].rearrange(
                 "p (kc t) -> p kc t", kc=NCH),
             rd=(xt,), wr=(xs,))
    emit_ln(k, pp, Bf["lnst"], xs, FFNP, 0, tn, xs, C, gcol=gcol, bcol=bcol)
    xs3 = xs.ap()[:, 0: NCH * FFNP].rearrange(
        "p (kc t) -> p kc t", kc=NCH)[:, :, 0:tn]
    if out_f32 is not None:
        k.op("dve", "tensor_copy",
             out_f32.ap()[:, 0: NCH * FFNP].rearrange(
                 "p (kc t) -> p kc t", kc=NCH)[:, :, 0:tn],
             xs3, rd=(xs,), wr=(out_f32,))
    if out_hbm is not None:
        k.dma("sp", out_hbm.ap().rearrange("p (kc t) -> p kc t", kc=NCH)[
                  :, :, t0: t0 + tn],
              xs3, key="lnw", rd=(xs,), wr=(out_hbm,))


def emit_perf_kv(k, pp, Bf, C, tg_hbm, wq_src, projT, kv_sb, n_tok):
    wt = [Bf["wt0"], Bf["wt1"]]
    kb, vt, kf = Bf["kb"], Bf["vt"], Bf["qb"]
    xsb = Bf["xs"]
    wv = Bf["wv"]
    k.dma("sp", wv[:, 0:NCH * 1024].rearrange("p (kc m) -> p kc m", kc=NCH),
          wq_src.rearrange("(kc p) m -> p kc m", p=128)[:, :, 2 * D: 3 * D],
          key="wv", wr=(wv,))
    for t0 in range(0, n_tok, 512):
        cn = min(512, n_tok - t0)
        first = t0 == 0
        k.dma("sp", xsb.h.ap()[:, : NCH * cn].rearrange(
                  "p (kc t) -> p kc t", kc=NCH),
              tg_hbm.ap().rearrange("p (kc t) -> p kc t", kc=NCH)[
                  :, :, t0: t0 + cn],
              key="xf0", rd=(tg_hbm,), wr=(xsb,))
        load_w_ff(k, wt[0], wq_src, D, NCH, 128, key="wq0")
        for c in range(NCH):
            if c + 1 < NCH:
                load_w_ff(k, wt[(c + 1) % 2], wq_src, D, NCH + c + 1, 128,
                          key="wq" + str((c + 1) % 2))
            wtile = wt[c % 2]
            pq = pp.take()
            for kc in range(NCH):
                mm(k, pq, pq[:, :cn], wtile[:, kc * 128: kc * 128 + 128],
                   xsb[:, kc * cn: kc * cn + cn], rd=(wtile, xsb),
                   start=(kc == 0), stop=(kc == NCH - 1))
            k.op("act", "copy", kb[:, c * 512: c * 512 + cn], pq[:, :cn],
                 rd=(pq,), wr=(kb,))
        for tc in range(ceil_div(cn, 128)):
            tn_ = min(128, cn - tc * 128)
            for half in range(2):
                pv = pp.take()
                for kc in range(NCH):
                    mm(k, pv, pv[:tn_, :512],
                       xsb[:, kc * cn + tc * 128: kc * cn + tc * 128 + tn_],
                       wv[:, kc * 1024 + half * 512:
                          kc * 1024 + half * 512 + 512],
                       rd=(xsb, wv), start=(kc == 0), stop=(kc == NCH - 1))
                k.op("act", "copy",
                     vt.h.ap()[0:tn_, tc * 1040 + half * 8 * 65:
                               tc * 1040 + half * 8 * 65 + 520].rearrange(
                         "p (h e) -> p h e", h=8)[:, :, 0:64],
                     pv.ap()[0:tn_, 0:512].rearrange(
                         "p (h e) -> p h e", h=8),
                     rd=(pv,), wr=(vt,))
            k.op("dve", "memset",
                 vt.h.ap()[0:tn_, tc * 1040: tc * 1040 + 1040].rearrange(
                     "p (h e) -> p h e", h=16)[:, :, 64:65],
                 1.0, wr=(vt,))
        for tc in range(ceil_div(cn, 128)):
            tn_ = min(128, cn - tc * 128)
            for grp in range(2):
                pfs = pp.take_contig(4)
                for cc in range(4):
                    c = grp * 4 + cc
                    pf = pfs[cc]
                    for par in range(2):
                        plo, phi = (0, 64) if par == 0 else (64, 128)
                        mm(k, pf, pf[:tn_, par * 256: par * 256 + 256],
                           kb[plo:phi, c * 512 + tc * 128:
                              c * 512 + tc * 128 + tn_],
                           projT[plo:phi, c * 256: c * 256 + 256],
                           rd=(kb, projT), start=True, stop=True)
                span = pp.big.h.ap()[0:tn_, pfs[0].col0: pfs[0].col0 + 2048]
                eb = Bf["at"]
                k.op("act", "activation", eb[:tn_, :2048], span,
                     AF.Exp, rd=tuple(pfs), wr=(eb,))
                k.op("dve", "tensor_scalar_min", eb[:tn_, :2048],
                     eb[:tn_, :2048], 1.0, rd=(eb,), wr=(eb,))
                k.op("dve", "scalar_tensor_tensor",
                     kf[:tn_, grp * 2048: grp * 2048 + 2048],
                     span, 0.0, eb[:tn_, :2048],
                     ALU.max, ALU.add, rd=tuple(pfs) + (eb,), wr=(kf,))
            for h in range(H):
                pks = pp.take_contig(2)
                for fc in range(2):
                    mm(k, pks[fc], pks[fc][:, :65],
                       kf[0:tn_, h * 256 + fc * 128:
                          h * 256 + fc * 128 + 128],
                       vt[0:tn_, tc * 1040 + h * 65: tc * 1040 + h * 65 + 65],
                       rd=(kf, vt), start=True, stop=True)
                dst3 = kv_sb.ap()[:, h * 130: h * 130 + 130].rearrange(
                    "p (f e) -> p f e", f=2)
                src3 = pp.big.h.ap()[:, pks[0].col0: pks[0].col0
                                     + 1024].rearrange(
                    "p (f e) -> p f e", f=2)[:, :, 0:65]
                if first and tc == 0:
                    k.op("dve", "tensor_copy", dst3, src3,
                         rd=tuple(pks), wr=(kv_sb,))
                else:
                    k.op("dve", "tensor_add", dst3, dst3, src3,
                         rd=tuple(pks) + (kv_sb,), wr=(kv_sb,))


def emit_outer_block(k, pp, Bf, C, x0w, x0g, x_nxt, x1_hbm, x_cur, tg_hbm,
                     wq_src, projT_hbm, wo_src, w1_src, w2_src, b1o, ocols,
                     lbuf=L_TOK, n_tok=N, out_lo=OUT0):
    projT = k.sbuf("projT", (128, 2048), BF16)
    k.dma("pool", projT[:, :], projT_hbm.ap(), key="cst", wr=(projT,))
    k.op("dve", "memset", Bf["xnew"][:, 0: NCH * 244], 0.0,
         wr=(Bf["xnew"],))
    k.dma("sp", x_cur.ap().rearrange("p (kc t) -> p kc t", kc=NCH)[
              :, :, 0: out_lo],
          Bf["xnew"].ap()[:, 0: NCH * out_lo].rearrange(
              "p (kc t) -> p kc t", kc=NCH),
          key="xw", rd=(Bf["xnew"],), wr=(x_cur,))
    kv_sb = Bf["xacc"]
    kvb = Bf["at"]

    for p0 in range(0, n_tok, FFNP):
        pn = min(FFNP, n_tok - p0)
        emit_ln_stream(k, pp, Bf, C, x0g, n_tok, p0, pn, out_hbm=tg_hbm)
    emit_perf_kv(k, pp, Bf, C, tg_hbm, wq_src, projT, kv_sb, n_tok)
    k.op("dve", "tensor_copy",
         kvb.ap()[:, 0:2048].rearrange("p (g e) -> p g e", g=2 * H),
         kv_sb.ap()[:, 0: 65 * 2 * H].rearrange(
             "p (g e) -> p g e", g=2 * H)[:, :, 0:64],
         rd=(kv_sb,), wr=(kvb,))
    for h2 in range(2 * H):
        k.op("dve", "tensor_scalar_mul", kvb[:, 2048 + h2 * 64:
                                             2048 + h2 * 64 + 64],
             C["ones64"][:, 0:64], kv_sb.ap()[:, h2 * 65 + 64: h2 * 65 + 65],
             rd=(C["ones64"], kv_sb), wr=(kvb,))

    wv = Bf["wv"]
    k.dma("sp", wv[:, 0:NCH * 1024].rearrange("p (kc m) -> p kc m", kc=NCH),
          wo_src.rearrange("(kc p) m -> p kc m", p=128), key="wv", wr=(wv,))
    wt = [Bf["wt0"], Bf["wt1"]]
    xs, q_fm, qf, rc = Bf["xs"], Bf["kb"], Bf["p0"], Bf["rec0"]
    attn_o, xnew = Bf["qb"], Bf["xnew"]
    for p0 in range(out_lo, lbuf, FFNP):
        pn = min(FFNP, lbuf - p0)
        emit_ln_stream(k, pp, Bf, C, x0w, lbuf, p0, pn)
        load_w_ff(k, wt[0], wq_src, D, 0, 128, key="wq0")
        for c in range(NCH):
            if c + 1 < NCH:
                load_w_ff(k, wt[(c + 1) % 2], wq_src, D, c + 1, 128,
                          key="wq" + str((c + 1) % 2))
            wtile = wt[c % 2]
            for j in range(ceil_div(pn, 512)):
                c0, cn = j * 512, min(512, pn - j * 512)
                pq = pp.take()
                for kc in range(NCH):
                    mm(k, pq, pq[:, :cn], wtile[:, kc * 128: kc * 128 + 128],
                       xs[:, kc * FFNP + c0: kc * FFNP + c0 + cn],
                       rd=(wtile, xs), start=(kc == 0), stop=(kc == NCH - 1))
                k.op("act", "copy", q_fm[:, c * FFNP + c0: c * FFNP + c0 + cn],
                     pq[:, :cn], rd=(pq,), wr=(q_fm,))
        for j in range(ceil_div(pn, 512)):
            c0, cn = j * 512, min(512, pn - j * 512)
            for h in range(H):
                c, par = h // 2, h % 2
                plo, phi = (0, 64) if par == 0 else (64, 128)
                pfab = pp.take_contig(2)
                for fc in range(2):
                    mm(k, pfab[fc], pfab[fc][:, :cn],
                       projT[plo:phi, c * 256 + fc * 128:
                             c * 256 + fc * 128 + 128],
                       q_fm[plo:phi, c * FFNP + c0: c * FFNP + c0 + cn],
                       rd=(projT, q_fm), start=True, stop=True)
                span = pp.span_ap(pfab[0], 2, (0, 128), cn)
                e1 = Bf["rt1"]
                eview = e1.ap()[:, 0:1024].rearrange(
                    "p (f e) -> p f e", f=2)[:, :, 0:cn]
                k.op("act", "activation", eview, span,
                     AF.Exp, rd=tuple(pfab), wr=(e1,))
                k.op("dve", "tensor_scalar_min", e1[:, :1024],
                     e1[:, :1024], 1.0, rd=(e1,), wr=(e1,))
                k.op("dve", "scalar_tensor_tensor",
                     qf.ap()[:, 0:1024].rearrange(
                         "p (f e) -> p f e", f=2)[:, :, 0:cn],
                     span, 0.0, eview,
                     ALU.max, ALU.add, rd=tuple(pfab) + (e1,), wr=(qf,))
                pa = pp.take()
                pd = pp.take()
                for fc in range(2):
                    h2 = h * 2 + fc
                    mm(k, pa, pa[plo:phi, :cn], kvb[:, h2 * 64: h2 * 64 + 64],
                       qf[:, fc * 512: fc * 512 + cn], rd=(kvb, qf),
                       start=(fc == 0), stop=(fc == 1))
                    mm(k, pd, pd[plo:phi, :cn],
                       kvb[:, 2048 + h2 * 64: 2048 + h2 * 64 + 64],
                       qf[:, fc * 512: fc * 512 + cn], rd=(kvb, qf),
                       start=(fc == 0), stop=(fc == 1))
                k.op("dve", "tensor_scalar_add", pd[plo:phi, :cn],
                     pd[plo:phi, :cn], EPS_ATTN, rd=(pd,), wr=(pd,))
                k.op("dve", "reciprocal", rc[plo:phi, :cn], pd[plo:phi, :cn],
                     rd=(pd,), wr=(rc,))
                k.op("dve", "tensor_mul",
                     attn_o[plo:phi, c * FFNP + c0: c * FFNP + c0 + cn],
                     pa[plo:phi, :cn], rc[plo:phi, :cn], rd=(pa, rc),
                     wr=(attn_o,))
        wob = Bf["xacc"]
        for g0 in range(0, pn, 512):
            gn = min(512, pn - g0)
            for mc in range(NCH):
                pb = pp.take()
                for kc in range(NCH):
                    mm(k, pb, pb[:, :gn],
                       wv[:, kc * 1024 + mc * 128: kc * 1024 + mc * 128 + 128],
                       attn_o[:, kc * FFNP + g0: kc * FFNP + g0 + gn],
                       rd=(wv, attn_o), start=(kc == 0), stop=(kc == NCH - 1))
                k.op("dve", "scalar_tensor_tensor",
                     wob[:, mc * 512: mc * 512 + gn], pb[:, :gn],
                     ocols.ap()[:, OCOL_BOUT + mc: OCOL_BOUT + mc + 1],
                     xs[:, mc * FFNP + g0: mc * FFNP + g0 + gn],
                     ALU.add, ALU.add, rd=(pb, ocols, xs), wr=(wob,))
            k.dma("sp", x_nxt.ap().rearrange("p (kc t) -> p kc t", kc=NCH)[
                      :, :, p0 + g0: p0 + g0 + gn],
                  wob.ap()[:, 0: NCH * 512].rearrange(
                      "p (kc t) -> p kc t", kc=NCH)[:, :, 0:gn],
                  key="xw", rd=(wob,), wr=(x_nxt,))
        emit_ln_stream(k, pp, Bf, C, x_nxt, lbuf, p0, pn,
                       gcol=(ocols, OCOL_G), bcol=(ocols, OCOL_B))
        emit_ffn_plain(k, pp, Bf, C, pn, w1_src, w2_src, b1o, ocols)
        for g0 in range(0, pn, 256):
            cn = min(256, pn - g0)
            for mc in range(NCH):
                k.op("dve", "scalar_tensor_tensor",
                     xnew[:, mc * 256: mc * 256 + cn],
                     Bf["xacc"][:, mc * FFNP + g0: mc * FFNP + g0 + cn],
                     ocols.ap()[:, OCOL_B2 + mc: OCOL_B2 + mc + 1],
                     xs[:, mc * FFNP + g0: mc * FFNP + g0 + cn],
                     ALU.add, ALU.add, rd=(Bf["xacc"], ocols, xs), wr=(xnew,))
            k.dma("sp", x_nxt.ap().rearrange("p (kc t) -> p kc t", kc=NCH)[
                      :, :, p0 + g0: p0 + g0 + cn],
                  xnew.ap()[:, 0: NCH * 256].rearrange(
                      "p (mc t) -> p mc t", mc=NCH)[:, :, 0:cn],
                  key="xw", rd=(xnew,), wr=(x_nxt,))
        emit_ln_stream(k, pp, Bf, C, x_nxt, lbuf, p0, pn,
                       gcol=(ocols, OCOL_G), bcol=(ocols, OCOL_B))
        emit_ln(k, pp, Bf["lnst"], xs, FFNP, 0, pn, xs, C)
        for g0 in range(0, pn, 256):
            cn = min(256, pn - g0)
            xt = Bf["xf"][(g0 // 256) % 2]
            k.dma("sp", xt.h.ap()[:, : NCH * cn].rearrange(
                      "p (kc t) -> p kc t", kc=NCH),
                  x0w.ap().rearrange("p (kc t) -> p kc t", kc=NCH)[
                      :, :, p0 + g0: p0 + g0 + cn],
                  key="xf" + str((g0 // 256) % 2), rd=(x0w,), wr=(xt,))
            k.op("dve", "tensor_add",
                 xnew.ap()[:, 0: NCH * 256].rearrange(
                     "p (mc t) -> p mc t", mc=NCH)[:, :, 0:cn],
                 xs.ap()[:, 0: NCH * FFNP].rearrange(
                     "p (mc t) -> p mc t", mc=NCH)[:, :, g0: g0 + cn],
                 xt.ap()[:, 0: NCH * cn].rearrange(
                     "p (mc t) -> p mc t", mc=NCH),
                 rd=(xs, xt), wr=(xnew,))
            k.dma("sp", x1_hbm.ap().rearrange("p (kc t) -> p kc t", kc=NCH)[
                      :, :, p0 + g0: p0 + g0 + cn],
                  xnew.ap()[:, 0: NCH * 256].rearrange(
                      "p (mc t) -> p mc t", mc=NCH)[:, :, 0:cn],
                  key="xw", rd=(xnew,), wr=(x1_hbm,))
        emit_ln_stream(k, pp, Bf, C, x1_hbm, lbuf, p0, pn,
                       out_f32=Bf["xacc"])
        k.dma("sp", x_cur.ap().rearrange("p (kc t) -> p kc t", kc=NCH)[
                  :, :, p0: p0 + pn],
              Bf["xacc"].ap()[:, 0: NCH * FFNP].rearrange(
                  "p (kc t) -> p kc t", kc=NCH)[:, :, 0:pn],
              key="xw", rd=(Bf["xacc"],), wr=(x_cur,))


def emit_ffn_plain(k, pp, Bf, C, tn, w1src, w2src, b1o, ocols):
    w1t = [Bf["wt0"], Bf["wt1"]]
    w2t, hg, xacc = Bf["vt"], Bf["kb"], Bf["xacc"]
    groups = [list(range(0, 11)), list(range(11, 22)), list(range(22, 32))]
    xs = Bf["xs"]
    for gi, grp in enumerate(groups):
        for jj, j in enumerate(grp):
            wa = w1t[jj % 2]
            load_w_ff(k, wa, w1src, D, j, 128, key="w1a" + str(jj % 2))
            for j2 in range(ceil_div(tn, 512)):
                c0, cn = j2 * 512, min(512, tn - j2 * 512)
                pa = pp.take()
                for kc in range(NCH):
                    mm(k, pa, pa[:, :cn], wa[:, kc * 128: kc * 128 + 128],
                       xs[:, kc * FFNP + c0: kc * FFNP + c0 + cn],
                       rd=(wa, xs), start=(kc == 0), stop=(kc == NCH - 1))
                k.op("act", "activation",
                     hg[:, jj * tn + c0: jj * tn + c0 + cn], pa[:, :cn],
                     AF.Gelu, bias=b1o[:, j: j + 1], rd=(pa, b1o), wr=(hg,))
        for jj, j in enumerate(grp):
            k.dma("sp", w2t[:, jj * 1024: (jj + 1) * 1024],
                  w2src[128 * j: 128 * j + 128, :], key="w2", wr=(w2t,))
        for tt in range(ceil_div(tn, 512)):
            c0, cn = tt * 512, min(512, tn - tt * 512)
            for mh in range(2):
                pxs = pp.take_contig(4)
                for mi in range(4):
                    mc = mh * 4 + mi
                    px = pxs[mi]
                    for jj, j in enumerate(grp):
                        mm(k, px, px[:, :cn],
                           w2t[:, jj * 1024 + mc * 128:
                               jj * 1024 + mc * 128 + 128],
                           hg[:, jj * tn + c0: jj * tn + c0 + cn],
                           rd=(w2t, hg), start=(jj == 0),
                           stop=(jj == len(grp) - 1))
                dst3 = xacc.ap()[:, mh * 4 * FFNP: mh * 4 * FFNP
                                 + 4 * FFNP].rearrange(
                    "p (m t) -> p m t", m=4)[:, :, c0: c0 + cn]
                src3 = pp.span_ap(pxs[0], 4, (0, 128), cn)
                if gi == 0:
                    k.op("dve", "tensor_copy", dst3, src3,
                         rd=tuple(pxs), wr=(xacc,))
                else:
                    k.op("dve", "tensor_add", dst3, dst3, src3,
                         rd=tuple(pxs) + (xacc,), wr=(xacc,))


def emit_final(k, pp, Bf, C, x_cur, x1_hbm, out_hbm, lo=D_L, lbuf=L_TOK):
    xs, xnew = Bf["xs"], Bf["xnew"]
    span = lbuf - lo
    for p0 in range(lo, lbuf, FFNP):
        pn = min(FFNP, lbuf - p0)
        emit_ln_stream(k, pp, Bf, C, x_cur, lbuf, p0, pn)
        for g0 in range(0, pn, 256):
            cn = min(256, pn - g0)
            xt = Bf["xf"][(g0 // 256) % 2]
            k.dma("sp", xt.h.ap()[:, : NCH * cn].rearrange(
                      "p (kc t) -> p kc t", kc=NCH),
                  x1_hbm.ap().rearrange("p (kc t) -> p kc t", kc=NCH)[
                      :, :, p0 + g0: p0 + g0 + cn],
                  key="xf" + str((g0 // 256) % 2), rd=(x1_hbm,), wr=(xt,))
            k.op("dve", "tensor_add",
                 xnew.ap()[:, 0: NCH * 256].rearrange(
                     "p (mc t) -> p mc t", mc=NCH)[:, :, 0:cn],
                 xs.ap()[:, 0: NCH * FFNP].rearrange(
                     "p (mc t) -> p mc t", mc=NCH)[:, :, g0: g0 + cn],
                 xt.ap()[:, 0: NCH * cn].rearrange(
                     "p (mc t) -> p mc t", mc=NCH),
                 rd=(xs, xt), wr=(xnew,))
            k.dma("sp", out_hbm.ap().rearrange("p (mc t) -> p mc t", mc=NCH)[
                      :, :, p0 - lo + g0: p0 - lo + g0 + cn],
                  xnew.ap()[:, 0: NCH * 256].rearrange(
                      "p (mc t) -> p mc t", mc=NCH)[:, :, 0:cn],
                  key="ow", rd=(xnew,), wr=(out_hbm,))


def _fm(x):
    T = x.shape[0]
    return np.ascontiguousarray(
        x.reshape(T, NCH, 128).transpose(2, 1, 0).reshape(128, NCH * T))


def _defm(a, T):
    return a.reshape(128, NCH, T).transpose(2, 1, 0).reshape(T, 1024)


def _const_drams(nc):
    dt = nc.dram_tensor
    return {
        "hcbf": dt("hcbf", [128, 66], BF16, kind="ExternalInput").ap(),
        "hcsel": dt("hcsel", [2, 128], F32, kind="ExternalInput").ap(),
        "hconesf": dt("hconesf", [1, 128], F32, kind="ExternalInput").ap(),
        "hcmisc": dt("hcmisc", [128, 4], F32, kind="ExternalInput").ap(),
        "hcmask": dt("hcmask", [128, 3072], BF16, kind="ExternalInput").ap(),
        "hcsc": dt("hcsc", [128, 24], F32, kind="ExternalInput").ap(),
        "hcln": dt("hcln", [128, 144], F32, kind="ExternalInput").ap(),
    }


def build_outer():
    nc = bass.Bass("TRN2", target_bir_lowering=False)
    dt = nc.dram_tensor
    x0w = DBuf(dt("x0w", [128, NCH * L_TOK], F32, kind="ExternalInput").ap())
    x0g = DBuf(dt("x0g", [128, NCH * N], F32, kind="ExternalInput").ap())
    wq = dt("wq", [1024, 3072], F32, kind="ExternalInput").ap()
    pjh = DBuf(dt("hprojT", [128, 2048], F32, kind="ExternalInput").ap())
    wo = dt("wo", [1024, 1024], F32, kind="ExternalInput").ap()
    w1 = dt("w1", [1024, 4096], F32, kind="ExternalInput").ap()
    w2 = dt("w2", [4096, 1024], F32, kind="ExternalInput").ap()
    b1h = dt("b1h", [128, 32], F32, kind="ExternalInput").ap()
    och = dt("och", [128, 32], F32, kind="ExternalInput").ap()
    cd = _const_drams(nc)
    tg = DBuf(dt("tg", [128, NCH * N], BF16, kind="Internal").ap())
    xnx = DBuf(dt("xnx", [128, NCH * L_TOK], F32, kind="Internal").ap())
    x1 = DBuf(dt("x1o", [128, NCH * L_TOK], F32, kind="ExternalOutput").ap())
    xco = DBuf(dt("xco", [128, NCH * L_TOK], F32, kind="ExternalOutput").ap())
    k = Ker(nc)
    with nc.Block() as block:
        pp = PsumPool(k)
        Bf = make_bufs(k)
        b1o = k.sbuf("b1o", (128, 32), F32)
        oco = k.sbuf("oco", (128, 32), F32)

        @block.tensor
        def _(tensor):
            C = make_consts(k, cd["hcbf"], cd["hcsel"], cd["hconesf"],
                            cd["hcmisc"], cd["hcmask"], cd["hcsc"],
                            cd["hcln"])
            k.dma("pool", b1o[:, :], b1h, key="cst", wr=(b1o,))
            k.dma("pool", oco[:, :], och, key="cst", wr=(oco,))
            emit_outer_block(k, pp, Bf, C, x0w, x0g, xnx, x1, xco, tg,
                             wq, pjh, wo, w1, w2, b1o, oco)
            for t in k.barrier_tokens():
                k.wait("pool", t)
    return nc


def build_layer(attn_lo, ffn_lo):
    nc = bass.Bass("TRN2", target_bir_lowering=False)
    dt = nc.dram_tensor
    xin = DBuf(dt("xin", [128, NCH * L_TOK], F32, kind="ExternalInput").ap())
    xout = DBuf(dt("xout", [128, NCH * L_TOK], F32, kind="ExternalOutput").ap())
    xnx = DBuf(dt("xnx", [128, NCH * L_TOK], F32, kind="Internal").ap())
    wq = dt("wq", [1024, 3072], F32, kind="ExternalInput").ap()
    wo = dt("wo", [1024, 1024], F32, kind="ExternalInput").ap()
    w1 = dt("w1", [1024, 5460], F32, kind="ExternalInput").ap()
    w2 = dt("w2", [DI, 1024], F32, kind="ExternalInput").ap()
    tabh = DBuf(dt("tabh", [128, 4 * L_TOK], F32, kind="ExternalInput").ap())
    bmh = DBuf(dt("bmh", [128, 3072], BF16, kind="ExternalInput").ap())
    b1h = dt("b1h", [128, 44], F32, kind="ExternalInput").ap()
    cd = _const_drams(nc)
    k = Ker(nc)
    with nc.Block() as block:
        pp = PsumPool(k)
        Bf = make_bufs(k)
        b1t = k.sbuf("b1t", (128, 44), F32)

        @block.tensor
        def _(tensor):
            C = make_consts(k, cd["hcbf"], cd["hcsel"], cd["hconesf"],
                            cd["hcmisc"], cd["hcmask"], cd["hcsc"],
                            cd["hcln"])
            k.dma("pool", b1t[:, :], b1h, key="cst", wr=(b1t,))
            for kc in range(NCH):
                k.dma("sp", xout.ap()[:, kc * L_TOK: (kc + 1) * L_TOK],
                      xin.ap()[:, kc * L_TOK: (kc + 1) * L_TOK], key="xw",
                      rd=(xin,), wr=(xout,))
            emit_inner_layer(k, pp, Bf, C, 0, xin, xnx, wq, wo, w1, w2,
                             tabh, bmh, b1t, attn_lo, ffn_lo, D_L,
                             x_wr=xout)
            for t in k.barrier_tokens():
                k.wait("pool", t)
    return nc


def build_final():
    nc = bass.Bass("TRN2", target_bir_lowering=False)
    dt = nc.dram_tensor
    xin = DBuf(dt("xin", [128, NCH * L_TOK], F32, kind="ExternalInput").ap())
    x1h = DBuf(dt("x1h", [128, NCH * L_TOK], F32, kind="ExternalInput").ap())
    out = DBuf(dt("out", [128, NCH * (L_TOK - D_L)], F32,
                  kind="ExternalOutput").ap())
    cd = _const_drams(nc)
    k = Ker(nc)
    with nc.Block() as block:
        pp = PsumPool(k)
        Bf = make_bufs(k)

        @block.tensor
        def _(tensor):
            C = make_consts(k, cd["hcbf"], cd["hcsel"], cd["hconesf"],
                            cd["hcmisc"], cd["hcmask"], cd["hcsc"],
                            cd["hcln"])
            emit_final(k, pp, Bf, C, xin, x1h, out)
            for t in k.barrier_tokens():
                k.wait("pool", t)
    return nc


def _b1t_inner(b1):
    a = np.pad(b1[:DI], (0, 22 * 128 - DI)).reshape(22, 128).T
    g = np.pad(b1[DI:], (0, 22 * 128 - DI)).reshape(22, 128).T
    return np.ascontiguousarray(np.concatenate([a, g], 1)).astype(np.float32)


def _lncols_layer(g, b, b2):
    out = np.zeros((128, 144), np.float32)
    out[:, 0:8] = chunk_cols_np(g)
    out[:, 8:16] = chunk_cols_np(b)
    out[:, 96:104] = chunk_cols_np(b2)
    return out


def kernel_multi(**inputs):
    f32 = lambda a: np.asarray(a, dtype=np.float32)
    x0 = f32(inputs["L0_attn_x"])
    proj = f32(inputs["L0_attn_proj"])
    wqkv_o = f32(inputs["L0_attn_w_qkv"])
    wout_o = f32(inputs["L0_attn_w_out"])
    bout_o = f32(inputs["L0_attn_b_out"])
    gam_o = f32(inputs["L0_attn_gamma"])
    bet_o = f32(inputs["L0_attn_beta"])
    w1_o = f32(inputs["L0_attn_w1"])
    b1_o = f32(inputs["L0_attn_b1"])
    w2_o = f32(inputs["L0_attn_w2"])
    b2_o = f32(inputs["L0_attn_b2"])
    wqkv_i = f32(inputs["L0_ffn_Wqkv"])
    qs_i = f32(inputs["L0_ffn_q_scale"])
    ks_i = f32(inputs["L0_ffn_k_scale"])
    wo_i = f32(inputs["L0_ffn_Wo"])
    lng_i = f32(inputs["L0_ffn_ln_g"])
    lnb_i = f32(inputs["L0_ffn_ln_b"])
    w1_i = f32(inputs["L0_ffn_W1"])
    b1_i = f32(inputs["L0_ffn_b1"])
    w2_i = f32(inputs["L0_ffn_W2"])
    b2_i = f32(inputs["L0_ffn_b2"])

    cbf, csel, conesf = host_const_arrays()
    band = band_masks_np().astype(BF)
    cmisc = []
    for r in range(2):
        cm = np.zeros((128, 4), np.float32)
        cm[:, 0], cm[:, 1], cm[:, 2] = -8.0, 8.0, float(r)
        cmisc.append(cm)
    base_c = {"hcbf": cbf, "hcsel": csel, "hconesf": conesf, "hcmask": band}

    pj = np.zeros((128, 2048), np.float32)
    for c in range(8):
        pj[0:64, 256 * c: 256 * c + 256] = proj[2 * c]
        pj[64:128, 256 * c: 256 * c + 256] = proj[2 * c + 1]
    wqT_o = np.ascontiguousarray(wqkv_o.T)
    woT_o = np.ascontiguousarray(wout_o.T)
    w1T_o = np.ascontiguousarray(w1_o.T)
    w2T_o = np.ascontiguousarray(w2_o.T)
    b1h_o = np.ascontiguousarray(b1_o.reshape(32, 128).T).astype(np.float32)
    och = np.ascontiguousarray(np.concatenate(
        [chunk_cols_np(bout_o), chunk_cols_np(gam_o), chunk_cols_np(bet_o),
         chunk_cols_np(b2_o)], 1)).astype(np.float32)
    zsc = np.zeros((128, 24), np.float32)
    zln = np.zeros((128, 144), np.float32)
    x0w_c, x0g_c = [], []
    for core in range(8):
        b, r = core // 2, core % 2
        if r == 0:
            xw = np.concatenate(
                [np.zeros((D_L, D), np.float32), x0[b, :S_L]], 0)
        else:
            xw = x0[b, ORG_H:]
        x0w_c.append(_fm(xw))
        x0g_c.append(_fm(x0[b]))
    nc_outer = build_outer()
    maps = []
    for core in range(8):
        m = dict(base_c)
        m.update({"x0w": x0w_c[core], "x0g": x0g_c[core], "wq": wqT_o,
                  "hprojT": pj, "wo": woT_o, "w1": w1T_o, "w2": w2T_o,
                  "b1h": b1h_o, "och": och, "hcmisc": cmisc[core % 2],
                  "hcsc": zsc, "hcln": zln})
        maps.append(m)
    res = run_bass_kernel_spmd(nc_outer, maps,
                               core_ids=list(range(8))).results
    x1_c = [res[c]["x1o"] for c in range(8)]
    xc_c = [res[c]["xco"] for c in range(8)]

    tabs_c = []
    for core in range(8):
        org = -D_L if core % 2 == 0 else ORG_H
        tabs_c.append(np.ascontiguousarray(
            np.concatenate(rotary_tables_np(org, L_TOK), 1)))
    ranges = layer_ranges()
    progs = {}
    for li in range(6):
        al, fl = ranges[li]
        if (al, fl) not in progs:
            progs[(al, fl)] = build_layer(al, fl)
        ncl = progs[(al, fl)]
        wq_l = perm_wqkv_np(wqkv_i[li])
        sc_l = np.zeros((128, 24), np.float32)
        sc_l[:, 0:4] = scale_cols_np(qs_i[li], ks_i[li])
        ln_l = _lncols_layer(lng_i[li], lnb_i[li], b2_i[li])
        b1t_l = _b1t_inner(b1_i[li])
        bm_l = [boundary_masks_np(al, light=(core % 2 == 0)).astype(BF)
                for core in range(2)]
        maps = []
        for core in range(8):
            m = dict(base_c)
            m.update({"xin": xc_c[core], "wq": wq_l, "wo": wo_i[li],
                      "w1": w1_i[li], "w2": w2_i[li], "tabh": tabs_c[core],
                      "bmh": bm_l[core % 2], "b1h": b1t_l,
                      "hcmisc": cmisc[core % 2], "hcsc": sc_l,
                      "hcln": ln_l})
            maps.append(m)
        res = run_bass_kernel_spmd(ncl, maps,
                                   core_ids=list(range(8))).results
        xc_c = [res[c]["xout"] for c in range(8)]

    nc_fin = build_final()
    maps = []
    for core in range(8):
        m = dict(base_c)
        m.update({"xin": xc_c[core], "x1h": x1_c[core],
                  "hcmisc": cmisc[core % 2], "hcsc": zsc, "hcln": zln})
        maps.append(m)
    res = run_bass_kernel_spmd(nc_fin, maps,
                               core_ids=list(range(8))).results
    span = L_TOK - D_L
    y = np.empty((B, N, D), np.float32)
    for b in range(B):
        y[b, :S_L] = _defm(np.asarray(res[2 * b]["out"]), span)
        y[b, S_L:] = _defm(np.asarray(res[2 * b + 1]["out"]), span)[512:]
    return y


def emit_w_prep(k, Bf, pairs):
    bufs = [Bf["wv"], Bf["kb"]]
    slot = 0
    for src, dst, R, C in pairs:
        for r0 in range(0, R, 128):
            rn = min(128, R - r0)
            for c0 in range(0, C, 4096):
                cn = min(4096, C - c0)
                t = bufs[slot % 2]
                slot += 1
                k.dma("pool", t[0:rn, 0:cn], src[r0: r0 + rn, c0: c0 + cn],
                      key="wpc" + str(slot % 2), wr=(t,))
                k.dma("sp", dst[r0: r0 + rn, c0: c0 + cn], t[0:rn, 0:cn],
                      key="wps" + str(slot % 2), rd=(t,))


def build_fused():
    nc = bass.Bass("TRN2", target_bir_lowering=False)
    dt = nc.dram_tensor
    x0w = DBuf(dt("x0w", [128, NCH * L_TOK], F32, kind="ExternalInput").ap())
    x0g = DBuf(dt("x0g", [128, NCH * N], F32, kind="ExternalInput").ap())
    wqo = dt("wqo", [1024, 3072], F32, kind="ExternalInput").ap()
    pjh = DBuf(dt("hprojT", [128, 2048], F32, kind="ExternalInput").ap())
    woo = dt("woo", [1024, 1024], F32, kind="ExternalInput").ap()
    w1o = dt("w1o", [1024, 4096], F32, kind="ExternalInput").ap()
    w2o = dt("w2o", [4096, 1024], F32, kind="ExternalInput").ap()
    b1ho = dt("b1ho", [128, 32], F32, kind="ExternalInput").ap()
    och = dt("och", [128, 32], F32, kind="ExternalInput").ap()
    wqi = dt("wqi", [6 * 1024, 3072], F32, kind="ExternalInput").ap()
    woi = dt("woi", [6 * 1024, 1024], F32, kind="ExternalInput").ap()
    w1i = dt("w1i", [6 * 1024, 5460], F32, kind="ExternalInput").ap()
    w2i = dt("w2i", [6 * DI, 1024], F32, kind="ExternalInput").ap()
    b1i = dt("b1i", [128, 6 * 44], F32, kind="ExternalInput").ap()
    tabh = DBuf(dt("tabh", [128, 4 * L_TOK], F32, kind="ExternalInput").ap())
    bmi = dt("bmi", [128, 6 * 3072], BF16, kind="ExternalInput").ap()
    cd = _const_drams(nc)
    tg = DBuf(dt("tg", [128, NCH * N], BF16, kind="Internal").ap())
    x1h = DBuf(dt("x1h", [128, NCH * L_TOK], F32, kind="Internal").ap())
    xc = DBuf(dt("xc", [128, NCH * L_TOK], F32, kind="Internal").ap())
    xn = DBuf(dt("xn_s", [128, NCH * L_TOK], F32, kind="Internal").ap())
    out = DBuf(dt("out", [128, NCH * (L_TOK - D_L)], F32,
                  kind="ExternalOutput").ap())
    wqo_b = dt("wqo_b", [1024, 3072], BF16, kind="Internal").ap()
    woo_b = dt("woo_b", [1024, 1024], BF16, kind="Internal").ap()
    w1o_b = dt("w1o_b", [1024, 4096], BF16, kind="Internal").ap()
    w2o_b = dt("w2o_b", [4096, 1024], BF16, kind="Internal").ap()
    wqi_b = dt("wqi_b", [6 * 1024, 3072], BF16, kind="Internal").ap()
    woi_b = dt("woi_b", [6 * 1024, 1024], BF16, kind="Internal").ap()
    w1i_b = dt("w1i_b", [6 * 1024, 5460], BF16, kind="Internal").ap()
    w2i_b = dt("w2i_b", [6 * DI, 1024], BF16, kind="Internal").ap()
    tab_b = dt("tab_b", [128, 4 * L_TOK], BF16, kind="Internal").ap()
    k = Ker(nc)
    with nc.Block() as block:
        pp = PsumPool(k)
        Bf = make_bufs(k)
        b1o = k.sbuf("b1o", (128, 32), F32)
        oco = k.sbuf("oco", (128, 32), F32)
        b1t = k.sbuf("b1t", (128, 44), F32)

        @block.tensor
        def _(tensor):
            C = make_consts(k, cd["hcbf"], cd["hcsel"], cd["hconesf"],
                            cd["hcmisc"], cd["hcmask"], cd["hcsc"],
                            cd["hcln"])
            k.dma("pool", b1o[:, :], b1ho, key="cst", wr=(b1o,))
            k.dma("pool", oco[:, :], och, key="cst", wr=(oco,))
            emit_w_prep(k, Bf, [
                (wqo, wqo_b, 1024, 3072), (woo, woo_b, 1024, 1024),
                (w1o, w1o_b, 1024, 4096), (w2o, w2o_b, 4096, 1024),
                (tabh.ap(), tab_b, 128, 4 * L_TOK),
                (wqi, wqi_b, 6 * 1024, 3072), (woi, woi_b, 6 * 1024, 1024),
                (w1i, w1i_b, 6 * 1024, 5460), (w2i, w2i_b, 6 * DI, 1024),
            ])
            tab_bd = DBuf(tab_b)
            emit_outer_block(k, pp, Bf, C, x0w, x0g, xn, x1h, xc, tg,
                             wqo_b, pjh, woo_b, w1o_b, w2o_b, b1o, oco)
            ranges = layer_ranges()
            for li in range(6):
                al, fl = ranges[li]
                k.dma("pool", b1t[:, :], b1i[:, 44 * li: 44 * li + 44],
                      key="cst", wr=(b1t,))
                bm_l = DBuf(bmi[:, 3072 * li: 3072 * li + 3072])
                emit_inner_layer(
                    k, pp, Bf, C, li, xc, xn,
                    wqi_b[1024 * li: 1024 * li + 1024, :],
                    woi_b[1024 * li: 1024 * li + 1024, :],
                    w1i_b[1024 * li: 1024 * li + 1024, :],
                    w2i_b[DI * li: DI * li + DI, :],
                    tab_bd, bm_l, b1t, al, fl, D_L)
            emit_final(k, pp, Bf, C, xc, x1h, out)
            for t in k.barrier_tokens():
                k.wait("pool", t)
    return nc


def _lncols_all(lng, lnb, b2):
    out = np.zeros((128, 144), np.float32)
    for li in range(6):
        out[:, 16 * li: 16 * li + 8] = chunk_cols_np(lng[li])
        out[:, 16 * li + 8: 16 * li + 16] = chunk_cols_np(lnb[li])
        out[:, 96 + 8 * li: 104 + 8 * li] = chunk_cols_np(b2[li])
    return out


def kernel_fused(inputs):
    f32 = lambda a: np.asarray(a, dtype=np.float32)
    x0 = f32(inputs["L0_attn_x"])
    proj = f32(inputs["L0_attn_proj"])
    wqkv_o = f32(inputs["L0_attn_w_qkv"])
    wqkv_i = f32(inputs["L0_ffn_Wqkv"])
    qs_i = f32(inputs["L0_ffn_q_scale"])
    ks_i = f32(inputs["L0_ffn_k_scale"])

    cbf, csel, conesf = host_const_arrays()
    band = band_masks_np().astype(BF)
    cmisc = []
    for r in range(2):
        cm = np.zeros((128, 4), np.float32)
        cm[:, 0], cm[:, 1], cm[:, 2] = -8.0, 8.0, float(r)
        cmisc.append(cm)

    pj = np.zeros((128, 2048), np.float32)
    for c in range(8):
        pj[0:64, 256 * c: 256 * c + 256] = proj[2 * c]
        pj[64:128, 256 * c: 256 * c + 256] = proj[2 * c + 1]
    sc_all = np.zeros((128, 24), np.float32)
    for li in range(6):
        sc_all[:, 4 * li: 4 * li + 4] = scale_cols_np(qs_i[li], ks_i[li])
    ln_all = _lncols_all(f32(inputs["L0_ffn_ln_g"]),
                         f32(inputs["L0_ffn_ln_b"]),
                         f32(inputs["L0_ffn_b2"]))
    b1all = np.concatenate(
        [_b1t_inner(f32(inputs["L0_ffn_b1"])[li]) for li in range(6)], 1)
    b1all = np.ascontiguousarray(b1all)
    ranges = layer_ranges()
    bmi_r = []
    for r in range(2):
        bmi_r.append(np.ascontiguousarray(np.concatenate(
            [boundary_masks_np(ranges[li][0], light=(r == 0))
             for li in range(6)], 1)).astype(BF))
    wqi = np.concatenate([perm_wqkv_np(wqkv_i[li]) for li in range(6)], 0)
    woi = f32(inputs["L0_ffn_Wo"]).reshape(6 * 1024, 1024)
    w1i = f32(inputs["L0_ffn_W1"]).reshape(6 * 1024, 5460)
    w2i = f32(inputs["L0_ffn_W2"]).reshape(6 * DI, 1024)

    base = {
        "hcbf": cbf, "hcsel": csel, "hconesf": conesf, "hcmask": band,
        "hcsc": sc_all, "hcln": ln_all,
        "wqo": np.ascontiguousarray(wqkv_o.T), "hprojT": pj,
        "woo": np.ascontiguousarray(f32(inputs["L0_attn_w_out"]).T),
        "w1o": np.ascontiguousarray(f32(inputs["L0_attn_w1"]).T),
        "w2o": np.ascontiguousarray(f32(inputs["L0_attn_w2"]).T),
        "b1ho": np.ascontiguousarray(
            f32(inputs["L0_attn_b1"]).reshape(32, 128).T),
        "och": np.ascontiguousarray(np.concatenate(
            [chunk_cols_np(f32(inputs["L0_attn_b_out"])),
             chunk_cols_np(f32(inputs["L0_attn_gamma"])),
             chunk_cols_np(f32(inputs["L0_attn_beta"])),
             chunk_cols_np(f32(inputs["L0_attn_b2"]))], 1)),
        "wqi": wqi, "woi": woi, "w1i": w1i, "w2i": w2i, "b1i": b1all,
    }
    maps = []
    for core in range(8):
        b, r = core // 2, core % 2
        xw = (np.concatenate([np.zeros((D_L, D), np.float32),
                              x0[b, :S_L]], 0)
              if r == 0 else x0[b, ORG_H:])
        org = -D_L if r == 0 else ORG_H
        m = dict(base)
        m.update({
            "x0w": _fm(xw), "x0g": _fm(x0[b]),
            "tabh": np.ascontiguousarray(
                np.concatenate(rotary_tables_np(org, L_TOK), 1)),
            "bmi": bmi_r[r], "hcmisc": cmisc[r],
        })
        maps.append(m)
    nc = build_fused()
    res = run_bass_kernel_spmd(nc, maps, core_ids=list(range(8))).results
    span = L_TOK - D_L
    y = np.empty((B, N, D), np.float32)
    for b in range(B):
        y[b, :S_L] = _defm(np.asarray(res[2 * b]["out"]), span)
        y[b, S_L:] = _defm(np.asarray(res[2 * b + 1]["out"]), span)[512:]
    return y


def kernel(**inputs):
    return kernel_fused(inputs)


# version 07
# speedup vs baseline: 1.1336x; 1.1336x over previous
import numpy as np
import ml_dtypes
import concourse.bass as bass
import concourse.mybir as mybir
from concourse.bass_utils import run_bass_kernel_spmd

F32 = mybir.dt.float32
BF16 = mybir.dt.bfloat16
AF = mybir.ActivationFunctionType
ALU = mybir.AluOpType
BF = ml_dtypes.bfloat16

B, N, D = 4, 4096, 1024
H, HD = 16, 64
PF = 256
WIN = 256
DI = 2730
EPS_LN = 1e-5
EPS_ATTN = 1e-8
NCH = D // 128

L_TOK = 3584
S_L = 2304
ORG_H = N - L_TOK
D_L = L_TOK - S_L
W_BOUND = D_L // WIN
OUT0 = 244

ROT_PERM = np.arange(64).reshape(2, 32).T.reshape(-1)
SHUF_MASK = [i for p in range(16) for i in (2 * p + 1, 2 * p)]

TBUF = 1024
FFNP = 896


def ceil_div(a, b):
    return -(-a // b)


def even(x):
    return x - (x % 2)


def layer_ranges():
    out = []
    V = OUT0
    for i in range(6):
        A = V + 257
        wf = A // WIN
        attn_lo = min(D_L, wf * WIN)
        F = min(D_L, even(A + 1))
        out.append((attn_lo, F))
        V = A + 1
    assert V <= D_L + (N - ORG_H - S_L), ("pyramid chain too deep", V)
    return out


class Ker:

    ENGS = ("pe", "act", "dve", "pool", "sp")

    def __init__(self, nc):
        self.nc = nc
        self.eng = {
            "pe": nc.tensor, "act": nc.scalar, "dve": nc.vector,
            "pool": nc.gpsimd, "sp": nc.sync,
        }
        self._stack = []
        self.prog = {}
        self.count = {}
        for e in self.ENGS:
            self.prog[e] = self._enter(nc.semaphore(f"prog_{e}"))
            self.count[e] = 0
        self.dmasem = {}
        self.dmacnt = {}
        self.dmaout = {}
        self.waited = {e: {} for e in self.ENGS}

    def _enter(self, cm):
        v = cm.__enter__()
        self._stack.append(cm)
        return v

    def sbuf(self, name, shape, dtype):
        return Buf(self._enter(self.nc.sbuf_tensor(name, list(shape), dtype)))

    def psum(self, name, shape, dtype=F32):
        return Buf(self._enter(self.nc.psum_tensor(name, list(shape), dtype)))

    def wait(self, e, tok):
        if tok is None:
            return
        sem, val = tok
        if self.waited[e].get(sem.name, 0) >= val:
            return
        self.eng[e].wait_ge(sem, val)
        self.waited[e][sem.name] = val

    def _collect(self, e, deps, rd, wr):
        need = {}

        def add(tok):
            if tok is None:
                return
            sem, val = tok
            if self.waited[e].get(sem.name, 0) >= val:
                return
            if need.get(sem.name, (None, 0))[1] < val:
                need[sem.name] = (sem, val)
        for b in rd:
            add(b.w)
        for b in wr:
            add(b.w)
            for t in b.r:
                add(t)
        for t in deps:
            add(t)
        return list(need.values())

    def _pre(self, e, pend):
        for sem, val in pend[:-1]:
            self.eng[e].wait_ge(sem, val)
            self.waited[e][sem.name] = val
        if pend:
            sem, val = pend[-1]
            self.waited[e][sem.name] = val
            return (sem, val)
        return None

    def _post(self, tok, rd, wr):
        for b in rd:
            b.r.append(tok)
        for b in wr:
            b.w = tok
            b.r = []
        return tok

    def op(self, e, name, *args, deps=(), rd=(), wr=(), **kw):
        att = self._pre(e, self._collect(e, deps, rd, wr))
        inst = getattr(self.eng[e], name)(*args, **kw)
        if att is not None:
            inst._wait_ge(att[0], att[1])
        self.count[e] += 1
        tok = (self.prog[e], self.count[e])
        inst.then_inc(self.prog[e], 1)
        return self._post(tok, rd, wr)

    def dma(self, q, out, in_, key=None, deps=(), rd=(), wr=(), slow=False):
        e = q
        key = key if key is not None else q
        if key not in self.dmasem:
            self.dmasem[key] = self._enter(self.nc.semaphore(f"dma_{key}"))
            self.dmacnt[key] = 0
            self.dmaout[key] = None
        pend = self._collect(e, (self.dmaout[key],) + tuple(deps), rd, wr)
        att = self._pre(e, pend)
        inst = self.eng[e].dma_start(out=out, in_=in_,
                                     allow_slow_non_contiguous=slow)
        if att is not None:
            inst._wait_ge(att[0], att[1])
        self.dmacnt[key] += 16
        tok = (self.dmasem[key], self.dmacnt[key])
        inst.then_inc(self.dmasem[key], 16)
        self.dmaout[key] = tok
        return self._post(tok, rd, wr)

    def barrier_tokens(self):
        toks = [(self.prog[e], self.count[e]) for e in self.ENGS]
        toks += [(self.dmasem[kq], self.dmacnt[kq]) for kq in self.dmasem]
        return [t for t in toks if t[1] > 0]


class Buf:
    def __init__(self, h):
        self.h = h
        self.w = None
        self.r = []

    def __getitem__(self, idx):
        return self.h[idx]

    def ap(self):
        return self.h.ap()


class DBuf(Buf):
    def __init__(self, ap):
        self.a = ap
        self.w = None
        self.r = []

    def __getitem__(self, idx):
        return self.a[idx]

    def ap(self):
        return self.a


class PsumSlice:
    def __init__(self, t, col0):
        self.t = t
        self.col0 = col0
        self.w = None
        self.r = []

    def __getitem__(self, idx):
        p, c = idx
        c0 = self.col0 + (c.start or 0)
        c1 = self.col0 + (512 if c.stop is None else c.stop)
        return self.t[p, c0:c1]

    def ap(self):
        return self.t.ap()[:, self.col0: self.col0 + 512]


class PsumPool:
    def __init__(self, k):
        self.big = k.psum("psA", (128, 4096), F32)
        self.banks = [PsumSlice(self.big.h, i * 512) for i in range(8)]
        self.i = 0

    def take(self):
        b = self.banks[self.i % 8]
        self.i += 1
        return b

    def take_contig(self, n):
        s = self.i % 8
        if s + n > 8:
            self.i += 8 - s
            s = 0
        self.i += n
        return [self.banks[s + j] for j in range(n)]

    def span_ap(self, b0, nbanks, prow, ncols_per, stride=512):
        return self.big.h.ap()[prow[0]:prow[1],
                               b0.col0: b0.col0 + nbanks * 512].rearrange(
            "p (b c) -> p b c", b=nbanks)[:, :, 0:ncols_per]


def mm(k, pb, pslice, lh, rh, rd=(), start=False, stop=False):
    if start:
        att = k._pre("pe", k._collect("pe", (pb.w,) + tuple(pb.r), rd, ()))
    else:
        att = k._pre("pe", k._collect("pe", (), rd, ()))
    inst = k.eng["pe"].matmul(pslice, lh, rh, start=start, stop=stop)
    if att is not None:
        inst._wait_ge(att[0], att[1])
    k.count["pe"] += 1
    tok = (k.prog["pe"], k.count["pe"])
    inst.then_inc(k.prog["pe"], 1)
    for b in rd:
        b.r.append(tok)
    if start:
        pb.r = []
    pb.w = tok
    return tok


def rotary_tables_np(origin, L):
    half = np.arange(0, HD, 2, dtype=np.float64)
    inv_freq = 1.0 / 10000.0 ** (half / HD)
    t = np.arange(origin, origin + L, dtype=np.float64)
    freqs = np.concatenate([t[:, None] * inv_freq[None, :]] * 2, -1)
    base_scale = (half + 0.4 * HD) / (1.4 * HD)
    power = (t - N // 2) / (WIN // 2)
    scale = np.concatenate([base_scale[None, :] ** power[:, None]] * 2, -1)
    cos, sin = np.cos(freqs), np.sin(freqs)
    sign = np.where(np.arange(HD) % 2 == 0, -1.0, 1.0)
    tabs = []
    for sc in (scale, 1.0 / scale):
        tabs.append((cos * sc)[:, ROT_PERM])
        tabs.append((sin * sc)[:, ROT_PERM] * sign[None, :])
    return [np.ascontiguousarray(np.tile(a.T, (2, 1))).astype(np.float32)
            for a in tabs]


def perm_wqkv_np(wqkv):
    w = wqkv.copy()
    for h in range(H):
        w[:, h * HD: (h + 1) * HD] = wqkv[:, h * HD + ROT_PERM]
        w[:, D + h * HD: D + (h + 1) * HD] = wqkv[:, D + h * HD + ROT_PERM]
    return np.ascontiguousarray(w)


def scale_cols_np(qs, ks):
    pair = np.arange(HD).reshape(32, 2)[:, ::-1].reshape(-1)
    qi, ki = qs[ROT_PERM], ks[ROT_PERM]
    cols = np.stack([qi, qi[pair], ki, ki[pair]], 1)
    return np.ascontiguousarray(np.tile(cols, (2, 1))).astype(np.float32)


def band_masks_np(key_g0=None, blk_key0_g=None, blk_q0_g=None):
    q = np.arange(512)
    out = np.zeros((128, 6 * 512), np.float32)
    for kc in range(6):
        j = np.arange(128) + kc * 128
        m = (q[None, :] <= j[:, None]) & (j[:, None] <= q[None, :] + 256)
        if key_g0 is not None:
            keybad = (blk_key0_g + j[:, None]) < key_g0
            qreal = (blk_q0_g + q[None, :]) >= 0
            m = m & ~(keybad & qreal)
        out[:, kc * 512:(kc + 1) * 512] = np.where(m, 0.0, -1e5)
    return out


def chunk_cols_np(v):
    nch = len(v) // 128
    return np.ascontiguousarray(v.reshape(nch, 128).T).astype(np.float32)


def load_w_ff(k, wtile, wsrc, din, mc, mcols=128, key="w", base_col=0,
              q="sp"):
    kc_n = din // 128
    src = wsrc.rearrange("(kc p) m -> p kc m", p=128)
    src = src[:, :, base_col + mc * 128: base_col + mc * 128 + mcols]
    dst = wtile.h.ap()[:, 0:kc_n * mcols].rearrange(
        "p (kc m) -> p kc m", kc=kc_n)
    return k.dma(q, dst, src, key=key, wr=(wtile,))


def emit_ln(k, pp, st, xbf, tbuf, t0, tn, out_bf, C, gcol=None, bcol=None,
            eps=EPS_LN):
    sq0, sq1, row, tmp = st["sq0"], st["sq1"], st["row"], st["tmp"]
    ones2 = C["ones2"]
    nt = ceil_div(tn, 512)
    for j in range(nt):
        c0, cn = j * 512, min(512, tn - j * 512)
        ones_all = C["ones64"]
        pa = pp.take()
        for kc in range(NCH):
            mm(k, pa, pa[:1, :cn], ones_all[:, 0:1],
               xbf[:, kc * tbuf + t0 + c0: kc * tbuf + t0 + c0 + cn],
               rd=(ones_all, xbf), start=(kc == 0), stop=(kc == NCH - 1))
        pb = pp.take()
        for kc in range(NCH):
            sq = (sq0, sq1)[kc % 2]
            k.op("act", "activation", sq[:, :cn],
                 xbf[:, kc * tbuf + t0 + c0: kc * tbuf + t0 + c0 + cn],
                 AF.Square, rd=(xbf,), wr=(sq,))
            mm(k, pb, pb[:1, :cn], ones_all[:, 0:1], sq[:, :cn],
               rd=(ones_all, sq), start=(kc == 0), stop=(kc == NCH - 1))
        mu = row[:1, c0: c0 + cn]
        va = row[:1, tn + c0: tn + c0 + cn]
        t1 = tmp[:1, :cn]
        k.op("dve", "tensor_scalar_mul", mu, pa[:1, :cn], 1.0 / D,
             rd=(pa,), wr=(row,))
        k.op("dve", "tensor_scalar_mul", va, pb[:1, :cn], 1.0 / D,
             rd=(pb,), wr=(row,))
        k.op("dve", "tensor_mul", t1, mu, mu, rd=(row,), wr=(tmp,))
        k.op("dve", "tensor_sub", va, va, t1, rd=(row, tmp), wr=(row,))
        m8l = C["m8"]
        k.op("act", "activation", va, va, AF.Ln, bias=m8l[:1, 5:6],
             rd=(row, m8l), wr=(row,))
        k.op("act", "activation", va, va, AF.Exp, scale=m8l[:1, 4:5],
             rd=(row, m8l), wr=(row,))
        pm = pp.take()
        mm(k, pm, pm[:, :cn], C["onesf"][:1, 0:128], mu,
           rd=(C["onesf"], row), start=True, stop=True)
        pr = pp.take()
        mm(k, pr, pr[:, :cn], C["onesf"][:1, 0:128], va,
           rd=(C["onesf"], row), start=True, stop=True)
        for kc in range(NCH):
            xs = xbf[:, kc * tbuf + t0 + c0: kc * tbuf + t0 + c0 + cn]
            os_ = out_bf[:, kc * tbuf + t0 + c0: kc * tbuf + t0 + c0 + cn]
            tm = tmp[:, :cn]
            k.op("dve", "tensor_sub", tm, xs, pm[:, :cn], rd=(xbf, pm),
                 wr=(tmp,))
            if gcol is None:
                k.op("dve", "tensor_mul", os_, tm, pr[:, :cn], rd=(tmp, pr),
                     wr=(out_bf,))
            else:
                gb, gc = gcol
                bb, bc = bcol
                k.op("dve", "tensor_mul", tm, tm, pr[:, :cn], rd=(tmp, pr),
                     wr=(tmp,))
                k.op("act", "activation", os_, tm, AF.Identity,
                     bias=bb[:, bc + kc: bc + kc + 1],
                     scale=gb[:, gc + kc: gc + kc + 1],
                     rd=(tmp, gb, bb), wr=(out_bf,))


def emit_shift_cast(k, xs_bf, tbuf, dcol, src, src_tbuf, src_off, cn,
                    flag=None, fcol=None):
    dst3 = xs_bf.ap()[:, 0: NCH * tbuf].rearrange(
        "p (kc t) -> p kc t", kc=NCH)
    src3 = src.ap()[:, 0: NCH * src_tbuf].rearrange(
        "p (kc t) -> p kc t", kc=NCH)
    for half in range(2):
        sh = half
        s0 = src_off - sh
        if s0 >= 0:
            k.op("act", "copy",
                 dst3[:, 4 * half: 4 * half + 4, dcol: dcol + cn],
                 src3[:, 4 * half: 4 * half + 4, s0: s0 + cn],
                 rd=(src,), wr=(xs_bf,))
        else:
            k.op("act", "copy",
                 dst3[:, 4 * half: 4 * half + 4, dcol + 1: dcol + cn],
                 src3[:, 4 * half: 4 * half + 4, 0: cn - 1],
                 rd=(src,), wr=(xs_bf,))
            k.op("act", "copy",
                 dst3[:, 4 * half: 4 * half + 4, dcol: dcol + 1],
                 src3[:, 4 * half: 4 * half + 4, 0: 1],
                 rd=(src,), wr=(xs_bf,))
    if flag is not None and fcol is not None:
        for kc in range(4, NCH):
            cpos = kc * tbuf + dcol + fcol
            k.op("dve", "tensor_scalar_mul", xs_bf[:, cpos: cpos + 1],
                 xs_bf[:, cpos: cpos + 1], flag.ap()[:, 0:1],
                 rd=(xs_bf, flag), wr=(xs_bf,))


def emit_qkv(k, pp, Bf, C, xs_bf, xs_str, rn, t_dst, wq_src):
    q_bf, k_bf, v_tm = Bf["qb"], Bf["kb"], Bf["vt"]
    wt = [Bf["wt0"], Bf["wt1"]]
    stg = Bf["stg0"]
    srow, sq = Bf["srow"], Bf["sq"]
    rt1, rt2 = Bf["rt1"], Bf["rt2"]
    ones2, sel2 = C["ones2"], C["sel2"]
    tabs = Bf["tabs"]
    KBUF = 3 * WIN
    for qk, tbase in (("q", 0), ("k", 2)):
        base_mc = 0 if qk == "q" else NCH
        load_w_ff(k, wt[0], wq_src, D, base_mc, 128, key="wq0")
        for c in range(NCH):
            if c + 1 < NCH:
                load_w_ff(k, wt[(c + 1) % 2], wq_src, D, base_mc + c + 1,
                          128, key="wq" + str((c + 1) % 2))
            wtile = wt[c % 2]
            for j in range(ceil_div(rn, 512)):
                c0, cn = j * 512, min(512, rn - j * 512)
                pq = pp.take()
                for kc in range(NCH):
                    mm(k, pq, pq[:, :cn], wtile[:, kc * 128: kc * 128 + 128],
                       xs_bf[:, kc * xs_str + c0: kc * xs_str + c0 + cn],
                       rd=(wtile, xs_bf), start=(kc == 0),
                       stop=(kc == NCH - 1))
                ps = pp.take()
                k.op("act", "activation", sq[:, :cn], pq[:, :cn], AF.Square,
                     rd=(pq,), wr=(sq,))
                mm(k, ps, ps[:2, :cn], ones2[:, 0:2], sq[:, :cn],
                   rd=(ones2, sq), start=True, stop=True)
                rr = srow[:2, c0: c0 + cn]
                k.op("act", "activation", rr, ps[:2, :cn], AF.Ln,
                     rd=(ps,), wr=(srow,))
                k.op("act", "activation", rr, rr, AF.Exp,
                     scale=C["m8"][:2, 4:5], rd=(srow, C["m8"]), wr=(srow,))
                k.op("dve", "tensor_scalar_min", rr, rr, 1e12, rd=(srow,),
                     wr=(srow,))
                pbx = pp.take()
                mm(k, pbx, pbx[:, :cn], sel2[:2, :], rr, rd=(sel2, srow),
                   start=True, stop=True)
                tcs = tabs[:, tbase * 512 + c0: tbase * 512 + c0 + cn]
                tss = tabs[:, (tbase + 1) * 512 + c0:
                           (tbase + 1) * 512 + c0 + cn]
                k.op("dve", "tensor_mul", rt1[:, :cn], pq[:, :cn], tcs,
                     rd=(pq, tabs), wr=(rt1,))
                k.op("dve", "stream_shuffle", rt2[:, :cn], pq[:, :cn],
                     SHUF_MASK, rd=(pq,), wr=(rt2,))
                k.op("pool", "tensor_mul", rt2[:, :cn], rt2[:, :cn], tss,
                     rd=(rt2, tabs), wr=(rt2,))
                k.op("dve", "tensor_add", rt1[:, :cn], rt1[:, :cn],
                     rt2[:, :cn], rd=(rt1, rt2), wr=(rt1,))
                if qk == "q":
                    k.op("dve", "tensor_mul", q_bf[:, c * 512 + c0:
                                                   c * 512 + c0 + cn],
                         rt1[:, :cn], pbx[:, :cn], rd=(rt1, pbx), wr=(q_bf,))
                else:
                    w0 = (t_dst + c0) % KBUF
                    n1 = min(cn, KBUF - w0)
                    k.op("dve", "tensor_mul",
                         k_bf[:, c * KBUF + w0: c * KBUF + w0 + n1],
                         rt1[:, :n1], pbx[:, :n1], rd=(rt1, pbx), wr=(k_bf,))
                    if n1 < cn:
                        k.op("dve", "tensor_mul",
                             k_bf[:, c * KBUF: c * KBUF + cn - n1],
                             rt1[:, n1:cn], pbx[:, n1:cn], rd=(rt1, pbx),
                             wr=(k_bf,))
    wv = Bf["wv"]
    k.dma("sp", wv[:, 0:NCH * 1024].rearrange("p (kc m) -> p kc m", kc=NCH),
          wq_src.rearrange("(kc p) m -> p kc m", p=128)[:, :, 2 * D: 3 * D],
          key="wv", wr=(wv,))
    for tc in range(ceil_div(rn, 128)):
        cn = min(128, rn - tc * 128)
        slot = ((t_dst + tc * 128) // 128) % 6
        for half in range(2):
            pv = pp.take()
            for kc in range(NCH):
                mm(k, pv, pv[:cn, :512],
                   xs_bf[:, kc * xs_str + tc * 128: kc * xs_str + tc * 128 + cn],
                   wv[:, kc * 1024 + half * 512:
                      kc * 1024 + half * 512 + 512],
                   rd=(xs_bf, wv), start=(kc == 0), stop=(kc == NCH - 1))
            k.op("act", "copy",
                 v_tm[:cn, slot * 1024 + half * 512:
                      slot * 1024 + half * 512 + 512],
                 pv[:cn, :512], rd=(pv,), wr=(v_tm,))


def emit_block_attn(k, pp, Bf, C, wq0, nw, q_lo, q_hi, mask, mcol0):
    q_bf, k_bf, v_tm, attn_bf = Bf["qb"], Bf["kb"], Bf["vt"], Bf["at"]
    Pb, rc = Bf["p0"], Bf["rec0"]
    ones64 = C["ones64"]
    m8 = C["m8"]
    KBUF = 3 * WIN
    qa = max(q_lo, wq0 * WIN)
    qb_ = min(q_hi, (wq0 + nw) * WIN)
    qn = qb_ - qa
    qrel = qa - wq0 * WIN
    nkc = 2 * (nw + 1)
    for h in range(H):
        c, hp = h // 2, h % 2
        plo, phi = (0, 64) if hp == 0 else (64, 128)
        pss = pp.take_contig(nkc)
        pd = pp.take()
        pvp = pp.take()
        for kc in range(nkc):
            key0 = (wq0 - 1) * WIN + kc * 128
            kcol = (key0 % KBUF)
            mm(k, pss[kc], pss[kc][:, :qn],
               k_bf[plo:phi, c * KBUF + kcol: c * KBUF + kcol + 128],
               q_bf[plo:phi, c * 512 + qa - wq0 * WIN:
                    c * 512 + qa - wq0 * WIN + qn],
               rd=(k_bf, q_bf), start=True, stop=True)
        for hb0, hb1 in ((0, nkc // 2), (nkc // 2, nkc)):
            nb = hb1 - hb0
            sspan = pp.span_ap(pss[hb0], nb, (0, 128), qn)
            mspan = mask.ap()[:, mcol0 + hb0 * 512:
                              mcol0 + hb1 * 512].rearrange(
                "p (b c) -> p b c", b=nb)[:, :, qrel: qrel + qn]
            pbspan = Pb.ap()[:, hb0 * 512: hb1 * 512].rearrange(
                "p (b c) -> p b c", b=nb)[:, :, 0:qn]
            ph = tuple(pss[hb0:hb1])
            k.op("dve", "tensor_add", sspan, sspan, mspan,
                 rd=(mask,) + ph, wr=ph)
            k.op("act", "activation", pbspan, sspan, AF.Exp,
                 bias=m8[:, 0:1], scale=m8[:, 1:2],
                 rd=ph + (m8,), wr=(Pb,))
            for kc in range(hb0, hb1):
                key0 = (wq0 - 1) * WIN + kc * 128
                mm(k, pd, pd[plo:phi, :qn], ones64[:, 0:64],
                   Pb[:, kc * 512: kc * 512 + qn], rd=(ones64, Pb),
                   start=(kc == 0), stop=(kc == nkc - 1))
                mm(k, pvp, pvp[plo:phi, :qn],
                   v_tm[:, (key0 // 128 % 6) * 1024 + h * 64:
                        (key0 // 128 % 6) * 1024 + h * 64 + 64],
                   Pb[:, kc * 512: kc * 512 + qn], rd=(v_tm, Pb),
                   start=(kc == 0), stop=(kc == nkc - 1))
        k.op("act", "activation", rc[plo:phi, :qn], pd[plo:phi, :qn],
             AF.Ln, rd=(pd,), wr=(rc,))
        k.op("act", "activation", rc[plo:phi, :qn], rc[plo:phi, :qn],
             AF.Exp, scale=m8.ap()[plo:phi, 3:4], rd=(rc, m8), wr=(rc,))
        k.op("dve", "tensor_mul",
             attn_bf[plo:phi, c * 512 + qa - wq0 * WIN:
                     c * 512 + qa - wq0 * WIN + qn],
             pvp[plo:phi, :qn], rc[plo:phi, :qn], rd=(pvp, rc),
             wr=(attn_bf,))


def emit_wo_residual(k, pp, Bf, tn, wo_t, x_load, x_out):
    attn_bf, wob = Bf["at"], Bf["xacc"]
    pbs = []
    for mc in range(NCH):
        pb = pp.take()
        for kc in range(NCH):
            mm(k, pb, pb[:, :tn],
               wo_t[:, kc * 1024 + mc * 128: kc * 1024 + mc * 128 + 128],
               attn_bf[:, kc * 512: kc * 512 + tn],
               rd=(wo_t, attn_bf), start=(kc == 0), stop=(kc == NCH - 1))
        pbs.append(pb)
    for j in range(ceil_div(tn, 256)):
        c0, cn = j * 256, min(256, tn - j * 256)
        xres, xsl = x_load(c0, cn)
        for mc in range(NCH):
            k.op("dve", "tensor_add",
                 wob[:, mc * 512 + c0: mc * 512 + c0 + cn],
                 pbs[mc][:, c0: c0 + cn], xsl(mc, cn),
                 rd=(pbs[mc], xres), wr=(wob,))
    x_out(wob, tn)


def emit_ffn(k, pp, Bf, C, xn_bf, tbuf, tn, w1src, w2src, b1t, b2col,
             x_src, x_out):
    JN = ceil_div(DI, 128)
    w1a = [Bf["wt0"], Bf["wt1"]]
    w1g = [Bf["wg0"], Bf["wg1"]]
    w2t, hg, xacc, xnew, gbf = (Bf["vt"], Bf["kb"], Bf["xacc"], Bf["xnew"],
                                Bf["gbf"])
    groups = [list(range(0, 11)), list(range(11, JN))]
    for gi, grp in enumerate(groups):
        for jj, j in enumerate(grp):
            rj = min(128, DI - 128 * j)
            wa, wg = w1a[jj % 2], w1g[jj % 2]
            srca = w1src.rearrange("(kc p) m -> p kc m", p=128)
            k.dma("sp", wa[:, 0:NCH * rj].rearrange(
                      "p (kc m) -> p kc m", kc=NCH),
                  srca[:, :, 128 * j: 128 * j + rj],
                  key="w1a" + str(jj % 2), wr=(wa,))
            k.dma("sp", wg[:, 0:NCH * rj].rearrange(
                      "p (kc m) -> p kc m", kc=NCH),
                  srca[:, :, DI + 128 * j: DI + 128 * j + rj],
                  key="w1g" + str(jj % 2), wr=(wg,))
            for j2 in range(ceil_div(tn, 512)):
                c0, cn = j2 * 512, min(512, tn - j2 * 512)
                pg = pp.take()
                pa = pp.take()
                for kc in range(NCH):
                    mm(k, pg, pg[:rj, :cn], wg[:, kc * rj: kc * rj + rj],
                       xn_bf[:, kc * tbuf + c0: kc * tbuf + c0 + cn],
                       rd=(wg, xn_bf), start=(kc == 0), stop=(kc == NCH - 1))
                for kc in range(NCH):
                    mm(k, pa, pa[:rj, :cn], wa[:, kc * rj: kc * rj + rj],
                       xn_bf[:, kc * tbuf + c0: kc * tbuf + c0 + cn],
                       rd=(wa, xn_bf), start=(kc == 0), stop=(kc == NCH - 1))
                k.op("act", "activation", gbf[:rj, :cn], pg[:rj, :cn],
                     AF.Gelu, bias=b1t[:rj, 22 + j: 23 + j], rd=(pg, b1t),
                     wr=(gbf,))
                k.op("dve", "scalar_tensor_tensor",
                     hg[:rj, jj * tn + c0: jj * tn + c0 + cn],
                     pa[:rj, :cn], b1t.ap()[0:rj, j: j + 1], gbf[:rj, :cn],
                     ALU.add, ALU.mult, rd=(pa, b1t, gbf), wr=(hg,))
        for jj, j in enumerate(grp):
            rj = min(128, DI - 128 * j)
            k.dma("sp", w2t[0:rj, jj * 1024: (jj + 1) * 1024],
                  w2src[128 * j: 128 * j + rj, :], key="w2", wr=(w2t,))
        for tt in range(ceil_div(tn, 512)):
            c0, cn = tt * 512, min(512, tn - tt * 512)
            for mh in range(2):
                pxs = pp.take_contig(4)
                for mi in range(4):
                    mc = mh * 4 + mi
                    px = pxs[mi]
                    for jj, j in enumerate(grp):
                        rj = min(128, DI - 128 * j)
                        mm(k, px, px[:, :cn],
                           w2t[0:rj, jj * 1024 + mc * 128:
                               jj * 1024 + mc * 128 + 128],
                           hg[0:rj, jj * tn + c0: jj * tn + c0 + cn],
                           rd=(w2t, hg), start=(jj == 0),
                           stop=(jj == len(grp) - 1))
                dst3 = xacc.ap()[:, mh * 4 * tn: mh * 4 * tn
                                 + 4 * tn].rearrange(
                    "p (m t) -> p m t", m=4)[:, :, c0: c0 + cn]
                src3 = pp.span_ap(pxs[0], 4, (0, 128), cn)
                if gi == 0:
                    k.op("dve", "tensor_copy", dst3, src3,
                         rd=tuple(pxs), wr=(xacc,))
                else:
                    k.op("dve", "tensor_add", dst3, dst3, src3,
                         rd=tuple(pxs) + (xacc,), wr=(xacc,))
    b2b, b2c = b2col
    for tt in range(ceil_div(tn, 256)):
        c0, cn = tt * 256, min(256, tn - tt * 256)
        xb, xsl = x_src(tt, cn)
        for mc in range(NCH):
            k.op("dve", "scalar_tensor_tensor",
                 xnew[:, mc * 256: mc * 256 + cn],
                 xacc[:, mc * tn + c0: mc * tn + c0 + cn],
                 b2b.ap()[:, b2c + mc: b2c + mc + 1], xsl(mc, cn),
                 ALU.add, ALU.add, rd=(xacc, b2b, xb), wr=(xnew,))
        x_out(tt, xnew, cn)


XS_STR = 544


def emit_inner_layer(k, pp, Bf, C, li, x_cur, x_nxt, wq_src, wo_src, w1_src,
                     w2_src, tab_hbm, bm_hbm, b1t, attn_lo, ffn_lo,
                     gflag_col, lbuf=L_TOK, x_wr=None):
    if x_wr is None:
        x_wr = x_cur
    """One inner layer on the local window. x_cur: DBuf [128, NCH*lbuf]
    fp32 (input; ffn writes back on [ffn_lo, lbuf)); x_nxt: DBuf scratch
    (post-attn). tab_hbm: DBuf [128, 4*lbuf] rotary base tables.
    bm_hbm: DBuf [128, 6*512] this layer's boundary-block mask (per-core
    content; equals the standard band mask on heavy cores)."""
    sc = C["scalecols"]

    def x_tile_load(xsrc, lo, ln_, which):
        xt = Bf["xf"][which % 2]
        k.dma("sp", xt[:, : NCH * ln_].rearrange("p (kc t) -> p kc t",
                                                 kc=NCH),
              xsrc.ap().rearrange("p (kc t) -> p kc t", kc=NCH)[
                  :, :, lo: lo + ln_],
              key="xf" + str(which % 2), rd=(xsrc,), wr=(xt,))
        return xt

    def load_tables(r0, rn):
        tabs = Bf["tabs"]
        k.dma("sp", tabs.ap()[:, 0: 4 * 512].rearrange(
                  "p (ti t) -> p ti t", ti=4)[:, :, 0:rn],
              tab_hbm.ap().rearrange("p (ti t) -> p ti t", ti=4)[
                  :, :, r0: r0 + rn],
              key="tab0", rd=(tab_hbm,), wr=(tabs,))
        for ti in range(4):
            k.op("dve", "tensor_scalar_mul", tabs[:, ti * 512: ti * 512 + rn],
                 tabs[:, ti * 512: ti * 512 + rn],
                 sc.ap()[:, 4 * li + ti: 4 * li + ti + 1], rd=(tabs, sc),
                 wr=(tabs,))

    def shift_cast_range(xsrc, t0, tn):
        xs_bf = Bf["xs"]
        for j in range(ceil_div(tn, 256)):
            c0, cn = j * 256, min(256, tn - j * 256)
            lo = max(0, t0 + c0 - 1)
            ln_ = t0 + c0 + cn - lo
            xt = x_tile_load(xsrc, lo, ln_, j)
            off = t0 + c0 - lo
            fcol = None
            if gflag_col is not None and c0 <= gflag_col - t0 < c0 + cn:
                fcol = gflag_col - t0 - c0
            emit_shift_cast(k, xs_bf, XS_STR, c0, xt, ln_, off, cn,
                            C["flag"] if fcol is not None else None, fcol)
        return xs_bf

    k.dma("sp", C["bmask"][:, :], bm_hbm.ap(), key="bml",
          rd=(bm_hbm,), wr=(C["bmask"],))

    w_lo = attn_lo // WIN
    nwin = (lbuf - w_lo * WIN) // WIN
    blocks = []
    w = w_lo
    if nwin % 2 == 1:
        blocks.append((w, 1))
        w += 1
    while w < lbuf // WIN:
        blocks.append((w, 2))
        w += 2
    for bi, (wq0, nw) in enumerate(blocks):
        first = bi == 0
        if first:
            lb0 = (wq0 - 1) * WIN
            load_tables(lb0, WIN)
            xs = shift_cast_range(x_cur, lb0, WIN)
            emit_qkv(k, pp, Bf, C, xs, XS_STR, WIN, lb0, wq_src)
        t0 = wq0 * WIN
        tn = nw * WIN
        load_tables(t0, tn)
        xs = shift_cast_range(x_cur, t0, tn)
        emit_qkv(k, pp, Bf, C, xs, XS_STR, tn, t0, wq_src)
        bm = (wq0 <= W_BOUND_DYN[0] < wq0 + nw)
        mask = C["bmask"] if bm else C["masks"]
        emit_block_attn(k, pp, Bf, C, wq0, nw, wq0 * WIN, (wq0 + nw) * WIN,
                        mask, 0)
        wo_t = Bf["wv"]
        k.dma("sp", wo_t[:, 0:NCH * 1024].rearrange(
                  "p (kc m) -> p kc m", kc=NCH),
              wo_src.rearrange("(kc p) m -> p kc m", p=128), key="wv",
              wr=(wo_t,))
        q_lo = wq0 * WIN

        def x_load(c0, cn, q_lo=q_lo):
            xt = x_tile_load(x_cur, q_lo + c0, cn, c0 // 256)

            def sl(mc, cnn, xt=xt, cn=cn):
                return xt[:, mc * cn: mc * cn + cnn]
            return xt, sl

        def x_out(wob, gn, q_lo=q_lo):
            k.dma("sp", x_nxt.ap().rearrange("p (kc t) -> p kc t", kc=NCH)[
                      :, :, q_lo: q_lo + gn],
                  wob.ap()[:, 0: NCH * 512].rearrange(
                      "p (kc t) -> p kc t", kc=NCH)[:, :, 0:gn],
                  key="xw", rd=(wob,), wr=(x_nxt,))

        emit_wo_residual(k, pp, Bf, tn, wo_t, x_load, x_out)
    total = lbuf - ffn_lo
    for pi in range(ceil_div(total, FFNP)):
        p0 = ffn_lo + pi * FFNP
        pn = min(FFNP, lbuf - p0)
        xs2, xn = Bf["xs"], Bf["qb"]
        for j in range(ceil_div(pn, 256)):
            c0, cn = j * 256, min(256, pn - j * 256)
            xt = x_tile_load(x_nxt, p0 + c0 - 1, cn + 1, j)
            fcol = None
            if gflag_col is not None and c0 <= gflag_col - p0 < c0 + cn:
                fcol = gflag_col - p0 - c0
            emit_shift_cast(k, xs2, FFNP, c0, xt, cn + 1, 1, cn,
                            C["flag"] if fcol is not None else None, fcol)
        emit_ln(k, pp, Bf["lnst"], xs2, FFNP, 0, pn, xn, C,
                gcol=(C["lncols"], 16 * li), bcol=(C["lncols"], 16 * li + 8))

        def fx_src(tt, cn, p0=p0):
            xt = x_tile_load(x_nxt, p0 + tt * 256, cn, tt)

            def sl(mc, cnn, xt=xt, cn=cn):
                return xt[:, mc * cn: mc * cn + cnn]
            return xt, sl

        def fx_out(tt, xnew, cn, p0=p0):
            k.dma("sp", x_wr.ap().rearrange("p (kc t) -> p kc t", kc=NCH)[
                      :, :, p0 + tt * 256: p0 + tt * 256 + cn],
                  xnew.ap()[:, 0: NCH * 256].rearrange(
                      "p (mc t) -> p mc t", mc=NCH)[:, :, 0:cn],
                  key="xw", rd=(xnew,), wr=(x_wr,))

        emit_ffn(k, pp, Bf, C, xn, FFNP, pn, w1_src, w2_src, b1t,
                 (C["lncols"], 96 + 8 * li), fx_src, fx_out)


W_BOUND_DYN = [W_BOUND]


def layer_blocks(attn_lo, lbuf=L_TOK):
    w_lo = attn_lo // WIN
    nwin = (lbuf - w_lo * WIN) // WIN
    blocks = []
    w = w_lo
    if nwin % 2 == 1:
        blocks.append((w, 1))
        w += 1
    while w < lbuf // WIN:
        blocks.append((w, 2))
        w += 2
    return blocks


def boundary_masks_np(attn_lo, light, d_l=D_L, w_bound=None, lbuf=L_TOK):
    wb = w_bound if w_bound is not None else (d_l // WIN)
    blk = [b for b in layer_blocks(attn_lo, lbuf) if b[0] <= wb < b[0] + b[1]]
    if not blk or not light:
        return band_masks_np()
    wq0, nw = blk[0]
    return band_masks_np(key_g0=0, blk_key0_g=(wq0 - 1) * WIN - d_l,
                         blk_q0_g=wq0 * WIN - d_l)


def make_bufs(k):
    Bf = {
        "xs": k.sbuf("xs", (128, NCH * FFNP), BF16),
        "qb": k.sbuf("qb", (128, NCH * FFNP), BF16),
        "kb": k.sbuf("kb", (128, 11 * FFNP), BF16),
        "vt": k.sbuf("vt", (128, 11 * 1024), BF16),
        "at": k.sbuf("at", (128, NCH * 512), BF16),
        "tabs": k.sbuf("tabs", (128, 4 * 512), BF16),
        "stg0": k.sbuf("stg0", (128, 512), F32),
        "rt1": k.sbuf("rt1", (128, 1024), F32),
        "rt2": k.sbuf("rt2", (128, 512), F32),
        "sq": k.sbuf("sqx", (128, 512), BF16),
        "srow": k.sbuf("srow", (2, 1024 + 2 * FFNP), F32),
        "wt0": k.sbuf("wt0", (128, 1024), BF16),
        "wt1": k.sbuf("wt1", (128, 1024), BF16),
        "wg0": k.sbuf("wg0", (128, 1024), BF16),
        "wg1": k.sbuf("wg1", (128, 1024), BF16),
        "wv": k.sbuf("wv", (128, NCH * 1024), BF16),
        "xnew": k.sbuf("xnew", (128, NCH * 256), F32),
        "p0": k.sbuf("p0", (128, 6 * 512), BF16),
        "rec0": k.sbuf("rec0", (128, 512), F32),
        "xacc": k.sbuf("xacc", (128, NCH * FFNP), F32),
        "gbf": k.sbuf("gbf", (128, 512), BF16),
        "xf": [k.sbuf("xf0", (128, NCH * 257), F32),
               k.sbuf("xf1", (128, NCH * 257), F32)],
        "lnst": None,
    }
    Bf["lnst"] = {
        "sq0": Bf["sq"],
        "sq1": Bf["gbf"],
        "row": RowView(Bf["srow"], 0, 1024),
        "tmp": Bf["rt1"],
    }
    return Bf


def make_consts(k, cbf_hbm, csel_hbm, conesf_hbm, cmisc_hbm, mask_hbm,
                sc_hbm, ln_hbm):
    C = {}
    C["ones2"] = k.sbuf("ones2", (128, 2), BF16)
    C["ones64"] = k.sbuf("ones64", (128, 64), BF16)
    C["onesf"] = k.sbuf("onesf", (1, 128), F32)
    C["sel2"] = k.sbuf("sel2", (2, 128), F32)
    C["m8"] = k.sbuf("m8", (128, 8), F32)
    C["masks"] = k.sbuf("masks", (128, 6 * 512), BF16)
    C["bmask"] = k.sbuf("bmask", (128, 6 * 512), BF16)
    C["scalecols"] = k.sbuf("sccols", (128, 24), F32)
    C["lncols"] = k.sbuf("lncols", (128, 96 + 48), F32)
    C["flag"] = RowColView(C["m8"], 2)
    cbf = k.sbuf("cbf", (128, 66), BF16)
    k.dma("pool", cbf[:, :], cbf_hbm, key="cst", wr=(cbf,))
    k.op("dve", "tensor_copy", C["ones2"][:, :], cbf[:, 0:2], rd=(cbf,),
         wr=(C["ones2"],))
    k.op("dve", "tensor_copy", C["ones64"][:, :], cbf[:, 2:66], rd=(cbf,),
         wr=(C["ones64"],))
    k.dma("pool", C["sel2"][:, :], csel_hbm, key="cst", wr=(C["sel2"],))
    k.dma("pool", C["onesf"][:, :], conesf_hbm, key="cst", wr=(C["onesf"],))
    k.dma("pool", C["m8"][:, :], cmisc_hbm, key="cst", wr=(C["m8"],))
    k.dma("pool", C["masks"][:, :], mask_hbm, key="cst", wr=(C["masks"],))
    k.dma("pool", C["scalecols"][:, :], sc_hbm, key="cst",
          wr=(C["scalecols"],))
    k.dma("pool", C["lncols"][:, :], ln_hbm, key="cst", wr=(C["lncols"],))
    return C


class RowView:
    def __init__(self, buf, row, col0=0):
        self.b = buf
        self.row = row
        self.col0 = col0

    def __getitem__(self, idx):
        p, c = idx
        c = slice((c.start or 0) + self.col0, (c.stop or 0) + self.col0)
        return self.b.ap()[self.row: self.row + 1, c]

    def ap(self):
        return self.b.ap()[self.row: self.row + 1, self.col0:]

    @property
    def w(self):
        return self.b.w

    @w.setter
    def w(self, v):
        self.b.w = v

    @property
    def r(self):
        return self.b.r

    @r.setter
    def r(self, v):
        self.b.r = v


class RowColView:
    def __init__(self, buf, col0):
        self.b = buf
        self.col0 = col0

    def ap(self):
        return self.b.ap()[:, self.col0:]

    @property
    def w(self):
        return self.b.w

    @w.setter
    def w(self, v):
        self.b.w = v

    @property
    def r(self):
        return self.b.r

    @r.setter
    def r(self, v):
        self.b.r = v


def host_const_arrays():
    cbf = np.zeros((128, 66), BF)
    cbf[:, 0] = 1
    cbf[64:, 0] = 0
    cbf[64:, 1] = 1
    cbf[:, 2:66] = 1
    csel = np.zeros((2, 128), np.float32)
    csel[0, 0:64] = 1
    csel[1, 64:128] = 1
    conesf = np.ones((1, 128), np.float32)
    return cbf, csel, conesf


OCOL_BOUT, OCOL_G, OCOL_B, OCOL_B2 = 0, 8, 16, 24


def emit_ln_stream(k, pp, Bf, C, x_src, src_lbuf, t0, tn, out_f32=None,
                   out_hbm=None, gcol=None, bcol=None):
    xs = Bf["xs"]
    for j in range(ceil_div(tn, 256)):
        c0, cn = j * 256, min(256, tn - j * 256)
        xt = Bf["xf"][j % 2]
        k.dma("sp", xt.h.ap()[:, : NCH * cn].rearrange(
                  "p (kc t) -> p kc t", kc=NCH),
              x_src.ap().rearrange("p (kc t) -> p kc t", kc=NCH)[
                  :, :, t0 + c0: t0 + c0 + cn],
              key="xf" + str(j % 2), rd=(x_src,), wr=(xt,))
        k.op("act", "copy",
             xs.ap()[:, 0: NCH * FFNP].rearrange(
                 "p (kc t) -> p kc t", kc=NCH)[:, :, c0: c0 + cn],
             xt.ap()[:, 0: NCH * cn].rearrange(
                 "p (kc t) -> p kc t", kc=NCH),
             rd=(xt,), wr=(xs,))
    emit_ln(k, pp, Bf["lnst"], xs, FFNP, 0, tn, xs, C, gcol=gcol, bcol=bcol)
    xs3 = xs.ap()[:, 0: NCH * FFNP].rearrange(
        "p (kc t) -> p kc t", kc=NCH)[:, :, 0:tn]
    if out_f32 is not None:
        k.op("dve", "tensor_copy",
             out_f32.ap()[:, 0: NCH * FFNP].rearrange(
                 "p (kc t) -> p kc t", kc=NCH)[:, :, 0:tn],
             xs3, rd=(xs,), wr=(out_f32,))
    if out_hbm is not None:
        k.dma("sp", out_hbm.ap().rearrange("p (kc t) -> p kc t", kc=NCH)[
                  :, :, t0: t0 + tn],
              xs3, key="lnw", rd=(xs,), wr=(out_hbm,))


def emit_perf_kv(k, pp, Bf, C, tg_hbm, wq_src, projT, kv_sb, n_tok):
    wt = [Bf["wt0"], Bf["wt1"]]
    kb, vt, kf = Bf["kb"], Bf["vt"], Bf["qb"]
    xsb = Bf["xs"]
    wv = Bf["wv"]
    k.dma("sp", wv[:, 0:NCH * 1024].rearrange("p (kc m) -> p kc m", kc=NCH),
          wq_src.rearrange("(kc p) m -> p kc m", p=128)[:, :, 2 * D: 3 * D],
          key="wv", wr=(wv,))
    for t0 in range(0, n_tok, 512):
        cn = min(512, n_tok - t0)
        first = t0 == 0
        k.dma("sp", xsb.h.ap()[:, : NCH * cn].rearrange(
                  "p (kc t) -> p kc t", kc=NCH),
              tg_hbm.ap().rearrange("p (kc t) -> p kc t", kc=NCH)[
                  :, :, t0: t0 + cn],
              key="xf0", rd=(tg_hbm,), wr=(xsb,))
        load_w_ff(k, wt[0], wq_src, D, NCH, 128, key="wq0")
        for c in range(NCH):
            if c + 1 < NCH:
                load_w_ff(k, wt[(c + 1) % 2], wq_src, D, NCH + c + 1, 128,
                          key="wq" + str((c + 1) % 2))
            wtile = wt[c % 2]
            pq = pp.take()
            for kc in range(NCH):
                mm(k, pq, pq[:, :cn], wtile[:, kc * 128: kc * 128 + 128],
                   xsb[:, kc * cn: kc * cn + cn], rd=(wtile, xsb),
                   start=(kc == 0), stop=(kc == NCH - 1))
            k.op("act", "copy", kb[:, c * 512: c * 512 + cn], pq[:, :cn],
                 rd=(pq,), wr=(kb,))
        for tc in range(ceil_div(cn, 128)):
            tn_ = min(128, cn - tc * 128)
            for half in range(2):
                pv = pp.take()
                for kc in range(NCH):
                    mm(k, pv, pv[:tn_, :512],
                       xsb[:, kc * cn + tc * 128: kc * cn + tc * 128 + tn_],
                       wv[:, kc * 1024 + half * 512:
                          kc * 1024 + half * 512 + 512],
                       rd=(xsb, wv), start=(kc == 0), stop=(kc == NCH - 1))
                k.op("act", "copy",
                     vt.h.ap()[0:tn_, tc * 1040 + half * 8 * 65:
                               tc * 1040 + half * 8 * 65 + 520].rearrange(
                         "p (h e) -> p h e", h=8)[:, :, 0:64],
                     pv.ap()[0:tn_, 0:512].rearrange(
                         "p (h e) -> p h e", h=8),
                     rd=(pv,), wr=(vt,))
            k.op("dve", "memset",
                 vt.h.ap()[0:tn_, tc * 1040: tc * 1040 + 1040].rearrange(
                     "p (h e) -> p h e", h=16)[:, :, 64:65],
                 1.0, wr=(vt,))
        for tc in range(ceil_div(cn, 128)):
            tn_ = min(128, cn - tc * 128)
            for grp in range(2):
                pfs = pp.take_contig(4)
                for cc in range(4):
                    c = grp * 4 + cc
                    pf = pfs[cc]
                    for par in range(2):
                        plo, phi = (0, 64) if par == 0 else (64, 128)
                        mm(k, pf, pf[:tn_, par * 256: par * 256 + 256],
                           kb[plo:phi, c * 512 + tc * 128:
                              c * 512 + tc * 128 + tn_],
                           projT[plo:phi, c * 256: c * 256 + 256],
                           rd=(kb, projT), start=True, stop=True)
                span = pp.big.h.ap()[0:tn_, pfs[0].col0: pfs[0].col0 + 2048]
                eb = Bf["at"]
                k.op("act", "activation", eb[:tn_, :2048], span,
                     AF.Exp, rd=tuple(pfs), wr=(eb,))
                k.op("dve", "tensor_scalar_min", eb[:tn_, :2048],
                     eb[:tn_, :2048], 1.0, rd=(eb,), wr=(eb,))
                k.op("dve", "scalar_tensor_tensor",
                     kf[:tn_, grp * 2048: grp * 2048 + 2048],
                     span, 0.0, eb[:tn_, :2048],
                     ALU.max, ALU.add, rd=tuple(pfs) + (eb,), wr=(kf,))
            for h in range(H):
                pks = pp.take_contig(2)
                for fc in range(2):
                    mm(k, pks[fc], pks[fc][:, :65],
                       kf[0:tn_, h * 256 + fc * 128:
                          h * 256 + fc * 128 + 128],
                       vt[0:tn_, tc * 1040 + h * 65: tc * 1040 + h * 65 + 65],
                       rd=(kf, vt), start=True, stop=True)
                dst3 = kv_sb.ap()[:, h * 130: h * 130 + 130].rearrange(
                    "p (f e) -> p f e", f=2)
                src3 = pp.big.h.ap()[:, pks[0].col0: pks[0].col0
                                     + 1024].rearrange(
                    "p (f e) -> p f e", f=2)[:, :, 0:65]
                if first and tc == 0:
                    k.op("dve", "tensor_copy", dst3, src3,
                         rd=tuple(pks), wr=(kv_sb,))
                else:
                    k.op("dve", "tensor_add", dst3, dst3, src3,
                         rd=tuple(pks) + (kv_sb,), wr=(kv_sb,))


def emit_outer_block(k, pp, Bf, C, x0w, x0g, x_nxt, x1_hbm, x_cur, tg_hbm,
                     wq_src, projT_hbm, wo_src, w1_src, w2_src, b1o, ocols,
                     lbuf=L_TOK, n_tok=N, out_lo=OUT0):
    projT = k.sbuf("projT", (128, 2048), BF16)
    k.dma("pool", projT[:, :], projT_hbm.ap(), key="cst", wr=(projT,))
    k.op("dve", "memset", Bf["xnew"][:, 0: NCH * 244], 0.0,
         wr=(Bf["xnew"],))
    k.dma("sp", x_cur.ap().rearrange("p (kc t) -> p kc t", kc=NCH)[
              :, :, 0: out_lo],
          Bf["xnew"].ap()[:, 0: NCH * out_lo].rearrange(
              "p (kc t) -> p kc t", kc=NCH),
          key="xw", rd=(Bf["xnew"],), wr=(x_cur,))
    kv_sb = Bf["xacc"]
    kvb = Bf["at"]

    for p0 in range(0, n_tok, FFNP):
        pn = min(FFNP, n_tok - p0)
        emit_ln_stream(k, pp, Bf, C, x0g, n_tok, p0, pn, out_hbm=tg_hbm)
    emit_perf_kv(k, pp, Bf, C, tg_hbm, wq_src, projT, kv_sb, n_tok)
    k.op("dve", "tensor_copy",
         kvb.ap()[:, 0:2048].rearrange("p (g e) -> p g e", g=2 * H),
         kv_sb.ap()[:, 0: 65 * 2 * H].rearrange(
             "p (g e) -> p g e", g=2 * H)[:, :, 0:64],
         rd=(kv_sb,), wr=(kvb,))
    for h2 in range(2 * H):
        k.op("dve", "tensor_scalar_mul", kvb[:, 2048 + h2 * 64:
                                             2048 + h2 * 64 + 64],
             C["ones64"][:, 0:64], kv_sb.ap()[:, h2 * 65 + 64: h2 * 65 + 65],
             rd=(C["ones64"], kv_sb), wr=(kvb,))

    wv = Bf["wv"]
    k.dma("sp", wv[:, 0:NCH * 1024].rearrange("p (kc m) -> p kc m", kc=NCH),
          wo_src.rearrange("(kc p) m -> p kc m", p=128), key="wv", wr=(wv,))
    wt = [Bf["wt0"], Bf["wt1"]]
    xs, q_fm, qf, rc = Bf["xs"], Bf["kb"], Bf["p0"], Bf["rec0"]
    attn_o, xnew = Bf["qb"], Bf["xnew"]
    for p0 in range(out_lo, lbuf, FFNP):
        pn = min(FFNP, lbuf - p0)
        emit_ln_stream(k, pp, Bf, C, x0w, lbuf, p0, pn)
        load_w_ff(k, wt[0], wq_src, D, 0, 128, key="wq0")
        for c in range(NCH):
            if c + 1 < NCH:
                load_w_ff(k, wt[(c + 1) % 2], wq_src, D, c + 1, 128,
                          key="wq" + str((c + 1) % 2))
            wtile = wt[c % 2]
            for j in range(ceil_div(pn, 512)):
                c0, cn = j * 512, min(512, pn - j * 512)
                pq = pp.take()
                for kc in range(NCH):
                    mm(k, pq, pq[:, :cn], wtile[:, kc * 128: kc * 128 + 128],
                       xs[:, kc * FFNP + c0: kc * FFNP + c0 + cn],
                       rd=(wtile, xs), start=(kc == 0), stop=(kc == NCH - 1))
                k.op("act", "copy", q_fm[:, c * FFNP + c0: c * FFNP + c0 + cn],
                     pq[:, :cn], rd=(pq,), wr=(q_fm,))
        for j in range(ceil_div(pn, 512)):
            c0, cn = j * 512, min(512, pn - j * 512)
            for h in range(H):
                c, par = h // 2, h % 2
                plo, phi = (0, 64) if par == 0 else (64, 128)
                pfab = pp.take_contig(2)
                for fc in range(2):
                    mm(k, pfab[fc], pfab[fc][:, :cn],
                       projT[plo:phi, c * 256 + fc * 128:
                             c * 256 + fc * 128 + 128],
                       q_fm[plo:phi, c * FFNP + c0: c * FFNP + c0 + cn],
                       rd=(projT, q_fm), start=True, stop=True)
                span = pp.span_ap(pfab[0], 2, (0, 128), cn)
                e1 = Bf["rt1"]
                eview = e1.ap()[:, 0:1024].rearrange(
                    "p (f e) -> p f e", f=2)[:, :, 0:cn]
                k.op("act", "activation", eview, span,
                     AF.Exp, rd=tuple(pfab), wr=(e1,))
                k.op("dve", "tensor_scalar_min", e1[:, :1024],
                     e1[:, :1024], 1.0, rd=(e1,), wr=(e1,))
                k.op("dve", "scalar_tensor_tensor",
                     qf.ap()[:, 0:1024].rearrange(
                         "p (f e) -> p f e", f=2)[:, :, 0:cn],
                     span, 0.0, eview,
                     ALU.max, ALU.add, rd=tuple(pfab) + (e1,), wr=(qf,))
                pa = pp.take()
                pd = pp.take()
                for fc in range(2):
                    h2 = h * 2 + fc
                    mm(k, pa, pa[plo:phi, :cn], kvb[:, h2 * 64: h2 * 64 + 64],
                       qf[:, fc * 512: fc * 512 + cn], rd=(kvb, qf),
                       start=(fc == 0), stop=(fc == 1))
                    mm(k, pd, pd[plo:phi, :cn],
                       kvb[:, 2048 + h2 * 64: 2048 + h2 * 64 + 64],
                       qf[:, fc * 512: fc * 512 + cn], rd=(kvb, qf),
                       start=(fc == 0), stop=(fc == 1))
                m8o = C["m8"]
                k.op("act", "activation", rc[plo:phi, :cn], pd[plo:phi, :cn],
                     AF.Ln, bias=m8o.ap()[plo:phi, 6:7], rd=(pd, m8o),
                     wr=(rc,))
                k.op("act", "activation", rc[plo:phi, :cn], rc[plo:phi, :cn],
                     AF.Exp, scale=m8o.ap()[plo:phi, 3:4], rd=(rc, m8o),
                     wr=(rc,))
                k.op("dve", "tensor_mul",
                     attn_o[plo:phi, c * FFNP + c0: c * FFNP + c0 + cn],
                     pa[plo:phi, :cn], rc[plo:phi, :cn], rd=(pa, rc),
                     wr=(attn_o,))
        ub = Bf["kb"]
        for g0 in range(0, pn, 512):
            gn = min(512, pn - g0)
            for mc in range(NCH):
                pb = pp.take()
                for kc in range(NCH):
                    mm(k, pb, pb[:, :gn],
                       wv[:, kc * 1024 + mc * 128: kc * 1024 + mc * 128 + 128],
                       attn_o[:, kc * FFNP + g0: kc * FFNP + g0 + gn],
                       rd=(wv, attn_o), start=(kc == 0), stop=(kc == NCH - 1))
                k.op("dve", "scalar_tensor_tensor",
                     ub[:, mc * FFNP + g0: mc * FFNP + g0 + gn], pb[:, :gn],
                     ocols.ap()[:, OCOL_BOUT + mc: OCOL_BOUT + mc + 1],
                     xs[:, mc * FFNP + g0: mc * FFNP + g0 + gn],
                     ALU.add, ALU.add, rd=(pb, ocols, xs), wr=(ub,))
        emit_ln(k, pp, Bf["lnst"], ub, FFNP, 0, pn, xs, C,
                gcol=(ocols, OCOL_G), bcol=(ocols, OCOL_B))
        emit_ffn_plain(k, pp, Bf, C, pn, w1_src, w2_src, b1o, ocols)
        for mc in range(NCH):
            k.op("dve", "scalar_tensor_tensor",
                 ub[:, mc * FFNP: mc * FFNP + pn],
                 Bf["xacc"][:, mc * FFNP: mc * FFNP + pn],
                 ocols.ap()[:, OCOL_B2 + mc: OCOL_B2 + mc + 1],
                 xs[:, mc * FFNP: mc * FFNP + pn],
                 ALU.add, ALU.add, rd=(Bf["xacc"], ocols, xs), wr=(ub,))
        emit_ln(k, pp, Bf["lnst"], ub, FFNP, 0, pn, xs, C,
                gcol=(ocols, OCOL_G), bcol=(ocols, OCOL_B))
        emit_ln(k, pp, Bf["lnst"], xs, FFNP, 0, pn, xs, C)
        for g0 in range(0, pn, 256):
            cn = min(256, pn - g0)
            xt = Bf["xf"][(g0 // 256) % 2]
            k.dma("sp", xt.h.ap()[:, : NCH * cn].rearrange(
                      "p (kc t) -> p kc t", kc=NCH),
                  x0w.ap().rearrange("p (kc t) -> p kc t", kc=NCH)[
                      :, :, p0 + g0: p0 + g0 + cn],
                  key="xf" + str((g0 // 256) % 2), rd=(x0w,), wr=(xt,))
            k.op("dve", "tensor_add",
                 xnew.ap()[:, 0: NCH * 256].rearrange(
                     "p (mc t) -> p mc t", mc=NCH)[:, :, 0:cn],
                 xs.ap()[:, 0: NCH * FFNP].rearrange(
                     "p (mc t) -> p mc t", mc=NCH)[:, :, g0: g0 + cn],
                 xt.ap()[:, 0: NCH * cn].rearrange(
                     "p (mc t) -> p mc t", mc=NCH),
                 rd=(xs, xt), wr=(xnew,))
            k.dma("sp", x1_hbm.ap().rearrange("p (kc t) -> p kc t", kc=NCH)[
                      :, :, p0 + g0: p0 + g0 + cn],
                  xnew.ap()[:, 0: NCH * 256].rearrange(
                      "p (mc t) -> p mc t", mc=NCH)[:, :, 0:cn],
                  key="xw", rd=(xnew,), wr=(x1_hbm,))
            k.op("act", "copy",
                 ub.ap()[:, 0: NCH * FFNP].rearrange(
                     "p (mc t) -> p mc t", mc=NCH)[:, :, g0: g0 + cn],
                 xnew.ap()[:, 0: NCH * 256].rearrange(
                     "p (mc t) -> p mc t", mc=NCH)[:, :, 0:cn],
                 rd=(xnew,), wr=(ub,))
        emit_ln(k, pp, Bf["lnst"], ub, FFNP, 0, pn, Bf["xacc"], C)
        k.dma("sp", x_cur.ap().rearrange("p (kc t) -> p kc t", kc=NCH)[
                  :, :, p0: p0 + pn],
              Bf["xacc"].ap()[:, 0: NCH * FFNP].rearrange(
                  "p (kc t) -> p kc t", kc=NCH)[:, :, 0:pn],
              key="xw", rd=(Bf["xacc"],), wr=(x_cur,))


def emit_ffn_plain(k, pp, Bf, C, tn, w1src, w2src, b1o, ocols):
    w1t = [Bf["wt0"], Bf["wt1"]]
    w2t, hg, xacc = Bf["vt"], Bf["kb"], Bf["xacc"]
    groups = [list(range(0, 11)), list(range(11, 22)), list(range(22, 32))]
    xs = Bf["xs"]
    for gi, grp in enumerate(groups):
        for jj, j in enumerate(grp):
            wa = w1t[jj % 2]
            load_w_ff(k, wa, w1src, D, j, 128, key="w1a" + str(jj % 2))
            for j2 in range(ceil_div(tn, 512)):
                c0, cn = j2 * 512, min(512, tn - j2 * 512)
                pa = pp.take()
                for kc in range(NCH):
                    mm(k, pa, pa[:, :cn], wa[:, kc * 128: kc * 128 + 128],
                       xs[:, kc * FFNP + c0: kc * FFNP + c0 + cn],
                       rd=(wa, xs), start=(kc == 0), stop=(kc == NCH - 1))
                k.op("act", "activation",
                     hg[:, jj * tn + c0: jj * tn + c0 + cn], pa[:, :cn],
                     AF.Gelu, bias=b1o[:, j: j + 1], rd=(pa, b1o), wr=(hg,))
        for jj, j in enumerate(grp):
            k.dma("sp", w2t[:, jj * 1024: (jj + 1) * 1024],
                  w2src[128 * j: 128 * j + 128, :], key="w2", wr=(w2t,))
        for tt in range(ceil_div(tn, 512)):
            c0, cn = tt * 512, min(512, tn - tt * 512)
            for mh in range(2):
                pxs = pp.take_contig(4)
                for mi in range(4):
                    mc = mh * 4 + mi
                    px = pxs[mi]
                    for jj, j in enumerate(grp):
                        mm(k, px, px[:, :cn],
                           w2t[:, jj * 1024 + mc * 128:
                               jj * 1024 + mc * 128 + 128],
                           hg[:, jj * tn + c0: jj * tn + c0 + cn],
                           rd=(w2t, hg), start=(jj == 0),
                           stop=(jj == len(grp) - 1))
                dst3 = xacc.ap()[:, mh * 4 * FFNP: mh * 4 * FFNP
                                 + 4 * FFNP].rearrange(
                    "p (m t) -> p m t", m=4)[:, :, c0: c0 + cn]
                src3 = pp.span_ap(pxs[0], 4, (0, 128), cn)
                if gi == 0:
                    k.op("dve", "tensor_copy", dst3, src3,
                         rd=tuple(pxs), wr=(xacc,))
                else:
                    k.op("dve", "tensor_add", dst3, dst3, src3,
                         rd=tuple(pxs) + (xacc,), wr=(xacc,))


def emit_final(k, pp, Bf, C, x_cur, x1_hbm, out_hbm, lo=D_L, lbuf=L_TOK):
    xs, xnew = Bf["xs"], Bf["xnew"]
    span = lbuf - lo
    for p0 in range(lo, lbuf, FFNP):
        pn = min(FFNP, lbuf - p0)
        emit_ln_stream(k, pp, Bf, C, x_cur, lbuf, p0, pn)
        for g0 in range(0, pn, 256):
            cn = min(256, pn - g0)
            xt = Bf["xf"][(g0 // 256) % 2]
            k.dma("sp", xt.h.ap()[:, : NCH * cn].rearrange(
                      "p (kc t) -> p kc t", kc=NCH),
                  x1_hbm.ap().rearrange("p (kc t) -> p kc t", kc=NCH)[
                      :, :, p0 + g0: p0 + g0 + cn],
                  key="xf" + str((g0 // 256) % 2), rd=(x1_hbm,), wr=(xt,))
            k.op("dve", "tensor_add",
                 xnew.ap()[:, 0: NCH * 256].rearrange(
                     "p (mc t) -> p mc t", mc=NCH)[:, :, 0:cn],
                 xs.ap()[:, 0: NCH * FFNP].rearrange(
                     "p (mc t) -> p mc t", mc=NCH)[:, :, g0: g0 + cn],
                 xt.ap()[:, 0: NCH * cn].rearrange(
                     "p (mc t) -> p mc t", mc=NCH),
                 rd=(xs, xt), wr=(xnew,))
            k.dma("sp", out_hbm.ap().rearrange("p (mc t) -> p mc t", mc=NCH)[
                      :, :, p0 - lo + g0: p0 - lo + g0 + cn],
                  xnew.ap()[:, 0: NCH * 256].rearrange(
                      "p (mc t) -> p mc t", mc=NCH)[:, :, 0:cn],
                  key="ow", rd=(xnew,), wr=(out_hbm,))


def _fm(x):
    T = x.shape[0]
    return np.ascontiguousarray(
        x.reshape(T, NCH, 128).transpose(2, 1, 0).reshape(128, NCH * T))


def _defm(a, T):
    return a.reshape(128, NCH, T).transpose(2, 1, 0).reshape(T, 1024)


def _const_drams(nc):
    dt = nc.dram_tensor
    return {
        "hcbf": dt("hcbf", [128, 66], BF16, kind="ExternalInput").ap(),
        "hcsel": dt("hcsel", [2, 128], F32, kind="ExternalInput").ap(),
        "hconesf": dt("hconesf", [1, 128], F32, kind="ExternalInput").ap(),
        "hcmisc": dt("hcmisc", [128, 8], F32, kind="ExternalInput").ap(),
        "hcmask": dt("hcmask", [128, 3072], BF16, kind="ExternalInput").ap(),
        "hcsc": dt("hcsc", [128, 24], F32, kind="ExternalInput").ap(),
        "hcln": dt("hcln", [128, 144], F32, kind="ExternalInput").ap(),
    }


def build_outer():
    nc = bass.Bass("TRN2", target_bir_lowering=False)
    dt = nc.dram_tensor
    x0w = DBuf(dt("x0w", [128, NCH * L_TOK], F32, kind="ExternalInput").ap())
    x0g = DBuf(dt("x0g", [128, NCH * N], F32, kind="ExternalInput").ap())
    wq = dt("wq", [1024, 3072], F32, kind="ExternalInput").ap()
    pjh = DBuf(dt("hprojT", [128, 2048], F32, kind="ExternalInput").ap())
    wo = dt("wo", [1024, 1024], F32, kind="ExternalInput").ap()
    w1 = dt("w1", [1024, 4096], F32, kind="ExternalInput").ap()
    w2 = dt("w2", [4096, 1024], F32, kind="ExternalInput").ap()
    b1h = dt("b1h", [128, 32], F32, kind="ExternalInput").ap()
    och = dt("och", [128, 32], F32, kind="ExternalInput").ap()
    cd = _const_drams(nc)
    tg = DBuf(dt("tg", [128, NCH * N], BF16, kind="Internal").ap())
    xnx = DBuf(dt("xnx", [128, NCH * L_TOK], F32, kind="Internal").ap())
    x1 = DBuf(dt("x1o", [128, NCH * L_TOK], F32, kind="ExternalOutput").ap())
    xco = DBuf(dt("xco", [128, NCH * L_TOK], F32, kind="ExternalOutput").ap())
    k = Ker(nc)
    with nc.Block() as block:
        pp = PsumPool(k)
        Bf = make_bufs(k)
        b1o = k.sbuf("b1o", (128, 32), F32)
        oco = k.sbuf("oco", (128, 32), F32)

        @block.tensor
        def _(tensor):
            C = make_consts(k, cd["hcbf"], cd["hcsel"], cd["hconesf"],
                            cd["hcmisc"], cd["hcmask"], cd["hcsc"],
                            cd["hcln"])
            k.dma("pool", b1o[:, :], b1h, key="cst", wr=(b1o,))
            k.dma("pool", oco[:, :], och, key="cst", wr=(oco,))
            emit_outer_block(k, pp, Bf, C, x0w, x0g, xnx, x1, xco, tg,
                             wq, pjh, wo, w1, w2, b1o, oco)
            for t in k.barrier_tokens():
                k.wait("pool", t)
    return nc


def build_layer(attn_lo, ffn_lo):
    nc = bass.Bass("TRN2", target_bir_lowering=False)
    dt = nc.dram_tensor
    xin = DBuf(dt("xin", [128, NCH * L_TOK], F32, kind="ExternalInput").ap())
    xout = DBuf(dt("xout", [128, NCH * L_TOK], F32, kind="ExternalOutput").ap())
    xnx = DBuf(dt("xnx", [128, NCH * L_TOK], F32, kind="Internal").ap())
    wq = dt("wq", [1024, 3072], F32, kind="ExternalInput").ap()
    wo = dt("wo", [1024, 1024], F32, kind="ExternalInput").ap()
    w1 = dt("w1", [1024, 5460], F32, kind="ExternalInput").ap()
    w2 = dt("w2", [DI, 1024], F32, kind="ExternalInput").ap()
    tabh = DBuf(dt("tabh", [128, 4 * L_TOK], F32, kind="ExternalInput").ap())
    bmh = DBuf(dt("bmh", [128, 3072], BF16, kind="ExternalInput").ap())
    b1h = dt("b1h", [128, 44], F32, kind="ExternalInput").ap()
    cd = _const_drams(nc)
    k = Ker(nc)
    with nc.Block() as block:
        pp = PsumPool(k)
        Bf = make_bufs(k)
        b1t = k.sbuf("b1t", (128, 44), F32)

        @block.tensor
        def _(tensor):
            C = make_consts(k, cd["hcbf"], cd["hcsel"], cd["hconesf"],
                            cd["hcmisc"], cd["hcmask"], cd["hcsc"],
                            cd["hcln"])
            k.dma("pool", b1t[:, :], b1h, key="cst", wr=(b1t,))
            for kc in range(NCH):
                k.dma("sp", xout.ap()[:, kc * L_TOK: (kc + 1) * L_TOK],
                      xin.ap()[:, kc * L_TOK: (kc + 1) * L_TOK], key="xw",
                      rd=(xin,), wr=(xout,))
            emit_inner_layer(k, pp, Bf, C, 0, xin, xnx, wq, wo, w1, w2,
                             tabh, bmh, b1t, attn_lo, ffn_lo, D_L,
                             x_wr=xout)
            for t in k.barrier_tokens():
                k.wait("pool", t)
    return nc


def build_final():
    nc = bass.Bass("TRN2", target_bir_lowering=False)
    dt = nc.dram_tensor
    xin = DBuf(dt("xin", [128, NCH * L_TOK], F32, kind="ExternalInput").ap())
    x1h = DBuf(dt("x1h", [128, NCH * L_TOK], F32, kind="ExternalInput").ap())
    out = DBuf(dt("out", [128, NCH * (L_TOK - D_L)], F32,
                  kind="ExternalOutput").ap())
    cd = _const_drams(nc)
    k = Ker(nc)
    with nc.Block() as block:
        pp = PsumPool(k)
        Bf = make_bufs(k)

        @block.tensor
        def _(tensor):
            C = make_consts(k, cd["hcbf"], cd["hcsel"], cd["hconesf"],
                            cd["hcmisc"], cd["hcmask"], cd["hcsc"],
                            cd["hcln"])
            emit_final(k, pp, Bf, C, xin, x1h, out)
            for t in k.barrier_tokens():
                k.wait("pool", t)
    return nc


def _b1t_inner(b1):
    a = np.pad(b1[:DI], (0, 22 * 128 - DI)).reshape(22, 128).T
    g = np.pad(b1[DI:], (0, 22 * 128 - DI)).reshape(22, 128).T
    return np.ascontiguousarray(np.concatenate([a, g], 1)).astype(np.float32)


def _lncols_layer(g, b, b2):
    out = np.zeros((128, 144), np.float32)
    out[:, 0:8] = chunk_cols_np(g)
    out[:, 8:16] = chunk_cols_np(b)
    out[:, 96:104] = chunk_cols_np(b2)
    return out


def kernel_multi(**inputs):
    f32 = lambda a: np.asarray(a, dtype=np.float32)
    x0 = f32(inputs["L0_attn_x"])
    proj = f32(inputs["L0_attn_proj"])
    wqkv_o = f32(inputs["L0_attn_w_qkv"])
    wout_o = f32(inputs["L0_attn_w_out"])
    bout_o = f32(inputs["L0_attn_b_out"])
    gam_o = f32(inputs["L0_attn_gamma"])
    bet_o = f32(inputs["L0_attn_beta"])
    w1_o = f32(inputs["L0_attn_w1"])
    b1_o = f32(inputs["L0_attn_b1"])
    w2_o = f32(inputs["L0_attn_w2"])
    b2_o = f32(inputs["L0_attn_b2"])
    wqkv_i = f32(inputs["L0_ffn_Wqkv"])
    qs_i = f32(inputs["L0_ffn_q_scale"])
    ks_i = f32(inputs["L0_ffn_k_scale"])
    wo_i = f32(inputs["L0_ffn_Wo"])
    lng_i = f32(inputs["L0_ffn_ln_g"])
    lnb_i = f32(inputs["L0_ffn_ln_b"])
    w1_i = f32(inputs["L0_ffn_W1"])
    b1_i = f32(inputs["L0_ffn_b1"])
    w2_i = f32(inputs["L0_ffn_W2"])
    b2_i = f32(inputs["L0_ffn_b2"])

    cbf, csel, conesf = host_const_arrays()
    band = band_masks_np().astype(BF)
    cmisc = []
    for r in range(2):
        cm = np.zeros((128, 8), np.float32)
        cm[:, 0], cm[:, 1], cm[:, 2] = -8.0, 8.0, float(r)
        cm[:, 3], cm[:, 4], cm[:, 5], cm[:, 6] = -1.0, -0.5, EPS_LN, EPS_ATTN
        cmisc.append(cm)
    base_c = {"hcbf": cbf, "hcsel": csel, "hconesf": conesf, "hcmask": band}

    pj = np.zeros((128, 2048), np.float32)
    for c in range(8):
        pj[0:64, 256 * c: 256 * c + 256] = proj[2 * c]
        pj[64:128, 256 * c: 256 * c + 256] = proj[2 * c + 1]
    wqT_o = np.ascontiguousarray(wqkv_o.T)
    woT_o = np.ascontiguousarray(wout_o.T)
    w1T_o = np.ascontiguousarray(w1_o.T)
    w2T_o = np.ascontiguousarray(w2_o.T)
    b1h_o = np.ascontiguousarray(b1_o.reshape(32, 128).T).astype(np.float32)
    och = np.ascontiguousarray(np.concatenate(
        [chunk_cols_np(bout_o), chunk_cols_np(gam_o), chunk_cols_np(bet_o),
         chunk_cols_np(b2_o)], 1)).astype(np.float32)
    zsc = np.zeros((128, 24), np.float32)
    zln = np.zeros((128, 144), np.float32)
    x0w_c, x0g_c = [], []
    for core in range(8):
        b, r = core // 2, core % 2
        if r == 0:
            xw = np.concatenate(
                [np.zeros((D_L, D), np.float32), x0[b, :S_L]], 0)
        else:
            xw = x0[b, ORG_H:]
        x0w_c.append(_fm(xw))
        x0g_c.append(_fm(x0[b]))
    nc_outer = build_outer()
    maps = []
    for core in range(8):
        m = dict(base_c)
        m.update({"x0w": x0w_c[core], "x0g": x0g_c[core], "wq": wqT_o,
                  "hprojT": pj, "wo": woT_o, "w1": w1T_o, "w2": w2T_o,
                  "b1h": b1h_o, "och": och, "hcmisc": cmisc[core % 2],
                  "hcsc": zsc, "hcln": zln})
        maps.append(m)
    res = run_bass_kernel_spmd(nc_outer, maps,
                               core_ids=list(range(8))).results
    x1_c = [res[c]["x1o"] for c in range(8)]
    xc_c = [res[c]["xco"] for c in range(8)]

    tabs_c = []
    for core in range(8):
        org = -D_L if core % 2 == 0 else ORG_H
        tabs_c.append(np.ascontiguousarray(
            np.concatenate(rotary_tables_np(org, L_TOK), 1)))
    ranges = layer_ranges()
    progs = {}
    for li in range(6):
        al, fl = ranges[li]
        if (al, fl) not in progs:
            progs[(al, fl)] = build_layer(al, fl)
        ncl = progs[(al, fl)]
        wq_l = perm_wqkv_np(wqkv_i[li])
        sc_l = np.zeros((128, 24), np.float32)
        sc_l[:, 0:4] = scale_cols_np(qs_i[li], ks_i[li])
        ln_l = _lncols_layer(lng_i[li], lnb_i[li], b2_i[li])
        b1t_l = _b1t_inner(b1_i[li])
        bm_l = [boundary_masks_np(al, light=(core % 2 == 0)).astype(BF)
                for core in range(2)]
        maps = []
        for core in range(8):
            m = dict(base_c)
            m.update({"xin": xc_c[core], "wq": wq_l, "wo": wo_i[li],
                      "w1": w1_i[li], "w2": w2_i[li], "tabh": tabs_c[core],
                      "bmh": bm_l[core % 2], "b1h": b1t_l,
                      "hcmisc": cmisc[core % 2], "hcsc": sc_l,
                      "hcln": ln_l})
            maps.append(m)
        res = run_bass_kernel_spmd(ncl, maps,
                                   core_ids=list(range(8))).results
        xc_c = [res[c]["xout"] for c in range(8)]

    nc_fin = build_final()
    maps = []
    for core in range(8):
        m = dict(base_c)
        m.update({"xin": xc_c[core], "x1h": x1_c[core],
                  "hcmisc": cmisc[core % 2], "hcsc": zsc, "hcln": zln})
        maps.append(m)
    res = run_bass_kernel_spmd(nc_fin, maps,
                               core_ids=list(range(8))).results
    span = L_TOK - D_L
    y = np.empty((B, N, D), np.float32)
    for b in range(B):
        y[b, :S_L] = _defm(np.asarray(res[2 * b]["out"]), span)
        y[b, S_L:] = _defm(np.asarray(res[2 * b + 1]["out"]), span)[512:]
    return y


def emit_w_prep(k, Bf, pairs):
    bufs = [Bf["wv"], Bf["kb"]]
    slot = 0
    for src, dst, R, C in pairs:
        for r0 in range(0, R, 128):
            rn = min(128, R - r0)
            for c0 in range(0, C, 4096):
                cn = min(4096, C - c0)
                t = bufs[slot % 2]
                slot += 1
                k.dma("pool", t[0:rn, 0:cn], src[r0: r0 + rn, c0: c0 + cn],
                      key="wpc" + str(slot % 2), wr=(t,))
                k.dma("sp", dst[r0: r0 + rn, c0: c0 + cn], t[0:rn, 0:cn],
                      key="wps" + str(slot % 2), rd=(t,))


def build_fused():
    nc = bass.Bass("TRN2", target_bir_lowering=False)
    dt = nc.dram_tensor
    x0w = DBuf(dt("x0w", [128, NCH * L_TOK], F32, kind="ExternalInput").ap())
    x0g = DBuf(dt("x0g", [128, NCH * N], F32, kind="ExternalInput").ap())
    wqo = dt("wqo", [1024, 3072], F32, kind="ExternalInput").ap()
    pjh = DBuf(dt("hprojT", [128, 2048], F32, kind="ExternalInput").ap())
    woo = dt("woo", [1024, 1024], F32, kind="ExternalInput").ap()
    w1o = dt("w1o", [1024, 4096], F32, kind="ExternalInput").ap()
    w2o = dt("w2o", [4096, 1024], F32, kind="ExternalInput").ap()
    b1ho = dt("b1ho", [128, 32], F32, kind="ExternalInput").ap()
    och = dt("och", [128, 32], F32, kind="ExternalInput").ap()
    wqi = dt("wqi", [6 * 1024, 3072], F32, kind="ExternalInput").ap()
    woi = dt("woi", [6 * 1024, 1024], F32, kind="ExternalInput").ap()
    w1i = dt("w1i", [6 * 1024, 5460], F32, kind="ExternalInput").ap()
    w2i = dt("w2i", [6 * DI, 1024], F32, kind="ExternalInput").ap()
    b1i = dt("b1i", [128, 6 * 44], F32, kind="ExternalInput").ap()
    tabh = DBuf(dt("tabh", [128, 4 * L_TOK], F32, kind="ExternalInput").ap())
    bmi = dt("bmi", [128, 6 * 3072], BF16, kind="ExternalInput").ap()
    cd = _const_drams(nc)
    tg = DBuf(dt("tg", [128, NCH * N], BF16, kind="Internal").ap())
    x1h = DBuf(dt("x1h", [128, NCH * L_TOK], F32, kind="Internal").ap())
    xc = DBuf(dt("xc", [128, NCH * L_TOK], F32, kind="Internal").ap())
    xn = DBuf(dt("xn_s", [128, NCH * L_TOK], F32, kind="Internal").ap())
    out = DBuf(dt("out", [128, NCH * (L_TOK - D_L)], F32,
                  kind="ExternalOutput").ap())
    wqo_b = dt("wqo_b", [1024, 3072], BF16, kind="Internal").ap()
    woo_b = dt("woo_b", [1024, 1024], BF16, kind="Internal").ap()
    w1o_b = dt("w1o_b", [1024, 4096], BF16, kind="Internal").ap()
    w2o_b = dt("w2o_b", [4096, 1024], BF16, kind="Internal").ap()
    wqi_b = dt("wqi_b", [6 * 1024, 3072], BF16, kind="Internal").ap()
    woi_b = dt("woi_b", [6 * 1024, 1024], BF16, kind="Internal").ap()
    w1i_b = dt("w1i_b", [6 * 1024, 5460], BF16, kind="Internal").ap()
    w2i_b = dt("w2i_b", [6 * DI, 1024], BF16, kind="Internal").ap()
    tab_b = dt("tab_b", [128, 4 * L_TOK], BF16, kind="Internal").ap()
    k = Ker(nc)
    with nc.Block() as block:
        pp = PsumPool(k)
        Bf = make_bufs(k)
        b1o = k.sbuf("b1o", (128, 32), F32)
        oco = k.sbuf("oco", (128, 32), F32)
        b1t = k.sbuf("b1t", (128, 44), F32)

        @block.tensor
        def _(tensor):
            C = make_consts(k, cd["hcbf"], cd["hcsel"], cd["hconesf"],
                            cd["hcmisc"], cd["hcmask"], cd["hcsc"],
                            cd["hcln"])
            k.dma("pool", b1o[:, :], b1ho, key="cst", wr=(b1o,))
            k.dma("pool", oco[:, :], och, key="cst", wr=(oco,))
            emit_w_prep(k, Bf, [
                (wqo, wqo_b, 1024, 3072), (woo, woo_b, 1024, 1024),
                (w1o, w1o_b, 1024, 4096), (w2o, w2o_b, 4096, 1024),
                (tabh.ap(), tab_b, 128, 4 * L_TOK),
                (wqi, wqi_b, 6 * 1024, 3072), (woi, woi_b, 6 * 1024, 1024),
                (w1i, w1i_b, 6 * 1024, 5460), (w2i, w2i_b, 6 * DI, 1024),
            ])
            tab_bd = DBuf(tab_b)
            emit_outer_block(k, pp, Bf, C, x0w, x0g, xn, x1h, xc, tg,
                             wqo_b, pjh, woo_b, w1o_b, w2o_b, b1o, oco)
            ranges = layer_ranges()
            for li in range(6):
                al, fl = ranges[li]
                k.dma("pool", b1t[:, :], b1i[:, 44 * li: 44 * li + 44],
                      key="cst", wr=(b1t,))
                bm_l = DBuf(bmi[:, 3072 * li: 3072 * li + 3072])
                emit_inner_layer(
                    k, pp, Bf, C, li, xc, xn,
                    wqi_b[1024 * li: 1024 * li + 1024, :],
                    woi_b[1024 * li: 1024 * li + 1024, :],
                    w1i_b[1024 * li: 1024 * li + 1024, :],
                    w2i_b[DI * li: DI * li + DI, :],
                    tab_bd, bm_l, b1t, al, fl, D_L)
            emit_final(k, pp, Bf, C, xc, x1h, out)
            for t in k.barrier_tokens():
                k.wait("pool", t)
    return nc


def _lncols_all(lng, lnb, b2):
    out = np.zeros((128, 144), np.float32)
    for li in range(6):
        out[:, 16 * li: 16 * li + 8] = chunk_cols_np(lng[li])
        out[:, 16 * li + 8: 16 * li + 16] = chunk_cols_np(lnb[li])
        out[:, 96 + 8 * li: 104 + 8 * li] = chunk_cols_np(b2[li])
    return out


def kernel_fused(inputs):
    f32 = lambda a: np.asarray(a, dtype=np.float32)
    x0 = f32(inputs["L0_attn_x"])
    proj = f32(inputs["L0_attn_proj"])
    wqkv_o = f32(inputs["L0_attn_w_qkv"])
    wqkv_i = f32(inputs["L0_ffn_Wqkv"])
    qs_i = f32(inputs["L0_ffn_q_scale"])
    ks_i = f32(inputs["L0_ffn_k_scale"])

    cbf, csel, conesf = host_const_arrays()
    band = band_masks_np().astype(BF)
    cmisc = []
    for r in range(2):
        cm = np.zeros((128, 8), np.float32)
        cm[:, 0], cm[:, 1], cm[:, 2] = -8.0, 8.0, float(r)
        cm[:, 3], cm[:, 4], cm[:, 5], cm[:, 6] = -1.0, -0.5, EPS_LN, EPS_ATTN
        cmisc.append(cm)

    pj = np.zeros((128, 2048), np.float32)
    for c in range(8):
        pj[0:64, 256 * c: 256 * c + 256] = proj[2 * c]
        pj[64:128, 256 * c: 256 * c + 256] = proj[2 * c + 1]
    sc_all = np.zeros((128, 24), np.float32)
    for li in range(6):
        sc_all[:, 4 * li: 4 * li + 4] = scale_cols_np(qs_i[li], ks_i[li])
    ln_all = _lncols_all(f32(inputs["L0_ffn_ln_g"]),
                         f32(inputs["L0_ffn_ln_b"]),
                         f32(inputs["L0_ffn_b2"]))
    b1all = np.concatenate(
        [_b1t_inner(f32(inputs["L0_ffn_b1"])[li]) for li in range(6)], 1)
    b1all = np.ascontiguousarray(b1all)
    ranges = layer_ranges()
    bmi_r = []
    for r in range(2):
        bmi_r.append(np.ascontiguousarray(np.concatenate(
            [boundary_masks_np(ranges[li][0], light=(r == 0))
             for li in range(6)], 1)).astype(BF))
    wqi = np.concatenate([perm_wqkv_np(wqkv_i[li]) for li in range(6)], 0)
    woi = f32(inputs["L0_ffn_Wo"]).reshape(6 * 1024, 1024)
    w1i = f32(inputs["L0_ffn_W1"]).reshape(6 * 1024, 5460)
    w2i = f32(inputs["L0_ffn_W2"]).reshape(6 * DI, 1024)

    base = {
        "hcbf": cbf, "hcsel": csel, "hconesf": conesf, "hcmask": band,
        "hcsc": sc_all, "hcln": ln_all,
        "wqo": np.ascontiguousarray(wqkv_o.T), "hprojT": pj,
        "woo": np.ascontiguousarray(f32(inputs["L0_attn_w_out"]).T),
        "w1o": np.ascontiguousarray(f32(inputs["L0_attn_w1"]).T),
        "w2o": np.ascontiguousarray(f32(inputs["L0_attn_w2"]).T),
        "b1ho": np.ascontiguousarray(
            f32(inputs["L0_attn_b1"]).reshape(32, 128).T),
        "och": np.ascontiguousarray(np.concatenate(
            [chunk_cols_np(f32(inputs["L0_attn_b_out"])),
             chunk_cols_np(f32(inputs["L0_attn_gamma"])),
             chunk_cols_np(f32(inputs["L0_attn_beta"])),
             chunk_cols_np(f32(inputs["L0_attn_b2"]))], 1)),
        "wqi": wqi, "woi": woi, "w1i": w1i, "w2i": w2i, "b1i": b1all,
    }
    maps = []
    for core in range(8):
        b, r = core // 2, core % 2
        xw = (np.concatenate([np.zeros((D_L, D), np.float32),
                              x0[b, :S_L]], 0)
              if r == 0 else x0[b, ORG_H:])
        org = -D_L if r == 0 else ORG_H
        m = dict(base)
        m.update({
            "x0w": _fm(xw), "x0g": _fm(x0[b]),
            "tabh": np.ascontiguousarray(
                np.concatenate(rotary_tables_np(org, L_TOK), 1)),
            "bmi": bmi_r[r], "hcmisc": cmisc[r],
        })
        maps.append(m)
    nc = build_fused()
    res = run_bass_kernel_spmd(nc, maps, core_ids=list(range(8))).results
    span = L_TOK - D_L
    y = np.empty((B, N, D), np.float32)
    for b in range(B):
        y[b, :S_L] = _defm(np.asarray(res[2 * b]["out"]), span)
        y[b, S_L:] = _defm(np.asarray(res[2 * b + 1]["out"]), span)[512:]
    return y


def kernel(**inputs):
    return kernel_fused(inputs)


# version 08
# speedup vs baseline: 1.1518x; 1.0160x over previous
import numpy as np
import ml_dtypes
import concourse.bass as bass
import concourse.mybir as mybir
from concourse.bass_utils import run_bass_kernel_spmd

F32 = mybir.dt.float32
BF16 = mybir.dt.bfloat16
AF = mybir.ActivationFunctionType
ALU = mybir.AluOpType
BF = ml_dtypes.bfloat16

B, N, D = 4, 4096, 1024
H, HD = 16, 64
PF = 256
WIN = 256
DI = 2730
EPS_LN = 1e-5
EPS_ATTN = 1e-8
NCH = D // 128

L_TOK = 3584
S_L = 2304
ORG_H = N - L_TOK
D_L = L_TOK - S_L
W_BOUND = D_L // WIN
OUT0 = 244

ROT_PERM = np.arange(64).reshape(2, 32).T.reshape(-1)
SHUF_MASK = [i for p in range(16) for i in (2 * p + 1, 2 * p)]

TBUF = 1024
FFNP = 896


def ceil_div(a, b):
    return -(-a // b)


def even(x):
    return x - (x % 2)


def layer_ranges():
    out = []
    V = OUT0
    for i in range(6):
        A = V + 257
        wf = A // WIN
        attn_lo = min(D_L, wf * WIN)
        F = min(D_L, even(A + 1))
        out.append((attn_lo, F))
        V = A + 1
    assert V <= D_L + (N - ORG_H - S_L), ("pyramid chain too deep", V)
    return out


class Ker:

    ENGS = ("pe", "act", "dve", "pool", "sp")

    def __init__(self, nc):
        self.nc = nc
        self.eng = {
            "pe": nc.tensor, "act": nc.scalar, "dve": nc.vector,
            "pool": nc.gpsimd, "sp": nc.sync,
        }
        self._stack = []
        self.prog = {}
        self.count = {}
        for e in self.ENGS:
            self.prog[e] = self._enter(nc.semaphore(f"prog_{e}"))
            self.count[e] = 0
        self.dmasem = {}
        self.dmacnt = {}
        self.dmaout = {}
        self.waited = {e: {} for e in self.ENGS}

    def _enter(self, cm):
        v = cm.__enter__()
        self._stack.append(cm)
        return v

    def sbuf(self, name, shape, dtype):
        return Buf(self._enter(self.nc.sbuf_tensor(name, list(shape), dtype)))

    def psum(self, name, shape, dtype=F32):
        return Buf(self._enter(self.nc.psum_tensor(name, list(shape), dtype)))

    def wait(self, e, tok):
        if tok is None:
            return
        sem, val = tok
        if self.waited[e].get(sem.name, 0) >= val:
            return
        self.eng[e].wait_ge(sem, val)
        self.waited[e][sem.name] = val

    def _collect(self, e, deps, rd, wr):
        need = {}

        def add(tok):
            if tok is None:
                return
            sem, val = tok
            if self.waited[e].get(sem.name, 0) >= val:
                return
            if need.get(sem.name, (None, 0))[1] < val:
                need[sem.name] = (sem, val)
        for b in rd:
            add(b.w)
        for b in wr:
            add(b.w)
            for t in b.r:
                add(t)
        for t in deps:
            add(t)
        return list(need.values())

    def _pre(self, e, pend):
        for sem, val in pend[:-1]:
            self.eng[e].wait_ge(sem, val)
            self.waited[e][sem.name] = val
        if pend:
            sem, val = pend[-1]
            self.waited[e][sem.name] = val
            return (sem, val)
        return None

    def _post(self, tok, rd, wr):
        for b in rd:
            b.r.append(tok)
        for b in wr:
            b.w = tok
            b.r = []
        return tok

    def op(self, e, name, *args, deps=(), rd=(), wr=(), **kw):
        att = self._pre(e, self._collect(e, deps, rd, wr))
        inst = getattr(self.eng[e], name)(*args, **kw)
        if att is not None:
            inst._wait_ge(att[0], att[1])
        self.count[e] += 1
        tok = (self.prog[e], self.count[e])
        inst.then_inc(self.prog[e], 1)
        return self._post(tok, rd, wr)

    def dma(self, q, out, in_, key=None, deps=(), rd=(), wr=(), slow=False):
        e = q
        key = key if key is not None else q
        if key not in self.dmasem:
            self.dmasem[key] = self._enter(self.nc.semaphore(f"dma_{key}"))
            self.dmacnt[key] = 0
            self.dmaout[key] = None
        pend = self._collect(e, (self.dmaout[key],) + tuple(deps), rd, wr)
        att = self._pre(e, pend)
        inst = self.eng[e].dma_start(out=out, in_=in_,
                                     allow_slow_non_contiguous=slow)
        if att is not None:
            inst._wait_ge(att[0], att[1])
        self.dmacnt[key] += 16
        tok = (self.dmasem[key], self.dmacnt[key])
        inst.then_inc(self.dmasem[key], 16)
        self.dmaout[key] = tok
        return self._post(tok, rd, wr)

    def barrier_tokens(self):
        toks = [(self.prog[e], self.count[e]) for e in self.ENGS]
        toks += [(self.dmasem[kq], self.dmacnt[kq]) for kq in self.dmasem]
        return [t for t in toks if t[1] > 0]


class Buf:
    def __init__(self, h):
        self.h = h
        self.w = None
        self.r = []

    def __getitem__(self, idx):
        return self.h[idx]

    def ap(self):
        return self.h.ap()


class DBuf(Buf):
    def __init__(self, ap):
        self.a = ap
        self.w = None
        self.r = []

    def __getitem__(self, idx):
        return self.a[idx]

    def ap(self):
        return self.a


class PsumSlice:
    def __init__(self, t, col0):
        self.t = t
        self.col0 = col0
        self.w = None
        self.r = []

    def __getitem__(self, idx):
        p, c = idx
        c0 = self.col0 + (c.start or 0)
        c1 = self.col0 + (512 if c.stop is None else c.stop)
        return self.t[p, c0:c1]

    def ap(self):
        return self.t.ap()[:, self.col0: self.col0 + 512]


class PsumPool:
    def __init__(self, k):
        self.big = k.psum("psA", (128, 4096), F32)
        self.banks = [PsumSlice(self.big.h, i * 512) for i in range(8)]
        self.i = 0

    def take(self):
        b = self.banks[self.i % 8]
        self.i += 1
        return b

    def take_contig(self, n):
        s = self.i % 8
        if s + n > 8:
            self.i += 8 - s
            s = 0
        self.i += n
        return [self.banks[s + j] for j in range(n)]

    def span_ap(self, b0, nbanks, prow, ncols_per, stride=512):
        return self.big.h.ap()[prow[0]:prow[1],
                               b0.col0: b0.col0 + nbanks * 512].rearrange(
            "p (b c) -> p b c", b=nbanks)[:, :, 0:ncols_per]


def mm(k, pb, pslice, lh, rh, rd=(), start=False, stop=False):
    if start:
        att = k._pre("pe", k._collect("pe", (pb.w,) + tuple(pb.r), rd, ()))
    else:
        att = k._pre("pe", k._collect("pe", (), rd, ()))
    inst = k.eng["pe"].matmul(pslice, lh, rh, start=start, stop=stop)
    if att is not None:
        inst._wait_ge(att[0], att[1])
    k.count["pe"] += 1
    tok = (k.prog["pe"], k.count["pe"])
    inst.then_inc(k.prog["pe"], 1)
    for b in rd:
        b.r.append(tok)
    if start:
        pb.r = []
    pb.w = tok
    return tok


def rotary_tables_np(origin, L):
    half = np.arange(0, HD, 2, dtype=np.float64)
    inv_freq = 1.0 / 10000.0 ** (half / HD)
    t = np.arange(origin, origin + L, dtype=np.float64)
    freqs = np.concatenate([t[:, None] * inv_freq[None, :]] * 2, -1)
    base_scale = (half + 0.4 * HD) / (1.4 * HD)
    power = (t - N // 2) / (WIN // 2)
    scale = np.concatenate([base_scale[None, :] ** power[:, None]] * 2, -1)
    cos, sin = np.cos(freqs), np.sin(freqs)
    sign = np.where(np.arange(HD) % 2 == 0, -1.0, 1.0)
    tabs = []
    for sc in (scale, 1.0 / scale):
        tabs.append((cos * sc)[:, ROT_PERM])
        tabs.append((sin * sc)[:, ROT_PERM] * sign[None, :])
    return [np.ascontiguousarray(np.tile(a.T, (2, 1))).astype(np.float32)
            for a in tabs]


def perm_wqkv_np(wqkv):
    w = wqkv.copy()
    for h in range(H):
        w[:, h * HD: (h + 1) * HD] = wqkv[:, h * HD + ROT_PERM]
        w[:, D + h * HD: D + (h + 1) * HD] = wqkv[:, D + h * HD + ROT_PERM]
    return np.ascontiguousarray(w)


def scale_cols_np(qs, ks):
    pair = np.arange(HD).reshape(32, 2)[:, ::-1].reshape(-1)
    qi, ki = qs[ROT_PERM], ks[ROT_PERM]
    cols = np.stack([qi, qi[pair], ki, ki[pair]], 1)
    return np.ascontiguousarray(np.tile(cols, (2, 1))).astype(np.float32)


def band_masks_np(key_g0=None, blk_key0_g=None, blk_q0_g=None):
    q = np.arange(512)
    out = np.zeros((128, 6 * 512), np.float32)
    for kc in range(6):
        j = np.arange(128) + kc * 128
        m = (q[None, :] <= j[:, None]) & (j[:, None] <= q[None, :] + 256)
        if key_g0 is not None:
            keybad = (blk_key0_g + j[:, None]) < key_g0
            qreal = (blk_q0_g + q[None, :]) >= 0
            m = m & ~(keybad & qreal)
        out[:, kc * 512:(kc + 1) * 512] = np.where(m, 0.0, -1e5)
    return out


def chunk_cols_np(v):
    nch = len(v) // 128
    return np.ascontiguousarray(v.reshape(nch, 128).T).astype(np.float32)


def load_w_ff(k, wtile, wsrc, din, mc, mcols=128, key="w", base_col=0,
              q="sp"):
    kc_n = din // 128
    src = wsrc.rearrange("(kc p) m -> p kc m", p=128)
    src = src[:, :, base_col + mc * 128: base_col + mc * 128 + mcols]
    dst = wtile.h.ap()[:, 0:kc_n * mcols].rearrange(
        "p (kc m) -> p kc m", kc=kc_n)
    return k.dma(q, dst, src, key=key, wr=(wtile,))


def emit_ln(k, pp, st, xbf, tbuf, t0, tn, out_bf, C, gcol=None, bcol=None,
            eps=EPS_LN):
    sq0, sq1, row, tmp = st["sq0"], st["sq1"], st["row"], st["tmp"]
    ones2 = C["ones2"]
    nt = ceil_div(tn, 512)
    for j in range(nt):
        c0, cn = j * 512, min(512, tn - j * 512)
        ones_all = C["ones64"]
        pa = pp.take()
        for kc in range(NCH):
            mm(k, pa, pa[:1, :cn], ones_all[:, 0:1],
               xbf[:, kc * tbuf + t0 + c0: kc * tbuf + t0 + c0 + cn],
               rd=(ones_all, xbf), start=(kc == 0), stop=(kc == NCH - 1))
        pb = pp.take()
        for kc in range(NCH):
            sq = (sq0, sq1)[kc % 2]
            k.op("act", "activation", sq[:, :cn],
                 xbf[:, kc * tbuf + t0 + c0: kc * tbuf + t0 + c0 + cn],
                 AF.Square, rd=(xbf,), wr=(sq,))
            mm(k, pb, pb[:1, :cn], ones_all[:, 0:1], sq[:, :cn],
               rd=(ones_all, sq), start=(kc == 0), stop=(kc == NCH - 1))
        mu = row[:1, c0: c0 + cn]
        va = row[:1, tn + c0: tn + c0 + cn]
        t1 = tmp[:1, :cn]
        k.op("dve", "tensor_scalar_mul", mu, pa[:1, :cn], 1.0 / D,
             rd=(pa,), wr=(row,))
        k.op("dve", "tensor_scalar_mul", va, pb[:1, :cn], 1.0 / D,
             rd=(pb,), wr=(row,))
        k.op("dve", "tensor_mul", t1, mu, mu, rd=(row,), wr=(tmp,))
        k.op("dve", "tensor_sub", va, va, t1, rd=(row, tmp), wr=(row,))
        m8l = C["m8"]
        k.op("act", "activation", va, va, AF.Ln, bias=m8l[:1, 5:6],
             rd=(row, m8l), wr=(row,))
        k.op("act", "activation", va, va, AF.Exp, scale=m8l[:1, 4:5],
             rd=(row, m8l), wr=(row,))
        pm = pp.take()
        mm(k, pm, pm[:, :cn], C["onesf"][:1, 0:128], mu,
           rd=(C["onesf"], row), start=True, stop=True)
        pr = pp.take()
        mm(k, pr, pr[:, :cn], C["onesf"][:1, 0:128], va,
           rd=(C["onesf"], row), start=True, stop=True)
        for kc in range(NCH):
            xs = xbf[:, kc * tbuf + t0 + c0: kc * tbuf + t0 + c0 + cn]
            os_ = out_bf[:, kc * tbuf + t0 + c0: kc * tbuf + t0 + c0 + cn]
            tm = tmp[:, :cn]
            k.op("dve", "tensor_sub", tm, xs, pm[:, :cn], rd=(xbf, pm),
                 wr=(tmp,))
            if gcol is None:
                k.op("dve", "tensor_mul", os_, tm, pr[:, :cn], rd=(tmp, pr),
                     wr=(out_bf,))
            else:
                gb, gc = gcol
                bb, bc = bcol
                k.op("dve", "tensor_mul", tm, tm, pr[:, :cn], rd=(tmp, pr),
                     wr=(tmp,))
                k.op("act", "activation", os_, tm, AF.Identity,
                     bias=bb[:, bc + kc: bc + kc + 1],
                     scale=gb[:, gc + kc: gc + kc + 1],
                     rd=(tmp, gb, bb), wr=(out_bf,))


def emit_shift_cast(k, xs_bf, tbuf, dcol, src, src_tbuf, src_off, cn,
                    flag=None, fcol=None):
    dst3 = xs_bf.ap()[:, 0: NCH * tbuf].rearrange(
        "p (kc t) -> p kc t", kc=NCH)
    src3 = src.ap()[:, 0: NCH * src_tbuf].rearrange(
        "p (kc t) -> p kc t", kc=NCH)
    for half in range(2):
        sh = half
        s0 = src_off - sh
        if s0 >= 0:
            k.op("act", "copy",
                 dst3[:, 4 * half: 4 * half + 4, dcol: dcol + cn],
                 src3[:, 4 * half: 4 * half + 4, s0: s0 + cn],
                 rd=(src,), wr=(xs_bf,))
        else:
            k.op("act", "copy",
                 dst3[:, 4 * half: 4 * half + 4, dcol + 1: dcol + cn],
                 src3[:, 4 * half: 4 * half + 4, 0: cn - 1],
                 rd=(src,), wr=(xs_bf,))
            k.op("act", "copy",
                 dst3[:, 4 * half: 4 * half + 4, dcol: dcol + 1],
                 src3[:, 4 * half: 4 * half + 4, 0: 1],
                 rd=(src,), wr=(xs_bf,))
    if flag is not None and fcol is not None:
        for kc in range(4, NCH):
            cpos = kc * tbuf + dcol + fcol
            k.op("dve", "tensor_scalar_mul", xs_bf[:, cpos: cpos + 1],
                 xs_bf[:, cpos: cpos + 1], flag.ap()[:, 0:1],
                 rd=(xs_bf, flag), wr=(xs_bf,))


def emit_qkv(k, pp, Bf, C, xs_bf, xs_str, rn, t_dst, wq_src):
    q_bf, k_bf, v_tm = Bf["qb"], Bf["kb"], Bf["vt"]
    wt = [Bf["wt0"], Bf["wt1"]]
    stg = Bf["stg0"]
    srow, sq = Bf["srow"], Bf["sq"]
    rt1, rt2 = Bf["rt1"], Bf["rt2"]
    ones2, sel2 = C["ones2"], C["sel2"]
    tabs = Bf["tabs"]
    KBUF = 3 * WIN
    for qk, tbase in (("q", 0), ("k", 2)):
        base_mc = 0 if qk == "q" else NCH
        load_w_ff(k, wt[0], wq_src, D, base_mc, 128, key="wq0")
        for c in range(NCH):
            if c + 1 < NCH:
                load_w_ff(k, wt[(c + 1) % 2], wq_src, D, base_mc + c + 1,
                          128, key="wq" + str((c + 1) % 2),
                          q="sp" if (c + 1) % 2 == 0 else "pool")
            wtile = wt[c % 2]
            for j in range(ceil_div(rn, 512)):
                c0, cn = j * 512, min(512, rn - j * 512)
                pq = pp.take()
                for kc in range(NCH):
                    mm(k, pq, pq[:, :cn], wtile[:, kc * 128: kc * 128 + 128],
                       xs_bf[:, kc * xs_str + c0: kc * xs_str + c0 + cn],
                       rd=(wtile, xs_bf), start=(kc == 0),
                       stop=(kc == NCH - 1))
                ps = pp.take()
                k.op("act", "activation", sq[:, :cn], pq[:, :cn], AF.Square,
                     rd=(pq,), wr=(sq,))
                mm(k, ps, ps[:2, :cn], ones2[:, 0:2], sq[:, :cn],
                   rd=(ones2, sq), start=True, stop=True)
                rr = srow[:2, c0: c0 + cn]
                k.op("act", "activation", rr, ps[:2, :cn], AF.Ln,
                     rd=(ps,), wr=(srow,))
                k.op("act", "activation", rr, rr, AF.Exp,
                     scale=C["m8"][:2, 4:5], rd=(srow, C["m8"]), wr=(srow,))
                k.op("dve", "tensor_scalar_min", rr, rr, 1e12, rd=(srow,),
                     wr=(srow,))
                pbx = pp.take()
                mm(k, pbx, pbx[:, :cn], sel2[:2, :], rr, rd=(sel2, srow),
                   start=True, stop=True)
                tcs = tabs[:, tbase * 512 + c0: tbase * 512 + c0 + cn]
                tss = tabs[:, (tbase + 1) * 512 + c0:
                           (tbase + 1) * 512 + c0 + cn]
                k.op("dve", "tensor_mul", rt1[:, :cn], pq[:, :cn], tcs,
                     rd=(pq, tabs), wr=(rt1,))
                k.op("dve", "stream_shuffle", rt2[:, :cn], pq[:, :cn],
                     SHUF_MASK, rd=(pq,), wr=(rt2,))
                k.op("pool", "tensor_mul", rt2[:, :cn], rt2[:, :cn], tss,
                     rd=(rt2, tabs), wr=(rt2,))
                k.op("dve", "tensor_add", rt1[:, :cn], rt1[:, :cn],
                     rt2[:, :cn], rd=(rt1, rt2), wr=(rt1,))
                if qk == "q":
                    k.op("dve", "tensor_mul", q_bf[:, c * 512 + c0:
                                                   c * 512 + c0 + cn],
                         rt1[:, :cn], pbx[:, :cn], rd=(rt1, pbx), wr=(q_bf,))
                else:
                    w0 = (t_dst + c0) % KBUF
                    n1 = min(cn, KBUF - w0)
                    k.op("dve", "tensor_mul",
                         k_bf[:, c * KBUF + w0: c * KBUF + w0 + n1],
                         rt1[:, :n1], pbx[:, :n1], rd=(rt1, pbx), wr=(k_bf,))
                    if n1 < cn:
                        k.op("dve", "tensor_mul",
                             k_bf[:, c * KBUF: c * KBUF + cn - n1],
                             rt1[:, n1:cn], pbx[:, n1:cn], rd=(rt1, pbx),
                             wr=(k_bf,))
    wv = Bf["wv"]
    k.dma("sp", wv[:, 0:NCH * 1024].rearrange("p (kc m) -> p kc m", kc=NCH),
          wq_src.rearrange("(kc p) m -> p kc m", p=128)[:, :, 2 * D: 3 * D],
          key="wv", wr=(wv,))
    for tc in range(ceil_div(rn, 128)):
        cn = min(128, rn - tc * 128)
        slot = ((t_dst + tc * 128) // 128) % 6
        for half in range(2):
            pv = pp.take()
            for kc in range(NCH):
                mm(k, pv, pv[:cn, :512],
                   xs_bf[:, kc * xs_str + tc * 128: kc * xs_str + tc * 128 + cn],
                   wv[:, kc * 1024 + half * 512:
                      kc * 1024 + half * 512 + 512],
                   rd=(xs_bf, wv), start=(kc == 0), stop=(kc == NCH - 1))
            k.op("act", "copy",
                 v_tm[:cn, slot * 1024 + half * 512:
                      slot * 1024 + half * 512 + 512],
                 pv[:cn, :512], rd=(pv,), wr=(v_tm,))


def emit_block_attn(k, pp, Bf, C, wq0, nw, q_lo, q_hi, mask, mcol0):
    q_bf, k_bf, v_tm, attn_bf = Bf["qb"], Bf["kb"], Bf["vt"], Bf["at"]
    Pb, rc = Bf["p0"], Bf["rec0"]
    ones64 = C["ones64"]
    m8 = C["m8"]
    KBUF = 3 * WIN
    qa = max(q_lo, wq0 * WIN)
    qb_ = min(q_hi, (wq0 + nw) * WIN)
    qn = qb_ - qa
    qrel = qa - wq0 * WIN
    nkc = 2 * (nw + 1)
    for h in range(H):
        c, hp = h // 2, h % 2
        plo, phi = (0, 64) if hp == 0 else (64, 128)
        pss = pp.take_contig(nkc)
        pd = pp.take()
        pvp = pp.take()
        for kc in range(nkc):
            key0 = (wq0 - 1) * WIN + kc * 128
            kcol = (key0 % KBUF)
            mm(k, pss[kc], pss[kc][:, :qn],
               k_bf[plo:phi, c * KBUF + kcol: c * KBUF + kcol + 128],
               q_bf[plo:phi, c * 512 + qa - wq0 * WIN:
                    c * 512 + qa - wq0 * WIN + qn],
               rd=(k_bf, q_bf), start=True, stop=True)
        for hb0, hb1 in ((0, nkc // 2), (nkc // 2, nkc)):
            nb = hb1 - hb0
            sspan = pp.span_ap(pss[hb0], nb, (0, 128), qn)
            mspan = mask.ap()[:, mcol0 + hb0 * 512:
                              mcol0 + hb1 * 512].rearrange(
                "p (b c) -> p b c", b=nb)[:, :, qrel: qrel + qn]
            pbspan = Pb.ap()[:, hb0 * 512: hb1 * 512].rearrange(
                "p (b c) -> p b c", b=nb)[:, :, 0:qn]
            ph = tuple(pss[hb0:hb1])
            k.op("dve", "tensor_add", sspan, sspan, mspan,
                 rd=(mask,) + ph, wr=ph)
            k.op("act", "activation", pbspan, sspan, AF.Exp,
                 bias=m8[:, 0:1], scale=m8[:, 1:2],
                 rd=ph + (m8,), wr=(Pb,))
            for kc in range(hb0, hb1):
                key0 = (wq0 - 1) * WIN + kc * 128
                mm(k, pd, pd[plo:phi, :qn], ones64[:, 0:64],
                   Pb[:, kc * 512: kc * 512 + qn], rd=(ones64, Pb),
                   start=(kc == 0), stop=(kc == nkc - 1))
                mm(k, pvp, pvp[plo:phi, :qn],
                   v_tm[:, (key0 // 128 % 6) * 1024 + h * 64:
                        (key0 // 128 % 6) * 1024 + h * 64 + 64],
                   Pb[:, kc * 512: kc * 512 + qn], rd=(v_tm, Pb),
                   start=(kc == 0), stop=(kc == nkc - 1))
        k.op("act", "activation", rc[plo:phi, :qn], pd[plo:phi, :qn],
             AF.Ln, rd=(pd,), wr=(rc,))
        k.op("act", "activation", rc[plo:phi, :qn], rc[plo:phi, :qn],
             AF.Exp, scale=m8.ap()[plo:phi, 3:4], rd=(rc, m8), wr=(rc,))
        k.op("dve", "tensor_mul",
             attn_bf[plo:phi, c * 512 + qa - wq0 * WIN:
                     c * 512 + qa - wq0 * WIN + qn],
             pvp[plo:phi, :qn], rc[plo:phi, :qn], rd=(pvp, rc),
             wr=(attn_bf,))


def emit_wo_residual(k, pp, Bf, tn, wo_t, x_load, x_out):
    attn_bf, wob = Bf["at"], Bf["xacc"]
    pbs = []
    for mc in range(NCH):
        pb = pp.take()
        for kc in range(NCH):
            mm(k, pb, pb[:, :tn],
               wo_t[:, kc * 1024 + mc * 128: kc * 1024 + mc * 128 + 128],
               attn_bf[:, kc * 512: kc * 512 + tn],
               rd=(wo_t, attn_bf), start=(kc == 0), stop=(kc == NCH - 1))
        pbs.append(pb)
    for j in range(ceil_div(tn, 256)):
        c0, cn = j * 256, min(256, tn - j * 256)
        xres, xsl = x_load(c0, cn)
        for mc in range(NCH):
            k.op("dve", "tensor_add",
                 wob[:, mc * 512 + c0: mc * 512 + c0 + cn],
                 pbs[mc][:, c0: c0 + cn], xsl(mc, cn),
                 rd=(pbs[mc], xres), wr=(wob,))
    x_out(wob, tn)


def emit_ffn(k, pp, Bf, C, xn_bf, tbuf, tn, w1src, w2src, b1t, b2col,
             x_src, x_out):
    JN = ceil_div(DI, 128)
    w1a = [Bf["wt0"], Bf["wt1"]]
    w1g = [Bf["wg0"], Bf["wg1"]]
    w2t, hg, xacc, xnew, gbf = (Bf["vt"], Bf["kb"], Bf["xacc"], Bf["xnew"],
                                Bf["gbf"])
    groups = [list(range(0, 11)), list(range(11, JN))]
    for gi, grp in enumerate(groups):
        for jj, j in enumerate(grp):
            rj = min(128, DI - 128 * j)
            wa, wg = w1a[jj % 2], w1g[jj % 2]
            srca = w1src.rearrange("(kc p) m -> p kc m", p=128)
            k.dma("sp", wa[:, 0:NCH * rj].rearrange(
                      "p (kc m) -> p kc m", kc=NCH),
                  srca[:, :, 128 * j: 128 * j + rj],
                  key="w1a" + str(jj % 2), wr=(wa,))
            k.dma("pool", wg[:, 0:NCH * rj].rearrange(
                      "p (kc m) -> p kc m", kc=NCH),
                  srca[:, :, DI + 128 * j: DI + 128 * j + rj],
                  key="w1g" + str(jj % 2), wr=(wg,))
            for j2 in range(ceil_div(tn, 512)):
                c0, cn = j2 * 512, min(512, tn - j2 * 512)
                pg = pp.take()
                pa = pp.take()
                for kc in range(NCH):
                    mm(k, pg, pg[:rj, :cn], wg[:, kc * rj: kc * rj + rj],
                       xn_bf[:, kc * tbuf + c0: kc * tbuf + c0 + cn],
                       rd=(wg, xn_bf), start=(kc == 0), stop=(kc == NCH - 1))
                for kc in range(NCH):
                    mm(k, pa, pa[:rj, :cn], wa[:, kc * rj: kc * rj + rj],
                       xn_bf[:, kc * tbuf + c0: kc * tbuf + c0 + cn],
                       rd=(wa, xn_bf), start=(kc == 0), stop=(kc == NCH - 1))
                k.op("act", "activation", gbf[:rj, :cn], pg[:rj, :cn],
                     AF.Gelu, bias=b1t[:rj, 22 + j: 23 + j], rd=(pg, b1t),
                     wr=(gbf,))
                k.op("dve", "scalar_tensor_tensor",
                     hg[:rj, jj * tn + c0: jj * tn + c0 + cn],
                     pa[:rj, :cn], b1t.ap()[0:rj, j: j + 1], gbf[:rj, :cn],
                     ALU.add, ALU.mult, rd=(pa, b1t, gbf), wr=(hg,))
        for jj, j in enumerate(grp):
            rj = min(128, DI - 128 * j)
            k.dma("pool", w2t[0:rj, jj * 1024: (jj + 1) * 1024],
                  w2src[128 * j: 128 * j + rj, :], key="w2", wr=(w2t,))
        for tt in range(ceil_div(tn, 512)):
            c0, cn = tt * 512, min(512, tn - tt * 512)
            for mh in range(2):
                pxs = pp.take_contig(4)
                for mi in range(4):
                    mc = mh * 4 + mi
                    px = pxs[mi]
                    for jj, j in enumerate(grp):
                        rj = min(128, DI - 128 * j)
                        mm(k, px, px[:, :cn],
                           w2t[0:rj, jj * 1024 + mc * 128:
                               jj * 1024 + mc * 128 + 128],
                           hg[0:rj, jj * tn + c0: jj * tn + c0 + cn],
                           rd=(w2t, hg), start=(jj == 0),
                           stop=(jj == len(grp) - 1))
                dst3 = xacc.ap()[:, mh * 4 * tn: mh * 4 * tn
                                 + 4 * tn].rearrange(
                    "p (m t) -> p m t", m=4)[:, :, c0: c0 + cn]
                src3 = pp.span_ap(pxs[0], 4, (0, 128), cn)
                if gi == 0:
                    k.op("dve", "tensor_copy", dst3, src3,
                         rd=tuple(pxs), wr=(xacc,))
                else:
                    k.op("dve", "tensor_add", dst3, dst3, src3,
                         rd=tuple(pxs) + (xacc,), wr=(xacc,))
    b2b, b2c = b2col
    for tt in range(ceil_div(tn, 256)):
        c0, cn = tt * 256, min(256, tn - tt * 256)
        xb, xsl = x_src(tt, cn)
        for mc in range(NCH):
            k.op("dve", "scalar_tensor_tensor",
                 xnew[:, mc * 256: mc * 256 + cn],
                 xacc[:, mc * tn + c0: mc * tn + c0 + cn],
                 b2b.ap()[:, b2c + mc: b2c + mc + 1], xsl(mc, cn),
                 ALU.add, ALU.add, rd=(xacc, b2b, xb), wr=(xnew,))
        x_out(tt, xnew, cn)


XS_STR = 544


def emit_inner_layer(k, pp, Bf, C, li, x_cur, x_nxt, wq_src, wo_src, w1_src,
                     w2_src, tab_hbm, bm_hbm, b1t, attn_lo, ffn_lo,
                     gflag_col, lbuf=L_TOK, x_wr=None):
    if x_wr is None:
        x_wr = x_cur
    """One inner layer on the local window. x_cur: DBuf [128, NCH*lbuf]
    fp32 (input; ffn writes back on [ffn_lo, lbuf)); x_nxt: DBuf scratch
    (post-attn). tab_hbm: DBuf [128, 4*lbuf] rotary base tables.
    bm_hbm: DBuf [128, 6*512] this layer's boundary-block mask (per-core
    content; equals the standard band mask on heavy cores)."""
    sc = C["scalecols"]

    def x_tile_load(xsrc, lo, ln_, which):
        xt = Bf["xf"][which % 2]
        k.dma("sp", xt[:, : NCH * ln_].rearrange("p (kc t) -> p kc t",
                                                 kc=NCH),
              xsrc.ap().rearrange("p (kc t) -> p kc t", kc=NCH)[
                  :, :, lo: lo + ln_],
              key="xf" + str(which % 2), rd=(xsrc,), wr=(xt,))
        return xt

    def load_tables(r0, rn):
        tabs = Bf["tabs"]
        k.dma("pool", tabs.ap()[:, 0: 4 * 512].rearrange(
                  "p (ti t) -> p ti t", ti=4)[:, :, 0:rn],
              tab_hbm.ap().rearrange("p (ti t) -> p ti t", ti=4)[
                  :, :, r0: r0 + rn],
              key="tab0", rd=(tab_hbm,), wr=(tabs,))
        for ti in range(4):
            k.op("dve", "tensor_scalar_mul", tabs[:, ti * 512: ti * 512 + rn],
                 tabs[:, ti * 512: ti * 512 + rn],
                 sc.ap()[:, 4 * li + ti: 4 * li + ti + 1], rd=(tabs, sc),
                 wr=(tabs,))

    def shift_cast_range(xsrc, t0, tn):
        xs_bf = Bf["xs"]
        for j in range(ceil_div(tn, 256)):
            c0, cn = j * 256, min(256, tn - j * 256)
            lo = max(0, t0 + c0 - 1)
            ln_ = t0 + c0 + cn - lo
            xt = x_tile_load(xsrc, lo, ln_, j)
            off = t0 + c0 - lo
            fcol = None
            if gflag_col is not None and c0 <= gflag_col - t0 < c0 + cn:
                fcol = gflag_col - t0 - c0
            emit_shift_cast(k, xs_bf, XS_STR, c0, xt, ln_, off, cn,
                            C["flag"] if fcol is not None else None, fcol)
        return xs_bf

    k.dma("sp", C["bmask"][:, :], bm_hbm.ap(), key="bml",
          rd=(bm_hbm,), wr=(C["bmask"],))

    w_lo = attn_lo // WIN
    nwin = (lbuf - w_lo * WIN) // WIN
    blocks = []
    w = w_lo
    if nwin % 2 == 1:
        blocks.append((w, 1))
        w += 1
    while w < lbuf // WIN:
        blocks.append((w, 2))
        w += 2
    for bi, (wq0, nw) in enumerate(blocks):
        first = bi == 0
        if first:
            lb0 = (wq0 - 1) * WIN
            load_tables(lb0, WIN)
            xs = shift_cast_range(x_cur, lb0, WIN)
            emit_qkv(k, pp, Bf, C, xs, XS_STR, WIN, lb0, wq_src)
        t0 = wq0 * WIN
        tn = nw * WIN
        load_tables(t0, tn)
        xs = shift_cast_range(x_cur, t0, tn)
        emit_qkv(k, pp, Bf, C, xs, XS_STR, tn, t0, wq_src)
        bm = (wq0 <= W_BOUND_DYN[0] < wq0 + nw)
        mask = C["bmask"] if bm else C["masks"]
        emit_block_attn(k, pp, Bf, C, wq0, nw, wq0 * WIN, (wq0 + nw) * WIN,
                        mask, 0)
        wo_t = Bf["wv"]
        k.dma("sp", wo_t[:, 0:NCH * 1024].rearrange(
                  "p (kc m) -> p kc m", kc=NCH),
              wo_src.rearrange("(kc p) m -> p kc m", p=128), key="wv",
              wr=(wo_t,))
        q_lo = wq0 * WIN

        def x_load(c0, cn, q_lo=q_lo):
            xt = x_tile_load(x_cur, q_lo + c0, cn, c0 // 256)

            def sl(mc, cnn, xt=xt, cn=cn):
                return xt[:, mc * cn: mc * cn + cnn]
            return xt, sl

        def x_out(wob, gn, q_lo=q_lo):
            k.dma("sp", x_nxt.ap().rearrange("p (kc t) -> p kc t", kc=NCH)[
                      :, :, q_lo: q_lo + gn],
                  wob.ap()[:, 0: NCH * 512].rearrange(
                      "p (kc t) -> p kc t", kc=NCH)[:, :, 0:gn],
                  key="xw", rd=(wob,), wr=(x_nxt,))

        emit_wo_residual(k, pp, Bf, tn, wo_t, x_load, x_out)
    total = lbuf - ffn_lo
    for pi in range(ceil_div(total, FFNP)):
        p0 = ffn_lo + pi * FFNP
        pn = min(FFNP, lbuf - p0)
        xs2, xn = Bf["xs"], Bf["qb"]
        for j in range(ceil_div(pn, 256)):
            c0, cn = j * 256, min(256, pn - j * 256)
            xt = x_tile_load(x_nxt, p0 + c0 - 1, cn + 1, j)
            fcol = None
            if gflag_col is not None and c0 <= gflag_col - p0 < c0 + cn:
                fcol = gflag_col - p0 - c0
            emit_shift_cast(k, xs2, FFNP, c0, xt, cn + 1, 1, cn,
                            C["flag"] if fcol is not None else None, fcol)
        emit_ln(k, pp, Bf["lnst"], xs2, FFNP, 0, pn, xn, C,
                gcol=(C["lncols"], 16 * li), bcol=(C["lncols"], 16 * li + 8))

        def fx_src(tt, cn, p0=p0):
            xt = x_tile_load(x_nxt, p0 + tt * 256, cn, tt)

            def sl(mc, cnn, xt=xt, cn=cn):
                return xt[:, mc * cn: mc * cn + cnn]
            return xt, sl

        def fx_out(tt, xnew, cn, p0=p0):
            k.dma("sp", x_wr.ap().rearrange("p (kc t) -> p kc t", kc=NCH)[
                      :, :, p0 + tt * 256: p0 + tt * 256 + cn],
                  xnew.ap()[:, 0: NCH * 256].rearrange(
                      "p (mc t) -> p mc t", mc=NCH)[:, :, 0:cn],
                  key="xw", rd=(xnew,), wr=(x_wr,))

        emit_ffn(k, pp, Bf, C, xn, FFNP, pn, w1_src, w2_src, b1t,
                 (C["lncols"], 96 + 8 * li), fx_src, fx_out)


W_BOUND_DYN = [W_BOUND]


def layer_blocks(attn_lo, lbuf=L_TOK):
    w_lo = attn_lo // WIN
    nwin = (lbuf - w_lo * WIN) // WIN
    blocks = []
    w = w_lo
    if nwin % 2 == 1:
        blocks.append((w, 1))
        w += 1
    while w < lbuf // WIN:
        blocks.append((w, 2))
        w += 2
    return blocks


def boundary_masks_np(attn_lo, light, d_l=D_L, w_bound=None, lbuf=L_TOK):
    wb = w_bound if w_bound is not None else (d_l // WIN)
    blk = [b for b in layer_blocks(attn_lo, lbuf) if b[0] <= wb < b[0] + b[1]]
    if not blk or not light:
        return band_masks_np()
    wq0, nw = blk[0]
    return band_masks_np(key_g0=0, blk_key0_g=(wq0 - 1) * WIN - d_l,
                         blk_q0_g=wq0 * WIN - d_l)


def make_bufs(k):
    Bf = {
        "xs": k.sbuf("xs", (128, NCH * FFNP), BF16),
        "qb": k.sbuf("qb", (128, NCH * FFNP), BF16),
        "kb": k.sbuf("kb", (128, 11 * FFNP), BF16),
        "vt": k.sbuf("vt", (128, 11 * 1024), BF16),
        "at": k.sbuf("at", (128, NCH * 512), BF16),
        "tabs": k.sbuf("tabs", (128, 4 * 512), BF16),
        "stg0": k.sbuf("stg0", (128, 512), F32),
        "rt1": k.sbuf("rt1", (128, 1024), F32),
        "rt2": k.sbuf("rt2", (128, 512), F32),
        "sq": k.sbuf("sqx", (128, 512), BF16),
        "srow": k.sbuf("srow", (2, 1024 + 2 * FFNP), F32),
        "wt0": k.sbuf("wt0", (128, 1024), BF16),
        "wt1": k.sbuf("wt1", (128, 1024), BF16),
        "wg0": k.sbuf("wg0", (128, 1024), BF16),
        "wg1": k.sbuf("wg1", (128, 1024), BF16),
        "wv": k.sbuf("wv", (128, NCH * 1024), BF16),
        "xnew": k.sbuf("xnew", (128, NCH * 256), F32),
        "p0": k.sbuf("p0", (128, 6 * 512), BF16),
        "rec0": k.sbuf("rec0", (128, 512), F32),
        "xacc": k.sbuf("xacc", (128, NCH * FFNP), F32),
        "gbf": k.sbuf("gbf", (128, 512), BF16),
        "xf": [k.sbuf("xf0", (128, NCH * 257), F32),
               k.sbuf("xf1", (128, NCH * 257), F32)],
        "lnst": None,
    }
    Bf["lnst"] = {
        "sq0": Bf["sq"],
        "sq1": Bf["gbf"],
        "row": RowView(Bf["srow"], 0, 1024),
        "tmp": Bf["rt1"],
    }
    return Bf


def make_consts(k, cbf_hbm, csel_hbm, conesf_hbm, cmisc_hbm, mask_hbm,
                sc_hbm, ln_hbm):
    C = {}
    C["ones2"] = k.sbuf("ones2", (128, 2), BF16)
    C["ones64"] = k.sbuf("ones64", (128, 64), BF16)
    C["onesf"] = k.sbuf("onesf", (1, 128), F32)
    C["sel2"] = k.sbuf("sel2", (2, 128), F32)
    C["m8"] = k.sbuf("m8", (128, 8), F32)
    C["masks"] = k.sbuf("masks", (128, 6 * 512), BF16)
    C["bmask"] = k.sbuf("bmask", (128, 6 * 512), BF16)
    C["scalecols"] = k.sbuf("sccols", (128, 24), F32)
    C["lncols"] = k.sbuf("lncols", (128, 96 + 48), F32)
    C["flag"] = RowColView(C["m8"], 2)
    cbf = k.sbuf("cbf", (128, 66), BF16)
    k.dma("pool", cbf[:, :], cbf_hbm, key="cst", wr=(cbf,))
    k.op("dve", "tensor_copy", C["ones2"][:, :], cbf[:, 0:2], rd=(cbf,),
         wr=(C["ones2"],))
    k.op("dve", "tensor_copy", C["ones64"][:, :], cbf[:, 2:66], rd=(cbf,),
         wr=(C["ones64"],))
    k.dma("pool", C["sel2"][:, :], csel_hbm, key="cst", wr=(C["sel2"],))
    k.dma("pool", C["onesf"][:, :], conesf_hbm, key="cst", wr=(C["onesf"],))
    k.dma("pool", C["m8"][:, :], cmisc_hbm, key="cst", wr=(C["m8"],))
    k.dma("pool", C["masks"][:, :], mask_hbm, key="cst", wr=(C["masks"],))
    k.dma("pool", C["scalecols"][:, :], sc_hbm, key="cst",
          wr=(C["scalecols"],))
    k.dma("pool", C["lncols"][:, :], ln_hbm, key="cst", wr=(C["lncols"],))
    return C


class RowView:
    def __init__(self, buf, row, col0=0):
        self.b = buf
        self.row = row
        self.col0 = col0

    def __getitem__(self, idx):
        p, c = idx
        c = slice((c.start or 0) + self.col0, (c.stop or 0) + self.col0)
        return self.b.ap()[self.row: self.row + 1, c]

    def ap(self):
        return self.b.ap()[self.row: self.row + 1, self.col0:]

    @property
    def w(self):
        return self.b.w

    @w.setter
    def w(self, v):
        self.b.w = v

    @property
    def r(self):
        return self.b.r

    @r.setter
    def r(self, v):
        self.b.r = v


class RowColView:
    def __init__(self, buf, col0):
        self.b = buf
        self.col0 = col0

    def ap(self):
        return self.b.ap()[:, self.col0:]

    @property
    def w(self):
        return self.b.w

    @w.setter
    def w(self, v):
        self.b.w = v

    @property
    def r(self):
        return self.b.r

    @r.setter
    def r(self, v):
        self.b.r = v


def host_const_arrays():
    cbf = np.zeros((128, 66), BF)
    cbf[:, 0] = 1
    cbf[64:, 0] = 0
    cbf[64:, 1] = 1
    cbf[:, 2:66] = 1
    csel = np.zeros((2, 128), np.float32)
    csel[0, 0:64] = 1
    csel[1, 64:128] = 1
    conesf = np.ones((1, 128), np.float32)
    return cbf, csel, conesf


OCOL_BOUT, OCOL_G, OCOL_B, OCOL_B2 = 0, 8, 16, 24


def emit_ln_stream(k, pp, Bf, C, x_src, src_lbuf, t0, tn, out_f32=None,
                   out_hbm=None, gcol=None, bcol=None):
    xs = Bf["xs"]
    for j in range(ceil_div(tn, 256)):
        c0, cn = j * 256, min(256, tn - j * 256)
        xt = Bf["xf"][j % 2]
        k.dma("sp", xt.h.ap()[:, : NCH * cn].rearrange(
                  "p (kc t) -> p kc t", kc=NCH),
              x_src.ap().rearrange("p (kc t) -> p kc t", kc=NCH)[
                  :, :, t0 + c0: t0 + c0 + cn],
              key="xf" + str(j % 2), rd=(x_src,), wr=(xt,))
        k.op("act", "copy",
             xs.ap()[:, 0: NCH * FFNP].rearrange(
                 "p (kc t) -> p kc t", kc=NCH)[:, :, c0: c0 + cn],
             xt.ap()[:, 0: NCH * cn].rearrange(
                 "p (kc t) -> p kc t", kc=NCH),
             rd=(xt,), wr=(xs,))
    emit_ln(k, pp, Bf["lnst"], xs, FFNP, 0, tn, xs, C, gcol=gcol, bcol=bcol)
    xs3 = xs.ap()[:, 0: NCH * FFNP].rearrange(
        "p (kc t) -> p kc t", kc=NCH)[:, :, 0:tn]
    if out_f32 is not None:
        k.op("dve", "tensor_copy",
             out_f32.ap()[:, 0: NCH * FFNP].rearrange(
                 "p (kc t) -> p kc t", kc=NCH)[:, :, 0:tn],
             xs3, rd=(xs,), wr=(out_f32,))
    if out_hbm is not None:
        k.dma("sp", out_hbm.ap().rearrange("p (kc t) -> p kc t", kc=NCH)[
                  :, :, t0: t0 + tn],
              xs3, key="lnw", rd=(xs,), wr=(out_hbm,))


def emit_perf_kv(k, pp, Bf, C, tg_hbm, wq_src, projT, kv_sb, n_tok):
    wt = [Bf["wt0"], Bf["wt1"]]
    kb, vt, kf = Bf["kb"], Bf["vt"], Bf["qb"]
    xsb = Bf["xs"]
    wv = Bf["wv"]
    k.dma("sp", wv[:, 0:NCH * 1024].rearrange("p (kc m) -> p kc m", kc=NCH),
          wq_src.rearrange("(kc p) m -> p kc m", p=128)[:, :, 2 * D: 3 * D],
          key="wv", wr=(wv,))
    for t0 in range(0, n_tok, 512):
        cn = min(512, n_tok - t0)
        first = t0 == 0
        k.dma("sp", xsb.h.ap()[:, : NCH * cn].rearrange(
                  "p (kc t) -> p kc t", kc=NCH),
              tg_hbm.ap().rearrange("p (kc t) -> p kc t", kc=NCH)[
                  :, :, t0: t0 + cn],
              key="xf0", rd=(tg_hbm,), wr=(xsb,))
        load_w_ff(k, wt[0], wq_src, D, NCH, 128, key="wq0")
        for c in range(NCH):
            if c + 1 < NCH:
                load_w_ff(k, wt[(c + 1) % 2], wq_src, D, NCH + c + 1, 128,
                          key="wq" + str((c + 1) % 2))
            wtile = wt[c % 2]
            pq = pp.take()
            for kc in range(NCH):
                mm(k, pq, pq[:, :cn], wtile[:, kc * 128: kc * 128 + 128],
                   xsb[:, kc * cn: kc * cn + cn], rd=(wtile, xsb),
                   start=(kc == 0), stop=(kc == NCH - 1))
            k.op("act", "copy", kb[:, c * 512: c * 512 + cn], pq[:, :cn],
                 rd=(pq,), wr=(kb,))
        for tc in range(ceil_div(cn, 128)):
            tn_ = min(128, cn - tc * 128)
            for half in range(2):
                pv = pp.take()
                for kc in range(NCH):
                    mm(k, pv, pv[:tn_, :512],
                       xsb[:, kc * cn + tc * 128: kc * cn + tc * 128 + tn_],
                       wv[:, kc * 1024 + half * 512:
                          kc * 1024 + half * 512 + 512],
                       rd=(xsb, wv), start=(kc == 0), stop=(kc == NCH - 1))
                k.op("act", "copy",
                     vt.h.ap()[0:tn_, tc * 1040 + half * 8 * 65:
                               tc * 1040 + half * 8 * 65 + 520].rearrange(
                         "p (h e) -> p h e", h=8)[:, :, 0:64],
                     pv.ap()[0:tn_, 0:512].rearrange(
                         "p (h e) -> p h e", h=8),
                     rd=(pv,), wr=(vt,))
            k.op("dve", "memset",
                 vt.h.ap()[0:tn_, tc * 1040: tc * 1040 + 1040].rearrange(
                     "p (h e) -> p h e", h=16)[:, :, 64:65],
                 1.0, wr=(vt,))
        for tc in range(ceil_div(cn, 128)):
            tn_ = min(128, cn - tc * 128)
            for grp in range(2):
                pfs = pp.take_contig(4)
                for cc in range(4):
                    c = grp * 4 + cc
                    pf = pfs[cc]
                    for par in range(2):
                        plo, phi = (0, 64) if par == 0 else (64, 128)
                        mm(k, pf, pf[:tn_, par * 256: par * 256 + 256],
                           kb[plo:phi, c * 512 + tc * 128:
                              c * 512 + tc * 128 + tn_],
                           projT[plo:phi, c * 256: c * 256 + 256],
                           rd=(kb, projT), start=True, stop=True)
                span = pp.big.h.ap()[0:tn_, pfs[0].col0: pfs[0].col0 + 2048]
                eb = Bf["at"]
                k.op("act", "activation", eb[:tn_, :2048], span,
                     AF.Exp, rd=tuple(pfs), wr=(eb,))
                k.op("dve", "tensor_scalar_min", eb[:tn_, :2048],
                     eb[:tn_, :2048], 1.0, rd=(eb,), wr=(eb,))
                k.op("dve", "scalar_tensor_tensor",
                     kf[:tn_, grp * 2048: grp * 2048 + 2048],
                     span, 0.0, eb[:tn_, :2048],
                     ALU.max, ALU.add, rd=tuple(pfs) + (eb,), wr=(kf,))
            for h in range(H):
                pks = pp.take_contig(2)
                for fc in range(2):
                    mm(k, pks[fc], pks[fc][:, :65],
                       kf[0:tn_, h * 256 + fc * 128:
                          h * 256 + fc * 128 + 128],
                       vt[0:tn_, tc * 1040 + h * 65: tc * 1040 + h * 65 + 65],
                       rd=(kf, vt), start=True, stop=True)
                dst3 = kv_sb.ap()[:, h * 130: h * 130 + 130].rearrange(
                    "p (f e) -> p f e", f=2)
                src3 = pp.big.h.ap()[:, pks[0].col0: pks[0].col0
                                     + 1024].rearrange(
                    "p (f e) -> p f e", f=2)[:, :, 0:65]
                if first and tc == 0:
                    k.op("dve", "tensor_copy", dst3, src3,
                         rd=tuple(pks), wr=(kv_sb,))
                else:
                    k.op("dve", "tensor_add", dst3, dst3, src3,
                         rd=tuple(pks) + (kv_sb,), wr=(kv_sb,))


def emit_outer_block(k, pp, Bf, C, x0w, x0g, x_nxt, x1_hbm, x_cur, tg_hbm,
                     wq_src, projT_hbm, wo_src, w1_src, w2_src, b1o, ocols,
                     lbuf=L_TOK, n_tok=N, out_lo=OUT0):
    projT = k.sbuf("projT", (128, 2048), BF16)
    k.dma("pool", projT[:, :], projT_hbm.ap(), key="cst", wr=(projT,))
    k.op("dve", "memset", Bf["xnew"][:, 0: NCH * 244], 0.0,
         wr=(Bf["xnew"],))
    k.dma("sp", x_cur.ap().rearrange("p (kc t) -> p kc t", kc=NCH)[
              :, :, 0: out_lo],
          Bf["xnew"].ap()[:, 0: NCH * out_lo].rearrange(
              "p (kc t) -> p kc t", kc=NCH),
          key="xw", rd=(Bf["xnew"],), wr=(x_cur,))
    kv_sb = Bf["xacc"]
    kvb = Bf["at"]

    for p0 in range(0, n_tok, FFNP):
        pn = min(FFNP, n_tok - p0)
        emit_ln_stream(k, pp, Bf, C, x0g, n_tok, p0, pn, out_hbm=tg_hbm)
    emit_perf_kv(k, pp, Bf, C, tg_hbm, wq_src, projT, kv_sb, n_tok)
    k.op("dve", "tensor_copy",
         kvb.ap()[:, 0:2048].rearrange("p (g e) -> p g e", g=2 * H),
         kv_sb.ap()[:, 0: 65 * 2 * H].rearrange(
             "p (g e) -> p g e", g=2 * H)[:, :, 0:64],
         rd=(kv_sb,), wr=(kvb,))
    for h2 in range(2 * H):
        k.op("dve", "tensor_scalar_mul", kvb[:, 2048 + h2 * 64:
                                             2048 + h2 * 64 + 64],
             C["ones64"][:, 0:64], kv_sb.ap()[:, h2 * 65 + 64: h2 * 65 + 65],
             rd=(C["ones64"], kv_sb), wr=(kvb,))

    wv = Bf["wv"]
    k.dma("sp", wv[:, 0:NCH * 1024].rearrange("p (kc m) -> p kc m", kc=NCH),
          wo_src.rearrange("(kc p) m -> p kc m", p=128), key="wv", wr=(wv,))
    wt = [Bf["wt0"], Bf["wt1"]]
    xs, q_fm, qf, rc = Bf["xs"], Bf["kb"], Bf["p0"], Bf["rec0"]
    attn_o, xnew = Bf["qb"], Bf["xnew"]
    for p0 in range(out_lo, lbuf, FFNP):
        pn = min(FFNP, lbuf - p0)
        emit_ln_stream(k, pp, Bf, C, x0w, lbuf, p0, pn)
        load_w_ff(k, wt[0], wq_src, D, 0, 128, key="wq0")
        for c in range(NCH):
            if c + 1 < NCH:
                load_w_ff(k, wt[(c + 1) % 2], wq_src, D, c + 1, 128,
                          key="wq" + str((c + 1) % 2))
            wtile = wt[c % 2]
            for j in range(ceil_div(pn, 512)):
                c0, cn = j * 512, min(512, pn - j * 512)
                pq = pp.take()
                for kc in range(NCH):
                    mm(k, pq, pq[:, :cn], wtile[:, kc * 128: kc * 128 + 128],
                       xs[:, kc * FFNP + c0: kc * FFNP + c0 + cn],
                       rd=(wtile, xs), start=(kc == 0), stop=(kc == NCH - 1))
                k.op("act", "copy", q_fm[:, c * FFNP + c0: c * FFNP + c0 + cn],
                     pq[:, :cn], rd=(pq,), wr=(q_fm,))
        for j in range(ceil_div(pn, 512)):
            c0, cn = j * 512, min(512, pn - j * 512)
            for h in range(H):
                c, par = h // 2, h % 2
                plo, phi = (0, 64) if par == 0 else (64, 128)
                pfab = pp.take_contig(2)
                for fc in range(2):
                    mm(k, pfab[fc], pfab[fc][:, :cn],
                       projT[plo:phi, c * 256 + fc * 128:
                             c * 256 + fc * 128 + 128],
                       q_fm[plo:phi, c * FFNP + c0: c * FFNP + c0 + cn],
                       rd=(projT, q_fm), start=True, stop=True)
                span = pp.span_ap(pfab[0], 2, (0, 128), cn)
                e1 = Bf["rt1"]
                eview = e1.ap()[:, 0:1024].rearrange(
                    "p (f e) -> p f e", f=2)[:, :, 0:cn]
                k.op("act", "activation", eview, span,
                     AF.Exp, rd=tuple(pfab), wr=(e1,))
                k.op("dve", "tensor_scalar_min", e1[:, :1024],
                     e1[:, :1024], 1.0, rd=(e1,), wr=(e1,))
                k.op("dve", "scalar_tensor_tensor",
                     qf.ap()[:, 0:1024].rearrange(
                         "p (f e) -> p f e", f=2)[:, :, 0:cn],
                     span, 0.0, eview,
                     ALU.max, ALU.add, rd=tuple(pfab) + (e1,), wr=(qf,))
                pa = pp.take()
                pd = pp.take()
                for fc in range(2):
                    h2 = h * 2 + fc
                    mm(k, pa, pa[plo:phi, :cn], kvb[:, h2 * 64: h2 * 64 + 64],
                       qf[:, fc * 512: fc * 512 + cn], rd=(kvb, qf),
                       start=(fc == 0), stop=(fc == 1))
                    mm(k, pd, pd[plo:phi, :cn],
                       kvb[:, 2048 + h2 * 64: 2048 + h2 * 64 + 64],
                       qf[:, fc * 512: fc * 512 + cn], rd=(kvb, qf),
                       start=(fc == 0), stop=(fc == 1))
                m8o = C["m8"]
                k.op("act", "activation", rc[plo:phi, :cn], pd[plo:phi, :cn],
                     AF.Ln, bias=m8o.ap()[plo:phi, 6:7], rd=(pd, m8o),
                     wr=(rc,))
                k.op("act", "activation", rc[plo:phi, :cn], rc[plo:phi, :cn],
                     AF.Exp, scale=m8o.ap()[plo:phi, 3:4], rd=(rc, m8o),
                     wr=(rc,))
                k.op("dve", "tensor_mul",
                     attn_o[plo:phi, c * FFNP + c0: c * FFNP + c0 + cn],
                     pa[plo:phi, :cn], rc[plo:phi, :cn], rd=(pa, rc),
                     wr=(attn_o,))
        ub = Bf["kb"]
        for g0 in range(0, pn, 512):
            gn = min(512, pn - g0)
            for mc in range(NCH):
                pb = pp.take()
                for kc in range(NCH):
                    mm(k, pb, pb[:, :gn],
                       wv[:, kc * 1024 + mc * 128: kc * 1024 + mc * 128 + 128],
                       attn_o[:, kc * FFNP + g0: kc * FFNP + g0 + gn],
                       rd=(wv, attn_o), start=(kc == 0), stop=(kc == NCH - 1))
                k.op("dve", "scalar_tensor_tensor",
                     ub[:, mc * FFNP + g0: mc * FFNP + g0 + gn], pb[:, :gn],
                     ocols.ap()[:, OCOL_BOUT + mc: OCOL_BOUT + mc + 1],
                     xs[:, mc * FFNP + g0: mc * FFNP + g0 + gn],
                     ALU.add, ALU.add, rd=(pb, ocols, xs), wr=(ub,))
        emit_ln(k, pp, Bf["lnst"], ub, FFNP, 0, pn, xs, C,
                gcol=(ocols, OCOL_G), bcol=(ocols, OCOL_B))
        emit_ffn_plain(k, pp, Bf, C, pn, w1_src, w2_src, b1o, ocols)
        for mc in range(NCH):
            k.op("dve", "scalar_tensor_tensor",
                 ub[:, mc * FFNP: mc * FFNP + pn],
                 Bf["xacc"][:, mc * FFNP: mc * FFNP + pn],
                 ocols.ap()[:, OCOL_B2 + mc: OCOL_B2 + mc + 1],
                 xs[:, mc * FFNP: mc * FFNP + pn],
                 ALU.add, ALU.add, rd=(Bf["xacc"], ocols, xs), wr=(ub,))
        emit_ln(k, pp, Bf["lnst"], ub, FFNP, 0, pn, xs, C,
                gcol=(ocols, OCOL_G), bcol=(ocols, OCOL_B))
        emit_ln(k, pp, Bf["lnst"], xs, FFNP, 0, pn, xs, C)
        for g0 in range(0, pn, 256):
            cn = min(256, pn - g0)
            xt = Bf["xf"][(g0 // 256) % 2]
            k.dma("sp", xt.h.ap()[:, : NCH * cn].rearrange(
                      "p (kc t) -> p kc t", kc=NCH),
                  x0w.ap().rearrange("p (kc t) -> p kc t", kc=NCH)[
                      :, :, p0 + g0: p0 + g0 + cn],
                  key="xf" + str((g0 // 256) % 2), rd=(x0w,), wr=(xt,))
            k.op("dve", "tensor_add",
                 xnew.ap()[:, 0: NCH * 256].rearrange(
                     "p (mc t) -> p mc t", mc=NCH)[:, :, 0:cn],
                 xs.ap()[:, 0: NCH * FFNP].rearrange(
                     "p (mc t) -> p mc t", mc=NCH)[:, :, g0: g0 + cn],
                 xt.ap()[:, 0: NCH * cn].rearrange(
                     "p (mc t) -> p mc t", mc=NCH),
                 rd=(xs, xt), wr=(xnew,))
            k.dma("sp", x1_hbm.ap().rearrange("p (kc t) -> p kc t", kc=NCH)[
                      :, :, p0 + g0: p0 + g0 + cn],
                  xnew.ap()[:, 0: NCH * 256].rearrange(
                      "p (mc t) -> p mc t", mc=NCH)[:, :, 0:cn],
                  key="xw", rd=(xnew,), wr=(x1_hbm,))
            k.op("act", "copy",
                 ub.ap()[:, 0: NCH * FFNP].rearrange(
                     "p (mc t) -> p mc t", mc=NCH)[:, :, g0: g0 + cn],
                 xnew.ap()[:, 0: NCH * 256].rearrange(
                     "p (mc t) -> p mc t", mc=NCH)[:, :, 0:cn],
                 rd=(xnew,), wr=(ub,))
        emit_ln(k, pp, Bf["lnst"], ub, FFNP, 0, pn, Bf["xacc"], C)
        k.dma("sp", x_cur.ap().rearrange("p (kc t) -> p kc t", kc=NCH)[
                  :, :, p0: p0 + pn],
              Bf["xacc"].ap()[:, 0: NCH * FFNP].rearrange(
                  "p (kc t) -> p kc t", kc=NCH)[:, :, 0:pn],
              key="xw", rd=(Bf["xacc"],), wr=(x_cur,))


def emit_ffn_plain(k, pp, Bf, C, tn, w1src, w2src, b1o, ocols):
    w1t = [Bf["wt0"], Bf["wt1"]]
    w2t, hg, xacc = Bf["vt"], Bf["kb"], Bf["xacc"]
    groups = [list(range(0, 11)), list(range(11, 22)), list(range(22, 32))]
    xs = Bf["xs"]
    for gi, grp in enumerate(groups):
        for jj, j in enumerate(grp):
            wa = w1t[jj % 2]
            load_w_ff(k, wa, w1src, D, j, 128, key="w1a" + str(jj % 2))
            for j2 in range(ceil_div(tn, 512)):
                c0, cn = j2 * 512, min(512, tn - j2 * 512)
                pa = pp.take()
                for kc in range(NCH):
                    mm(k, pa, pa[:, :cn], wa[:, kc * 128: kc * 128 + 128],
                       xs[:, kc * FFNP + c0: kc * FFNP + c0 + cn],
                       rd=(wa, xs), start=(kc == 0), stop=(kc == NCH - 1))
                k.op("act", "activation",
                     hg[:, jj * tn + c0: jj * tn + c0 + cn], pa[:, :cn],
                     AF.Gelu, bias=b1o[:, j: j + 1], rd=(pa, b1o), wr=(hg,))
        for jj, j in enumerate(grp):
            k.dma("pool", w2t[:, jj * 1024: (jj + 1) * 1024],
                  w2src[128 * j: 128 * j + 128, :], key="w2", wr=(w2t,))
        for tt in range(ceil_div(tn, 512)):
            c0, cn = tt * 512, min(512, tn - tt * 512)
            for mh in range(2):
                pxs = pp.take_contig(4)
                for mi in range(4):
                    mc = mh * 4 + mi
                    px = pxs[mi]
                    for jj, j in enumerate(grp):
                        mm(k, px, px[:, :cn],
                           w2t[:, jj * 1024 + mc * 128:
                               jj * 1024 + mc * 128 + 128],
                           hg[:, jj * tn + c0: jj * tn + c0 + cn],
                           rd=(w2t, hg), start=(jj == 0),
                           stop=(jj == len(grp) - 1))
                dst3 = xacc.ap()[:, mh * 4 * FFNP: mh * 4 * FFNP
                                 + 4 * FFNP].rearrange(
                    "p (m t) -> p m t", m=4)[:, :, c0: c0 + cn]
                src3 = pp.span_ap(pxs[0], 4, (0, 128), cn)
                if gi == 0:
                    k.op("dve", "tensor_copy", dst3, src3,
                         rd=tuple(pxs), wr=(xacc,))
                else:
                    k.op("dve", "tensor_add", dst3, dst3, src3,
                         rd=tuple(pxs) + (xacc,), wr=(xacc,))


def emit_final(k, pp, Bf, C, x_cur, x1_hbm, out_hbm, lo=D_L, lbuf=L_TOK):
    xs, xnew = Bf["xs"], Bf["xnew"]
    span = lbuf - lo
    for p0 in range(lo, lbuf, FFNP):
        pn = min(FFNP, lbuf - p0)
        emit_ln_stream(k, pp, Bf, C, x_cur, lbuf, p0, pn)
        for g0 in range(0, pn, 256):
            cn = min(256, pn - g0)
            xt = Bf["xf"][(g0 // 256) % 2]
            k.dma("sp", xt.h.ap()[:, : NCH * cn].rearrange(
                      "p (kc t) -> p kc t", kc=NCH),
                  x1_hbm.ap().rearrange("p (kc t) -> p kc t", kc=NCH)[
                      :, :, p0 + g0: p0 + g0 + cn],
                  key="xf" + str((g0 // 256) % 2), rd=(x1_hbm,), wr=(xt,))
            k.op("dve", "tensor_add",
                 xnew.ap()[:, 0: NCH * 256].rearrange(
                     "p (mc t) -> p mc t", mc=NCH)[:, :, 0:cn],
                 xs.ap()[:, 0: NCH * FFNP].rearrange(
                     "p (mc t) -> p mc t", mc=NCH)[:, :, g0: g0 + cn],
                 xt.ap()[:, 0: NCH * cn].rearrange(
                     "p (mc t) -> p mc t", mc=NCH),
                 rd=(xs, xt), wr=(xnew,))
            k.dma("sp", out_hbm.ap().rearrange("p (mc t) -> p mc t", mc=NCH)[
                      :, :, p0 - lo + g0: p0 - lo + g0 + cn],
                  xnew.ap()[:, 0: NCH * 256].rearrange(
                      "p (mc t) -> p mc t", mc=NCH)[:, :, 0:cn],
                  key="ow", rd=(xnew,), wr=(out_hbm,))


def _fm(x):
    T = x.shape[0]
    return np.ascontiguousarray(
        x.reshape(T, NCH, 128).transpose(2, 1, 0).reshape(128, NCH * T))


def _defm(a, T):
    return a.reshape(128, NCH, T).transpose(2, 1, 0).reshape(T, 1024)


def _const_drams(nc):
    dt = nc.dram_tensor
    return {
        "hcbf": dt("hcbf", [128, 66], BF16, kind="ExternalInput").ap(),
        "hcsel": dt("hcsel", [2, 128], F32, kind="ExternalInput").ap(),
        "hconesf": dt("hconesf", [1, 128], F32, kind="ExternalInput").ap(),
        "hcmisc": dt("hcmisc", [128, 8], F32, kind="ExternalInput").ap(),
        "hcmask": dt("hcmask", [128, 3072], BF16, kind="ExternalInput").ap(),
        "hcsc": dt("hcsc", [128, 24], F32, kind="ExternalInput").ap(),
        "hcln": dt("hcln", [128, 144], F32, kind="ExternalInput").ap(),
    }


def build_outer():
    nc = bass.Bass("TRN2", target_bir_lowering=False)
    dt = nc.dram_tensor
    x0w = DBuf(dt("x0w", [128, NCH * L_TOK], F32, kind="ExternalInput").ap())
    x0g = DBuf(dt("x0g", [128, NCH * N], F32, kind="ExternalInput").ap())
    wq = dt("wq", [1024, 3072], F32, kind="ExternalInput").ap()
    pjh = DBuf(dt("hprojT", [128, 2048], F32, kind="ExternalInput").ap())
    wo = dt("wo", [1024, 1024], F32, kind="ExternalInput").ap()
    w1 = dt("w1", [1024, 4096], F32, kind="ExternalInput").ap()
    w2 = dt("w2", [4096, 1024], F32, kind="ExternalInput").ap()
    b1h = dt("b1h", [128, 32], F32, kind="ExternalInput").ap()
    och = dt("och", [128, 32], F32, kind="ExternalInput").ap()
    cd = _const_drams(nc)
    tg = DBuf(dt("tg", [128, NCH * N], BF16, kind="Internal").ap())
    xnx = DBuf(dt("xnx", [128, NCH * L_TOK], F32, kind="Internal").ap())
    x1 = DBuf(dt("x1o", [128, NCH * L_TOK], F32, kind="ExternalOutput").ap())
    xco = DBuf(dt("xco", [128, NCH * L_TOK], F32, kind="ExternalOutput").ap())
    k = Ker(nc)
    with nc.Block() as block:
        pp = PsumPool(k)
        Bf = make_bufs(k)
        b1o = k.sbuf("b1o", (128, 32), F32)
        oco = k.sbuf("oco", (128, 32), F32)

        @block.tensor
        def _(tensor):
            C = make_consts(k, cd["hcbf"], cd["hcsel"], cd["hconesf"],
                            cd["hcmisc"], cd["hcmask"], cd["hcsc"],
                            cd["hcln"])
            k.dma("pool", b1o[:, :], b1h, key="cst", wr=(b1o,))
            k.dma("pool", oco[:, :], och, key="cst", wr=(oco,))
            emit_outer_block(k, pp, Bf, C, x0w, x0g, xnx, x1, xco, tg,
                             wq, pjh, wo, w1, w2, b1o, oco)
            for t in k.barrier_tokens():
                k.wait("pool", t)
    return nc


def build_layer(attn_lo, ffn_lo):
    nc = bass.Bass("TRN2", target_bir_lowering=False)
    dt = nc.dram_tensor
    xin = DBuf(dt("xin", [128, NCH * L_TOK], F32, kind="ExternalInput").ap())
    xout = DBuf(dt("xout", [128, NCH * L_TOK], F32, kind="ExternalOutput").ap())
    xnx = DBuf(dt("xnx", [128, NCH * L_TOK], F32, kind="Internal").ap())
    wq = dt("wq", [1024, 3072], F32, kind="ExternalInput").ap()
    wo = dt("wo", [1024, 1024], F32, kind="ExternalInput").ap()
    w1 = dt("w1", [1024, 5460], F32, kind="ExternalInput").ap()
    w2 = dt("w2", [DI, 1024], F32, kind="ExternalInput").ap()
    tabh = DBuf(dt("tabh", [128, 4 * L_TOK], F32, kind="ExternalInput").ap())
    bmh = DBuf(dt("bmh", [128, 3072], BF16, kind="ExternalInput").ap())
    b1h = dt("b1h", [128, 44], F32, kind="ExternalInput").ap()
    cd = _const_drams(nc)
    k = Ker(nc)
    with nc.Block() as block:
        pp = PsumPool(k)
        Bf = make_bufs(k)
        b1t = k.sbuf("b1t", (128, 44), F32)

        @block.tensor
        def _(tensor):
            C = make_consts(k, cd["hcbf"], cd["hcsel"], cd["hconesf"],
                            cd["hcmisc"], cd["hcmask"], cd["hcsc"],
                            cd["hcln"])
            k.dma("pool", b1t[:, :], b1h, key="cst", wr=(b1t,))
            for kc in range(NCH):
                k.dma("sp", xout.ap()[:, kc * L_TOK: (kc + 1) * L_TOK],
                      xin.ap()[:, kc * L_TOK: (kc + 1) * L_TOK], key="xw",
                      rd=(xin,), wr=(xout,))
            emit_inner_layer(k, pp, Bf, C, 0, xin, xnx, wq, wo, w1, w2,
                             tabh, bmh, b1t, attn_lo, ffn_lo, D_L,
                             x_wr=xout)
            for t in k.barrier_tokens():
                k.wait("pool", t)
    return nc


def build_final():
    nc = bass.Bass("TRN2", target_bir_lowering=False)
    dt = nc.dram_tensor
    xin = DBuf(dt("xin", [128, NCH * L_TOK], F32, kind="ExternalInput").ap())
    x1h = DBuf(dt("x1h", [128, NCH * L_TOK], F32, kind="ExternalInput").ap())
    out = DBuf(dt("out", [128, NCH * (L_TOK - D_L)], F32,
                  kind="ExternalOutput").ap())
    cd = _const_drams(nc)
    k = Ker(nc)
    with nc.Block() as block:
        pp = PsumPool(k)
        Bf = make_bufs(k)

        @block.tensor
        def _(tensor):
            C = make_consts(k, cd["hcbf"], cd["hcsel"], cd["hconesf"],
                            cd["hcmisc"], cd["hcmask"], cd["hcsc"],
                            cd["hcln"])
            emit_final(k, pp, Bf, C, xin, x1h, out)
            for t in k.barrier_tokens():
                k.wait("pool", t)
    return nc


def _b1t_inner(b1):
    a = np.pad(b1[:DI], (0, 22 * 128 - DI)).reshape(22, 128).T
    g = np.pad(b1[DI:], (0, 22 * 128 - DI)).reshape(22, 128).T
    return np.ascontiguousarray(np.concatenate([a, g], 1)).astype(np.float32)


def _lncols_layer(g, b, b2):
    out = np.zeros((128, 144), np.float32)
    out[:, 0:8] = chunk_cols_np(g)
    out[:, 8:16] = chunk_cols_np(b)
    out[:, 96:104] = chunk_cols_np(b2)
    return out


def kernel_multi(**inputs):
    f32 = lambda a: np.asarray(a, dtype=np.float32)
    x0 = f32(inputs["L0_attn_x"])
    proj = f32(inputs["L0_attn_proj"])
    wqkv_o = f32(inputs["L0_attn_w_qkv"])
    wout_o = f32(inputs["L0_attn_w_out"])
    bout_o = f32(inputs["L0_attn_b_out"])
    gam_o = f32(inputs["L0_attn_gamma"])
    bet_o = f32(inputs["L0_attn_beta"])
    w1_o = f32(inputs["L0_attn_w1"])
    b1_o = f32(inputs["L0_attn_b1"])
    w2_o = f32(inputs["L0_attn_w2"])
    b2_o = f32(inputs["L0_attn_b2"])
    wqkv_i = f32(inputs["L0_ffn_Wqkv"])
    qs_i = f32(inputs["L0_ffn_q_scale"])
    ks_i = f32(inputs["L0_ffn_k_scale"])
    wo_i = f32(inputs["L0_ffn_Wo"])
    lng_i = f32(inputs["L0_ffn_ln_g"])
    lnb_i = f32(inputs["L0_ffn_ln_b"])
    w1_i = f32(inputs["L0_ffn_W1"])
    b1_i = f32(inputs["L0_ffn_b1"])
    w2_i = f32(inputs["L0_ffn_W2"])
    b2_i = f32(inputs["L0_ffn_b2"])

    cbf, csel, conesf = host_const_arrays()
    band = band_masks_np().astype(BF)
    cmisc = []
    for r in range(2):
        cm = np.zeros((128, 8), np.float32)
        cm[:, 0], cm[:, 1], cm[:, 2] = -8.0, 8.0, float(r)
        cm[:, 3], cm[:, 4], cm[:, 5], cm[:, 6] = -1.0, -0.5, EPS_LN, EPS_ATTN
        cmisc.append(cm)
    base_c = {"hcbf": cbf, "hcsel": csel, "hconesf": conesf, "hcmask": band}

    pj = np.zeros((128, 2048), np.float32)
    for c in range(8):
        pj[0:64, 256 * c: 256 * c + 256] = proj[2 * c]
        pj[64:128, 256 * c: 256 * c + 256] = proj[2 * c + 1]
    wqT_o = np.ascontiguousarray(wqkv_o.T)
    woT_o = np.ascontiguousarray(wout_o.T)
    w1T_o = np.ascontiguousarray(w1_o.T)
    w2T_o = np.ascontiguousarray(w2_o.T)
    b1h_o = np.ascontiguousarray(b1_o.reshape(32, 128).T).astype(np.float32)
    och = np.ascontiguousarray(np.concatenate(
        [chunk_cols_np(bout_o), chunk_cols_np(gam_o), chunk_cols_np(bet_o),
         chunk_cols_np(b2_o)], 1)).astype(np.float32)
    zsc = np.zeros((128, 24), np.float32)
    zln = np.zeros((128, 144), np.float32)
    x0w_c, x0g_c = [], []
    for core in range(8):
        b, r = core // 2, core % 2
        if r == 0:
            xw = np.concatenate(
                [np.zeros((D_L, D), np.float32), x0[b, :S_L]], 0)
        else:
            xw = x0[b, ORG_H:]
        x0w_c.append(_fm(xw))
        x0g_c.append(_fm(x0[b]))
    nc_outer = build_outer()
    maps = []
    for core in range(8):
        m = dict(base_c)
        m.update({"x0w": x0w_c[core], "x0g": x0g_c[core], "wq": wqT_o,
                  "hprojT": pj, "wo": woT_o, "w1": w1T_o, "w2": w2T_o,
                  "b1h": b1h_o, "och": och, "hcmisc": cmisc[core % 2],
                  "hcsc": zsc, "hcln": zln})
        maps.append(m)
    res = run_bass_kernel_spmd(nc_outer, maps,
                               core_ids=list(range(8))).results
    x1_c = [res[c]["x1o"] for c in range(8)]
    xc_c = [res[c]["xco"] for c in range(8)]

    tabs_c = []
    for core in range(8):
        org = -D_L if core % 2 == 0 else ORG_H
        tabs_c.append(np.ascontiguousarray(
            np.concatenate(rotary_tables_np(org, L_TOK), 1)))
    ranges = layer_ranges()
    progs = {}
    for li in range(6):
        al, fl = ranges[li]
        if (al, fl) not in progs:
            progs[(al, fl)] = build_layer(al, fl)
        ncl = progs[(al, fl)]
        wq_l = perm_wqkv_np(wqkv_i[li])
        sc_l = np.zeros((128, 24), np.float32)
        sc_l[:, 0:4] = scale_cols_np(qs_i[li], ks_i[li])
        ln_l = _lncols_layer(lng_i[li], lnb_i[li], b2_i[li])
        b1t_l = _b1t_inner(b1_i[li])
        bm_l = [boundary_masks_np(al, light=(core % 2 == 0)).astype(BF)
                for core in range(2)]
        maps = []
        for core in range(8):
            m = dict(base_c)
            m.update({"xin": xc_c[core], "wq": wq_l, "wo": wo_i[li],
                      "w1": w1_i[li], "w2": w2_i[li], "tabh": tabs_c[core],
                      "bmh": bm_l[core % 2], "b1h": b1t_l,
                      "hcmisc": cmisc[core % 2], "hcsc": sc_l,
                      "hcln": ln_l})
            maps.append(m)
        res = run_bass_kernel_spmd(ncl, maps,
                                   core_ids=list(range(8))).results
        xc_c = [res[c]["xout"] for c in range(8)]

    nc_fin = build_final()
    maps = []
    for core in range(8):
        m = dict(base_c)
        m.update({"xin": xc_c[core], "x1h": x1_c[core],
                  "hcmisc": cmisc[core % 2], "hcsc": zsc, "hcln": zln})
        maps.append(m)
    res = run_bass_kernel_spmd(nc_fin, maps,
                               core_ids=list(range(8))).results
    span = L_TOK - D_L
    y = np.empty((B, N, D), np.float32)
    for b in range(B):
        y[b, :S_L] = _defm(np.asarray(res[2 * b]["out"]), span)
        y[b, S_L:] = _defm(np.asarray(res[2 * b + 1]["out"]), span)[512:]
    return y


def emit_w_prep(k, Bf, pairs):
    bufs = [Bf["wv"], Bf["kb"]]
    slot = 0
    for src, dst, R, C in pairs:
        for r0 in range(0, R, 128):
            rn = min(128, R - r0)
            for c0 in range(0, C, 4096):
                cn = min(4096, C - c0)
                t = bufs[slot % 2]
                slot += 1
                k.dma("pool", t[0:rn, 0:cn], src[r0: r0 + rn, c0: c0 + cn],
                      key="wpc" + str(slot % 2), wr=(t,))
                k.dma("sp", dst[r0: r0 + rn, c0: c0 + cn], t[0:rn, 0:cn],
                      key="wps" + str(slot % 2), rd=(t,))


def build_fused():
    nc = bass.Bass("TRN2", target_bir_lowering=False)
    dt = nc.dram_tensor
    x0w = DBuf(dt("x0w", [128, NCH * L_TOK], F32, kind="ExternalInput").ap())
    x0g = DBuf(dt("x0g", [128, NCH * N], F32, kind="ExternalInput").ap())
    wqo = dt("wqo", [1024, 3072], F32, kind="ExternalInput").ap()
    pjh = DBuf(dt("hprojT", [128, 2048], F32, kind="ExternalInput").ap())
    woo = dt("woo", [1024, 1024], F32, kind="ExternalInput").ap()
    w1o = dt("w1o", [1024, 4096], F32, kind="ExternalInput").ap()
    w2o = dt("w2o", [4096, 1024], F32, kind="ExternalInput").ap()
    b1ho = dt("b1ho", [128, 32], F32, kind="ExternalInput").ap()
    och = dt("och", [128, 32], F32, kind="ExternalInput").ap()
    wqi = dt("wqi", [6 * 1024, 3072], F32, kind="ExternalInput").ap()
    woi = dt("woi", [6 * 1024, 1024], F32, kind="ExternalInput").ap()
    w1i = dt("w1i", [6 * 1024, 5460], F32, kind="ExternalInput").ap()
    w2i = dt("w2i", [6 * DI, 1024], F32, kind="ExternalInput").ap()
    b1i = dt("b1i", [128, 6 * 44], F32, kind="ExternalInput").ap()
    tabh = DBuf(dt("tabh", [128, 4 * L_TOK], F32, kind="ExternalInput").ap())
    bmi = dt("bmi", [128, 6 * 3072], BF16, kind="ExternalInput").ap()
    cd = _const_drams(nc)
    tg = DBuf(dt("tg", [128, NCH * N], BF16, kind="Internal").ap())
    x1h = DBuf(dt("x1h", [128, NCH * L_TOK], F32, kind="Internal").ap())
    xc = DBuf(dt("xc", [128, NCH * L_TOK], F32, kind="Internal").ap())
    xn = DBuf(dt("xn_s", [128, NCH * L_TOK], F32, kind="Internal").ap())
    out = DBuf(dt("out", [128, NCH * (L_TOK - D_L)], F32,
                  kind="ExternalOutput").ap())
    wqo_b = dt("wqo_b", [1024, 3072], BF16, kind="Internal").ap()
    woo_b = dt("woo_b", [1024, 1024], BF16, kind="Internal").ap()
    w1o_b = dt("w1o_b", [1024, 4096], BF16, kind="Internal").ap()
    w2o_b = dt("w2o_b", [4096, 1024], BF16, kind="Internal").ap()
    wqi_b = dt("wqi_b", [6 * 1024, 3072], BF16, kind="Internal").ap()
    woi_b = dt("woi_b", [6 * 1024, 1024], BF16, kind="Internal").ap()
    w1i_b = dt("w1i_b", [6 * 1024, 5460], BF16, kind="Internal").ap()
    w2i_b = dt("w2i_b", [6 * DI, 1024], BF16, kind="Internal").ap()
    tab_b = dt("tab_b", [128, 4 * L_TOK], BF16, kind="Internal").ap()
    k = Ker(nc)
    with nc.Block() as block:
        pp = PsumPool(k)
        Bf = make_bufs(k)
        b1o = k.sbuf("b1o", (128, 32), F32)
        oco = k.sbuf("oco", (128, 32), F32)
        b1t = k.sbuf("b1t", (128, 44), F32)

        @block.tensor
        def _(tensor):
            C = make_consts(k, cd["hcbf"], cd["hcsel"], cd["hconesf"],
                            cd["hcmisc"], cd["hcmask"], cd["hcsc"],
                            cd["hcln"])
            k.dma("pool", b1o[:, :], b1ho, key="cst", wr=(b1o,))
            k.dma("pool", oco[:, :], och, key="cst", wr=(oco,))
            emit_w_prep(k, Bf, [
                (wqo, wqo_b, 1024, 3072), (woo, woo_b, 1024, 1024),
                (w1o, w1o_b, 1024, 4096), (w2o, w2o_b, 4096, 1024),
                (tabh.ap(), tab_b, 128, 4 * L_TOK),
                (wqi, wqi_b, 6 * 1024, 3072), (woi, woi_b, 6 * 1024, 1024),
                (w1i, w1i_b, 6 * 1024, 5460), (w2i, w2i_b, 6 * DI, 1024),
            ])
            tab_bd = DBuf(tab_b)
            emit_outer_block(k, pp, Bf, C, x0w, x0g, xn, x1h, xc, tg,
                             wqo_b, pjh, woo_b, w1o_b, w2o_b, b1o, oco)
            ranges = layer_ranges()
            for li in range(6):
                al, fl = ranges[li]
                k.dma("pool", b1t[:, :], b1i[:, 44 * li: 44 * li + 44],
                      key="cst", wr=(b1t,))
                bm_l = DBuf(bmi[:, 3072 * li: 3072 * li + 3072])
                emit_inner_layer(
                    k, pp, Bf, C, li, xc, xn,
                    wqi_b[1024 * li: 1024 * li + 1024, :],
                    woi_b[1024 * li: 1024 * li + 1024, :],
                    w1i_b[1024 * li: 1024 * li + 1024, :],
                    w2i_b[DI * li: DI * li + DI, :],
                    tab_bd, bm_l, b1t, al, fl, D_L)
            emit_final(k, pp, Bf, C, xc, x1h, out)
            for t in k.barrier_tokens():
                k.wait("pool", t)
    return nc


def _lncols_all(lng, lnb, b2):
    out = np.zeros((128, 144), np.float32)
    for li in range(6):
        out[:, 16 * li: 16 * li + 8] = chunk_cols_np(lng[li])
        out[:, 16 * li + 8: 16 * li + 16] = chunk_cols_np(lnb[li])
        out[:, 96 + 8 * li: 104 + 8 * li] = chunk_cols_np(b2[li])
    return out


def kernel_fused(inputs):
    f32 = lambda a: np.asarray(a, dtype=np.float32)
    x0 = f32(inputs["L0_attn_x"])
    proj = f32(inputs["L0_attn_proj"])
    wqkv_o = f32(inputs["L0_attn_w_qkv"])
    wqkv_i = f32(inputs["L0_ffn_Wqkv"])
    qs_i = f32(inputs["L0_ffn_q_scale"])
    ks_i = f32(inputs["L0_ffn_k_scale"])

    cbf, csel, conesf = host_const_arrays()
    band = band_masks_np().astype(BF)
    cmisc = []
    for r in range(2):
        cm = np.zeros((128, 8), np.float32)
        cm[:, 0], cm[:, 1], cm[:, 2] = -8.0, 8.0, float(r)
        cm[:, 3], cm[:, 4], cm[:, 5], cm[:, 6] = -1.0, -0.5, EPS_LN, EPS_ATTN
        cmisc.append(cm)

    pj = np.zeros((128, 2048), np.float32)
    for c in range(8):
        pj[0:64, 256 * c: 256 * c + 256] = proj[2 * c]
        pj[64:128, 256 * c: 256 * c + 256] = proj[2 * c + 1]
    sc_all = np.zeros((128, 24), np.float32)
    for li in range(6):
        sc_all[:, 4 * li: 4 * li + 4] = scale_cols_np(qs_i[li], ks_i[li])
    ln_all = _lncols_all(f32(inputs["L0_ffn_ln_g"]),
                         f32(inputs["L0_ffn_ln_b"]),
                         f32(inputs["L0_ffn_b2"]))
    b1all = np.concatenate(
        [_b1t_inner(f32(inputs["L0_ffn_b1"])[li]) for li in range(6)], 1)
    b1all = np.ascontiguousarray(b1all)
    ranges = layer_ranges()
    bmi_r = []
    for r in range(2):
        bmi_r.append(np.ascontiguousarray(np.concatenate(
            [boundary_masks_np(ranges[li][0], light=(r == 0))
             for li in range(6)], 1)).astype(BF))
    wqi = np.concatenate([perm_wqkv_np(wqkv_i[li]) for li in range(6)], 0)
    woi = f32(inputs["L0_ffn_Wo"]).reshape(6 * 1024, 1024)
    w1i = f32(inputs["L0_ffn_W1"]).reshape(6 * 1024, 5460)
    w2i = f32(inputs["L0_ffn_W2"]).reshape(6 * DI, 1024)

    base = {
        "hcbf": cbf, "hcsel": csel, "hconesf": conesf, "hcmask": band,
        "hcsc": sc_all, "hcln": ln_all,
        "wqo": np.ascontiguousarray(wqkv_o.T), "hprojT": pj,
        "woo": np.ascontiguousarray(f32(inputs["L0_attn_w_out"]).T),
        "w1o": np.ascontiguousarray(f32(inputs["L0_attn_w1"]).T),
        "w2o": np.ascontiguousarray(f32(inputs["L0_attn_w2"]).T),
        "b1ho": np.ascontiguousarray(
            f32(inputs["L0_attn_b1"]).reshape(32, 128).T),
        "och": np.ascontiguousarray(np.concatenate(
            [chunk_cols_np(f32(inputs["L0_attn_b_out"])),
             chunk_cols_np(f32(inputs["L0_attn_gamma"])),
             chunk_cols_np(f32(inputs["L0_attn_beta"])),
             chunk_cols_np(f32(inputs["L0_attn_b2"]))], 1)),
        "wqi": wqi, "woi": woi, "w1i": w1i, "w2i": w2i, "b1i": b1all,
    }
    maps = []
    for core in range(8):
        b, r = core // 2, core % 2
        xw = (np.concatenate([np.zeros((D_L, D), np.float32),
                              x0[b, :S_L]], 0)
              if r == 0 else x0[b, ORG_H:])
        org = -D_L if r == 0 else ORG_H
        m = dict(base)
        m.update({
            "x0w": _fm(xw), "x0g": _fm(x0[b]),
            "tabh": np.ascontiguousarray(
                np.concatenate(rotary_tables_np(org, L_TOK), 1)),
            "bmi": bmi_r[r], "hcmisc": cmisc[r],
        })
        maps.append(m)
    nc = build_fused()
    res = run_bass_kernel_spmd(nc, maps, core_ids=list(range(8))).results
    span = L_TOK - D_L
    y = np.empty((B, N, D), np.float32)
    for b in range(B):
        y[b, :S_L] = _defm(np.asarray(res[2 * b]["out"]), span)
        y[b, S_L:] = _defm(np.asarray(res[2 * b + 1]["out"]), span)[512:]
    return y


def kernel(**inputs):
    return kernel_fused(inputs)


# version 09 (final)
# speedup vs baseline: 1.1657x; 1.0120x over previous
import numpy as np
import ml_dtypes
import concourse.bass as bass
import concourse.mybir as mybir
from concourse.bass_utils import run_bass_kernel_spmd

F32 = mybir.dt.float32
BF16 = mybir.dt.bfloat16
AF = mybir.ActivationFunctionType
ALU = mybir.AluOpType
BF = ml_dtypes.bfloat16

B, N, D = 4, 4096, 1024
H, HD = 16, 64
PF = 256
WIN = 256
DI = 2730
EPS_LN = 1e-5
EPS_ATTN = 1e-8
NCH = D // 128

L_TOK = 3584
S_L = 2304
ORG_H = N - L_TOK
D_L = L_TOK - S_L
W_BOUND = D_L // WIN
OUT0 = 244

ROT_PERM = np.arange(64).reshape(2, 32).T.reshape(-1)
SHUF_MASK = [i for p in range(16) for i in (2 * p + 1, 2 * p)]

TBUF = 1024
FFNP = 896


def ceil_div(a, b):
    return -(-a // b)


def even(x):
    return x - (x % 2)


def layer_ranges():
    out = []
    V = OUT0
    for i in range(6):
        A = V + 257
        wf = A // WIN
        attn_lo = min(D_L, wf * WIN)
        F = min(D_L, even(A + 1))
        out.append((attn_lo, F))
        V = A + 1
    assert V <= D_L + (N - ORG_H - S_L), ("pyramid chain too deep", V)
    return out


class Ker:

    ENGS = ("pe", "act", "dve", "pool", "sp")

    def __init__(self, nc):
        self.nc = nc
        self.eng = {
            "pe": nc.tensor, "act": nc.scalar, "dve": nc.vector,
            "pool": nc.gpsimd, "sp": nc.sync,
        }
        self._stack = []
        self.prog = {}
        self.count = {}
        for e in self.ENGS:
            self.prog[e] = self._enter(nc.semaphore(f"prog_{e}"))
            self.count[e] = 0
        self.dmasem = {}
        self.dmacnt = {}
        self.dmaout = {}
        self.waited = {e: {} for e in self.ENGS}

    def _enter(self, cm):
        v = cm.__enter__()
        self._stack.append(cm)
        return v

    def sbuf(self, name, shape, dtype):
        return Buf(self._enter(self.nc.sbuf_tensor(name, list(shape), dtype)))

    def psum(self, name, shape, dtype=F32):
        return Buf(self._enter(self.nc.psum_tensor(name, list(shape), dtype)))

    def wait(self, e, tok):
        if tok is None:
            return
        sem, val = tok
        if self.waited[e].get(sem.name, 0) >= val:
            return
        self.eng[e].wait_ge(sem, val)
        self.waited[e][sem.name] = val

    def _collect(self, e, deps, rd, wr):
        need = {}

        def add(tok):
            if tok is None:
                return
            sem, val = tok
            if self.waited[e].get(sem.name, 0) >= val:
                return
            if need.get(sem.name, (None, 0))[1] < val:
                need[sem.name] = (sem, val)
        for b in rd:
            add(b.w)
        for b in wr:
            add(b.w)
            for t in b.r:
                add(t)
        for t in deps:
            add(t)
        return list(need.values())

    def _pre(self, e, pend):
        for sem, val in pend[:-1]:
            self.eng[e].wait_ge(sem, val)
            self.waited[e][sem.name] = val
        if pend:
            sem, val = pend[-1]
            self.waited[e][sem.name] = val
            return (sem, val)
        return None

    def _post(self, tok, rd, wr):
        for b in rd:
            b.r.append(tok)
        for b in wr:
            b.w = tok
            b.r = []
        return tok

    def op(self, e, name, *args, deps=(), rd=(), wr=(), **kw):
        att = self._pre(e, self._collect(e, deps, rd, wr))
        inst = getattr(self.eng[e], name)(*args, **kw)
        if att is not None:
            inst._wait_ge(att[0], att[1])
        self.count[e] += 1
        tok = (self.prog[e], self.count[e])
        inst.then_inc(self.prog[e], 1)
        return self._post(tok, rd, wr)

    def dma(self, q, out, in_, key=None, deps=(), rd=(), wr=(), slow=False):
        e = q
        key = key if key is not None else q
        if key not in self.dmasem:
            self.dmasem[key] = self._enter(self.nc.semaphore(f"dma_{key}"))
            self.dmacnt[key] = 0
            self.dmaout[key] = None
        pend = self._collect(e, (self.dmaout[key],) + tuple(deps), rd, wr)
        att = self._pre(e, pend)
        inst = self.eng[e].dma_start(out=out, in_=in_,
                                     allow_slow_non_contiguous=slow)
        if att is not None:
            inst._wait_ge(att[0], att[1])
        self.dmacnt[key] += 16
        tok = (self.dmasem[key], self.dmacnt[key])
        inst.then_inc(self.dmasem[key], 16)
        self.dmaout[key] = tok
        return self._post(tok, rd, wr)

    def barrier_tokens(self):
        toks = [(self.prog[e], self.count[e]) for e in self.ENGS]
        toks += [(self.dmasem[kq], self.dmacnt[kq]) for kq in self.dmasem]
        return [t for t in toks if t[1] > 0]


class Buf:
    def __init__(self, h):
        self.h = h
        self.w = None
        self.r = []

    def __getitem__(self, idx):
        return self.h[idx]

    def ap(self):
        return self.h.ap()


class DBuf(Buf):
    def __init__(self, ap):
        self.a = ap
        self.w = None
        self.r = []

    def __getitem__(self, idx):
        return self.a[idx]

    def ap(self):
        return self.a


class PsumSlice:
    def __init__(self, t, col0):
        self.t = t
        self.col0 = col0
        self.w = None
        self.r = []

    def __getitem__(self, idx):
        p, c = idx
        c0 = self.col0 + (c.start or 0)
        c1 = self.col0 + (512 if c.stop is None else c.stop)
        return self.t[p, c0:c1]

    def ap(self):
        return self.t.ap()[:, self.col0: self.col0 + 512]


class PsumPool:
    def __init__(self, k):
        self.big = k.psum("psA", (128, 4096), F32)
        self.banks = [PsumSlice(self.big.h, i * 512) for i in range(8)]
        self.i = 0

    def take(self):
        b = self.banks[self.i % 8]
        self.i += 1
        return b

    def take_contig(self, n):
        s = self.i % 8
        if s + n > 8:
            self.i += 8 - s
            s = 0
        self.i += n
        return [self.banks[s + j] for j in range(n)]

    def span_ap(self, b0, nbanks, prow, ncols_per, stride=512):
        return self.big.h.ap()[prow[0]:prow[1],
                               b0.col0: b0.col0 + nbanks * 512].rearrange(
            "p (b c) -> p b c", b=nbanks)[:, :, 0:ncols_per]


def mm(k, pb, pslice, lh, rh, rd=(), start=False, stop=False):
    if start:
        att = k._pre("pe", k._collect("pe", (pb.w,) + tuple(pb.r), rd, ()))
    else:
        att = k._pre("pe", k._collect("pe", (), rd, ()))
    inst = k.eng["pe"].matmul(pslice, lh, rh, start=start, stop=stop)
    if att is not None:
        inst._wait_ge(att[0], att[1])
    k.count["pe"] += 1
    tok = (k.prog["pe"], k.count["pe"])
    inst.then_inc(k.prog["pe"], 1)
    for b in rd:
        b.r.append(tok)
    if start:
        pb.r = []
    pb.w = tok
    return tok


def rotary_tables_np(origin, L):
    half = np.arange(0, HD, 2, dtype=np.float64)
    inv_freq = 1.0 / 10000.0 ** (half / HD)
    t = np.arange(origin, origin + L, dtype=np.float64)
    freqs = np.concatenate([t[:, None] * inv_freq[None, :]] * 2, -1)
    base_scale = (half + 0.4 * HD) / (1.4 * HD)
    power = (t - N // 2) / (WIN // 2)
    scale = np.concatenate([base_scale[None, :] ** power[:, None]] * 2, -1)
    cos, sin = np.cos(freqs), np.sin(freqs)
    sign = np.where(np.arange(HD) % 2 == 0, -1.0, 1.0)
    tabs = []
    for sc in (scale, 1.0 / scale):
        tabs.append((cos * sc)[:, ROT_PERM])
        tabs.append((sin * sc)[:, ROT_PERM] * sign[None, :])
    return [np.ascontiguousarray(np.tile(a.T, (2, 1))).astype(np.float32)
            for a in tabs]


def perm_wqkv_np(wqkv):
    w = wqkv.copy()
    for h in range(H):
        w[:, h * HD: (h + 1) * HD] = wqkv[:, h * HD + ROT_PERM]
        w[:, D + h * HD: D + (h + 1) * HD] = wqkv[:, D + h * HD + ROT_PERM]
    return np.ascontiguousarray(w)


def scale_cols_np(qs, ks):
    pair = np.arange(HD).reshape(32, 2)[:, ::-1].reshape(-1)
    qi, ki = qs[ROT_PERM], ks[ROT_PERM]
    cols = np.stack([qi, qi[pair], ki, ki[pair]], 1)
    return np.ascontiguousarray(np.tile(cols, (2, 1))).astype(np.float32)


def band_masks_np(key_g0=None, blk_key0_g=None, blk_q0_g=None):
    q = np.arange(512)
    out = np.zeros((128, 6 * 512), np.float32)
    for kc in range(6):
        j = np.arange(128) + kc * 128
        m = (q[None, :] <= j[:, None]) & (j[:, None] <= q[None, :] + 256)
        if key_g0 is not None:
            keybad = (blk_key0_g + j[:, None]) < key_g0
            qreal = (blk_q0_g + q[None, :]) >= 0
            m = m & ~(keybad & qreal)
        out[:, kc * 512:(kc + 1) * 512] = np.where(m, 0.0, -1e5)
    return out


def chunk_cols_np(v):
    nch = len(v) // 128
    return np.ascontiguousarray(v.reshape(nch, 128).T).astype(np.float32)


def load_w_ff(k, wtile, wsrc, din, mc, mcols=128, key="w", base_col=0,
              q="sp"):
    kc_n = din // 128
    src = wsrc.rearrange("(kc p) m -> p kc m", p=128)
    src = src[:, :, base_col + mc * 128: base_col + mc * 128 + mcols]
    dst = wtile.h.ap()[:, 0:kc_n * mcols].rearrange(
        "p (kc m) -> p kc m", kc=kc_n)
    return k.dma(q, dst, src, key=key, wr=(wtile,))


def emit_ln(k, pp, st, xbf, tbuf, t0, tn, out_bf, C, gcol=None, bcol=None,
            eps=EPS_LN):
    sq0, sq1, row, tmp = st["sq0"], st["sq1"], st["row"], st["tmp"]
    ones2 = C["ones2"]
    nt = ceil_div(tn, 512)
    for j in range(nt):
        c0, cn = j * 512, min(512, tn - j * 512)
        ones_all = C["ones64"]
        pa = pp.take()
        for kc in range(NCH):
            mm(k, pa, pa[:1, :cn], ones_all[:, 0:1],
               xbf[:, kc * tbuf + t0 + c0: kc * tbuf + t0 + c0 + cn],
               rd=(ones_all, xbf), start=(kc == 0), stop=(kc == NCH - 1))
        pb = pp.take()
        for kc in range(NCH):
            sq = (sq0, sq1)[kc % 2]
            k.op("act", "activation", sq[:, :cn],
                 xbf[:, kc * tbuf + t0 + c0: kc * tbuf + t0 + c0 + cn],
                 AF.Square, rd=(xbf,), wr=(sq,))
            mm(k, pb, pb[:1, :cn], ones_all[:, 0:1], sq[:, :cn],
               rd=(ones_all, sq), start=(kc == 0), stop=(kc == NCH - 1))
        mu = row[:1, c0: c0 + cn]
        va = row[:1, tn + c0: tn + c0 + cn]
        t1 = tmp[:1, :cn]
        k.op("dve", "tensor_scalar_mul", mu, pa[:1, :cn], 1.0 / D,
             rd=(pa,), wr=(row,))
        k.op("dve", "tensor_scalar_mul", va, pb[:1, :cn], 1.0 / D,
             rd=(pb,), wr=(row,))
        k.op("dve", "tensor_mul", t1, mu, mu, rd=(row,), wr=(tmp,))
        k.op("dve", "tensor_sub", va, va, t1, rd=(row, tmp), wr=(row,))
        m8l = C["m8"]
        k.op("act", "activation", va, va, AF.Ln, bias=m8l[:1, 5:6],
             rd=(row, m8l), wr=(row,))
        k.op("act", "activation", va, va, AF.Exp, scale=m8l[:1, 4:5],
             rd=(row, m8l), wr=(row,))
        pm = pp.take()
        mm(k, pm, pm[:, :cn], C["onesf"][:1, 0:128], mu,
           rd=(C["onesf"], row), start=True, stop=True)
        pr = pp.take()
        mm(k, pr, pr[:, :cn], C["onesf"][:1, 0:128], va,
           rd=(C["onesf"], row), start=True, stop=True)
        for kc in range(NCH):
            xs = xbf[:, kc * tbuf + t0 + c0: kc * tbuf + t0 + c0 + cn]
            os_ = out_bf[:, kc * tbuf + t0 + c0: kc * tbuf + t0 + c0 + cn]
            tm = tmp[:, :cn]
            k.op("dve", "tensor_sub", tm, xs, pm[:, :cn], rd=(xbf, pm),
                 wr=(tmp,))
            if gcol is None:
                k.op("dve", "tensor_mul", os_, tm, pr[:, :cn], rd=(tmp, pr),
                     wr=(out_bf,))
            else:
                gb, gc = gcol
                bb, bc = bcol
                k.op("dve", "tensor_mul", tm, tm, pr[:, :cn], rd=(tmp, pr),
                     wr=(tmp,))
                k.op("act", "activation", os_, tm, AF.Identity,
                     bias=bb[:, bc + kc: bc + kc + 1],
                     scale=gb[:, gc + kc: gc + kc + 1],
                     rd=(tmp, gb, bb), wr=(out_bf,))


def emit_shift_cast(k, xs_bf, tbuf, dcol, src, src_tbuf, src_off, cn,
                    flag=None, fcol=None):
    dst3 = xs_bf.ap()[:, 0: NCH * tbuf].rearrange(
        "p (kc t) -> p kc t", kc=NCH)
    src3 = src.ap()[:, 0: NCH * src_tbuf].rearrange(
        "p (kc t) -> p kc t", kc=NCH)
    for half in range(2):
        sh = half
        s0 = src_off - sh
        if s0 >= 0:
            k.op("act", "copy",
                 dst3[:, 4 * half: 4 * half + 4, dcol: dcol + cn],
                 src3[:, 4 * half: 4 * half + 4, s0: s0 + cn],
                 rd=(src,), wr=(xs_bf,))
        else:
            k.op("act", "copy",
                 dst3[:, 4 * half: 4 * half + 4, dcol + 1: dcol + cn],
                 src3[:, 4 * half: 4 * half + 4, 0: cn - 1],
                 rd=(src,), wr=(xs_bf,))
            k.op("act", "copy",
                 dst3[:, 4 * half: 4 * half + 4, dcol: dcol + 1],
                 src3[:, 4 * half: 4 * half + 4, 0: 1],
                 rd=(src,), wr=(xs_bf,))
    if flag is not None and fcol is not None:
        for kc in range(4, NCH):
            cpos = kc * tbuf + dcol + fcol
            k.op("dve", "tensor_scalar_mul", xs_bf[:, cpos: cpos + 1],
                 xs_bf[:, cpos: cpos + 1], flag.ap()[:, 0:1],
                 rd=(xs_bf, flag), wr=(xs_bf,))


def emit_qkv(k, pp, Bf, C, xs_bf, xs_str, rn, t_dst, wq_src):
    q_bf, k_bf, v_tm = Bf["qb"], Bf["kb"], Bf["vt"]
    wt = [Bf["wt0"], Bf["wt1"]]
    stg = Bf["stg0"]
    srow, sq = Bf["srow"], Bf["sq"]
    rt1, rt2 = Bf["rt1"], Bf["rt2"]
    ones2, sel2 = C["ones2"], C["sel2"]
    tabs = Bf["tabs"]
    KBUF = 3 * WIN
    for qk, tbase in (("q", 0), ("k", 2)):
        base_mc = 0 if qk == "q" else NCH
        load_w_ff(k, wt[0], wq_src, D, base_mc, 128, key="wq0")
        for c in range(NCH):
            if c + 1 < NCH:
                load_w_ff(k, wt[(c + 1) % 2], wq_src, D, base_mc + c + 1,
                          128, key="wq" + str((c + 1) % 2),
                          q="sp" if (c + 1) % 2 == 0 else "pool")
            wtile = wt[c % 2]
            for j in range(ceil_div(rn, 512)):
                c0, cn = j * 512, min(512, rn - j * 512)
                pq = pp.take()
                for kc in range(NCH):
                    mm(k, pq, pq[:, :cn], wtile[:, kc * 128: kc * 128 + 128],
                       xs_bf[:, kc * xs_str + c0: kc * xs_str + c0 + cn],
                       rd=(wtile, xs_bf), start=(kc == 0),
                       stop=(kc == NCH - 1))
                ps = pp.take()
                k.op("act", "activation", sq[:, :cn], pq[:, :cn], AF.Square,
                     rd=(pq,), wr=(sq,))
                mm(k, ps, ps[:2, :cn], ones2[:, 0:2], sq[:, :cn],
                   rd=(ones2, sq), start=True, stop=True)
                rr = srow[:2, c0: c0 + cn]
                k.op("act", "activation", rr, ps[:2, :cn], AF.Ln,
                     rd=(ps,), wr=(srow,))
                k.op("act", "activation", rr, rr, AF.Exp,
                     scale=C["m8"][:2, 4:5], rd=(srow, C["m8"]), wr=(srow,))
                k.op("dve", "tensor_scalar_min", rr, rr, 1e12, rd=(srow,),
                     wr=(srow,))
                pbx = pp.take()
                mm(k, pbx, pbx[:, :cn], sel2[:2, :], rr, rd=(sel2, srow),
                   start=True, stop=True)
                tcs = tabs[:, tbase * 512 + c0: tbase * 512 + c0 + cn]
                tss = tabs[:, (tbase + 1) * 512 + c0:
                           (tbase + 1) * 512 + c0 + cn]
                k.op("dve", "tensor_mul", rt1[:, :cn], pq[:, :cn], tcs,
                     rd=(pq, tabs), wr=(rt1,))
                k.op("dve", "stream_shuffle", rt2[:, :cn], pq[:, :cn],
                     SHUF_MASK, rd=(pq,), wr=(rt2,))
                k.op("pool", "tensor_mul", rt2[:, :cn], rt2[:, :cn], tss,
                     rd=(rt2, tabs), wr=(rt2,))
                k.op("dve", "tensor_add", rt1[:, :cn], rt1[:, :cn],
                     rt2[:, :cn], rd=(rt1, rt2), wr=(rt1,))
                if qk == "q":
                    k.op("dve", "tensor_mul", q_bf[:, c * 512 + c0:
                                                   c * 512 + c0 + cn],
                         rt1[:, :cn], pbx[:, :cn], rd=(rt1, pbx), wr=(q_bf,))
                else:
                    w0 = (t_dst + c0) % KBUF
                    n1 = min(cn, KBUF - w0)
                    k.op("dve", "tensor_mul",
                         k_bf[:, c * KBUF + w0: c * KBUF + w0 + n1],
                         rt1[:, :n1], pbx[:, :n1], rd=(rt1, pbx), wr=(k_bf,))
                    if n1 < cn:
                        k.op("dve", "tensor_mul",
                             k_bf[:, c * KBUF: c * KBUF + cn - n1],
                             rt1[:, n1:cn], pbx[:, n1:cn], rd=(rt1, pbx),
                             wr=(k_bf,))
    wv = Bf["wv"]
    k.dma("sp", wv[:, 0:NCH * 1024].rearrange("p (kc m) -> p kc m", kc=NCH),
          wq_src.rearrange("(kc p) m -> p kc m", p=128)[:, :, 2 * D: 3 * D],
          key="wv", wr=(wv,))
    for tc in range(ceil_div(rn, 128)):
        cn = min(128, rn - tc * 128)
        slot = ((t_dst + tc * 128) // 128) % 6
        for half in range(2):
            pv = pp.take()
            for kc in range(NCH):
                mm(k, pv, pv[:cn, :512],
                   xs_bf[:, kc * xs_str + tc * 128: kc * xs_str + tc * 128 + cn],
                   wv[:, kc * 1024 + half * 512:
                      kc * 1024 + half * 512 + 512],
                   rd=(xs_bf, wv), start=(kc == 0), stop=(kc == NCH - 1))
            k.op("act", "copy",
                 v_tm[:cn, slot * 1024 + half * 512:
                      slot * 1024 + half * 512 + 512],
                 pv[:cn, :512], rd=(pv,), wr=(v_tm,))


def emit_block_attn(k, pp, Bf, C, wq0, nw, q_lo, q_hi, mask, mcol0):
    q_bf, k_bf, v_tm, attn_bf = Bf["qb"], Bf["kb"], Bf["vt"], Bf["at"]
    Pb, rc = Bf["p0"], Bf["rec0"]
    ones64 = C["ones64"]
    m8 = C["m8"]
    KBUF = 3 * WIN
    qa = max(q_lo, wq0 * WIN)
    qb_ = min(q_hi, (wq0 + nw) * WIN)
    qn = qb_ - qa
    qrel = qa - wq0 * WIN
    nkc = 2 * (nw + 1)
    for h in range(H):
        c, hp = h // 2, h % 2
        plo, phi = (0, 64) if hp == 0 else (64, 128)
        pss = pp.take_contig(nkc)
        pd = pp.take()
        pvp = pp.take()
        for kc in range(nkc):
            key0 = (wq0 - 1) * WIN + kc * 128
            kcol = (key0 % KBUF)
            mm(k, pss[kc], pss[kc][:, :qn],
               k_bf[plo:phi, c * KBUF + kcol: c * KBUF + kcol + 128],
               q_bf[plo:phi, c * 512 + qa - wq0 * WIN:
                    c * 512 + qa - wq0 * WIN + qn],
               rd=(k_bf, q_bf), start=True, stop=True)
        for hb0, hb1 in ((0, nkc // 2), (nkc // 2, nkc)):
            nb = hb1 - hb0
            sspan = pp.span_ap(pss[hb0], nb, (0, 128), qn)
            mspan = mask.ap()[:, mcol0 + hb0 * 512:
                              mcol0 + hb1 * 512].rearrange(
                "p (b c) -> p b c", b=nb)[:, :, qrel: qrel + qn]
            pbspan = Pb.ap()[:, hb0 * 512: hb1 * 512].rearrange(
                "p (b c) -> p b c", b=nb)[:, :, 0:qn]
            ph = tuple(pss[hb0:hb1])
            k.op("dve", "tensor_add", sspan, sspan, mspan,
                 rd=(mask,) + ph, wr=ph)
            k.op("act", "activation", pbspan, sspan, AF.Exp,
                 bias=m8[:, 0:1], scale=m8[:, 1:2],
                 rd=ph + (m8,), wr=(Pb,))
            for kc in range(hb0, hb1):
                key0 = (wq0 - 1) * WIN + kc * 128
                mm(k, pd, pd[plo:phi, :qn], ones64[:, 0:64],
                   Pb[:, kc * 512: kc * 512 + qn], rd=(ones64, Pb),
                   start=(kc == 0), stop=(kc == nkc - 1))
                mm(k, pvp, pvp[plo:phi, :qn],
                   v_tm[:, (key0 // 128 % 6) * 1024 + h * 64:
                        (key0 // 128 % 6) * 1024 + h * 64 + 64],
                   Pb[:, kc * 512: kc * 512 + qn], rd=(v_tm, Pb),
                   start=(kc == 0), stop=(kc == nkc - 1))
        k.op("act", "activation", rc[plo:phi, :qn], pd[plo:phi, :qn],
             AF.Ln, rd=(pd,), wr=(rc,))
        k.op("act", "activation", rc[plo:phi, :qn], rc[plo:phi, :qn],
             AF.Exp, scale=m8.ap()[plo:phi, 3:4], rd=(rc, m8), wr=(rc,))
        k.op("dve", "tensor_mul",
             attn_bf[plo:phi, c * 512 + qa - wq0 * WIN:
                     c * 512 + qa - wq0 * WIN + qn],
             pvp[plo:phi, :qn], rc[plo:phi, :qn], rd=(pvp, rc),
             wr=(attn_bf,))


def emit_wo_residual(k, pp, Bf, tn, wo_t, x_load, x_out):
    attn_bf, wob = Bf["at"], Bf["xacc"]
    pbs = []
    for mc in range(NCH):
        pb = pp.take()
        for kc in range(NCH):
            mm(k, pb, pb[:, :tn],
               wo_t[:, kc * 1024 + mc * 128: kc * 1024 + mc * 128 + 128],
               attn_bf[:, kc * 512: kc * 512 + tn],
               rd=(wo_t, attn_bf), start=(kc == 0), stop=(kc == NCH - 1))
        pbs.append(pb)
    for j in range(ceil_div(tn, 256)):
        c0, cn = j * 256, min(256, tn - j * 256)
        xres, xsl = x_load(c0, cn)
        for mc in range(NCH):
            k.op("dve", "tensor_add",
                 wob[:, mc * 512 + c0: mc * 512 + c0 + cn],
                 pbs[mc][:, c0: c0 + cn], xsl(mc, cn),
                 rd=(pbs[mc], xres), wr=(wob,))
    x_out(wob, tn)


def emit_ffn(k, pp, Bf, C, xn_bf, tbuf, tn, w1src, w2src, b1t, b2col,
             x_src, x_out, mid_cb=None):
    JN = ceil_div(DI, 128)
    w1a = [Bf["wt0"], Bf["wt1"]]
    w1g = [Bf["wg0"], Bf["wg1"]]
    w2t, hg, xacc, xnew, gbf = (Bf["vt"], Bf["kb"], Bf["xacc"], Bf["xnew"],
                                Bf["gbf"])
    groups = [list(range(0, 11)), list(range(11, JN))]
    for gi, grp in enumerate(groups):
        for jj, j in enumerate(grp):
            rj = min(128, DI - 128 * j)
            wa, wg = w1a[jj % 2], w1g[jj % 2]
            srca = w1src.rearrange("(kc p) m -> p kc m", p=128)
            k.dma("sp", wa[:, 0:NCH * rj].rearrange(
                      "p (kc m) -> p kc m", kc=NCH),
                  srca[:, :, 128 * j: 128 * j + rj],
                  key="w1a" + str(jj % 2), wr=(wa,))
            k.dma("pool", wg[:, 0:NCH * rj].rearrange(
                      "p (kc m) -> p kc m", kc=NCH),
                  srca[:, :, DI + 128 * j: DI + 128 * j + rj],
                  key="w1g" + str(jj % 2), wr=(wg,))
            for j2 in range(ceil_div(tn, 512)):
                c0, cn = j2 * 512, min(512, tn - j2 * 512)
                pg = pp.take()
                pa = pp.take()
                for kc in range(NCH):
                    mm(k, pg, pg[:rj, :cn], wg[:, kc * rj: kc * rj + rj],
                       xn_bf[:, kc * tbuf + c0: kc * tbuf + c0 + cn],
                       rd=(wg, xn_bf), start=(kc == 0), stop=(kc == NCH - 1))
                for kc in range(NCH):
                    mm(k, pa, pa[:rj, :cn], wa[:, kc * rj: kc * rj + rj],
                       xn_bf[:, kc * tbuf + c0: kc * tbuf + c0 + cn],
                       rd=(wa, xn_bf), start=(kc == 0), stop=(kc == NCH - 1))
                k.op("act", "activation", gbf[:rj, :cn], pg[:rj, :cn],
                     AF.Gelu, bias=b1t[:rj, 22 + j: 23 + j], rd=(pg, b1t),
                     wr=(gbf,))
                k.op("dve", "scalar_tensor_tensor",
                     hg[:rj, jj * tn + c0: jj * tn + c0 + cn],
                     pa[:rj, :cn], b1t.ap()[0:rj, j: j + 1], gbf[:rj, :cn],
                     ALU.add, ALU.mult, rd=(pa, b1t, gbf), wr=(hg,))
        if gi == len(groups) - 1 and mid_cb is not None:
            mid_cb()
        for jj, j in enumerate(grp):
            rj = min(128, DI - 128 * j)
            k.dma("pool", w2t[0:rj, jj * 1024: (jj + 1) * 1024],
                  w2src[128 * j: 128 * j + rj, :], key="w2", wr=(w2t,))
        for tt in range(ceil_div(tn, 512)):
            c0, cn = tt * 512, min(512, tn - tt * 512)
            for mh in range(2):
                pxs = pp.take_contig(4)
                for mi in range(4):
                    mc = mh * 4 + mi
                    px = pxs[mi]
                    for jj, j in enumerate(grp):
                        rj = min(128, DI - 128 * j)
                        mm(k, px, px[:, :cn],
                           w2t[0:rj, jj * 1024 + mc * 128:
                               jj * 1024 + mc * 128 + 128],
                           hg[0:rj, jj * tn + c0: jj * tn + c0 + cn],
                           rd=(w2t, hg), start=(jj == 0),
                           stop=(jj == len(grp) - 1))
                dst3 = xacc.ap()[:, mh * 4 * tn: mh * 4 * tn
                                 + 4 * tn].rearrange(
                    "p (m t) -> p m t", m=4)[:, :, c0: c0 + cn]
                src3 = pp.span_ap(pxs[0], 4, (0, 128), cn)
                if gi == 0:
                    k.op("dve", "tensor_copy", dst3, src3,
                         rd=tuple(pxs), wr=(xacc,))
                else:
                    k.op("dve", "tensor_add", dst3, dst3, src3,
                         rd=tuple(pxs) + (xacc,), wr=(xacc,))
    b2b, b2c = b2col
    for tt in range(ceil_div(tn, 256)):
        c0, cn = tt * 256, min(256, tn - tt * 256)
        xb, xsl = x_src(tt, cn)
        for mc in range(NCH):
            k.op("dve", "scalar_tensor_tensor",
                 xnew[:, mc * 256: mc * 256 + cn],
                 xacc[:, mc * tn + c0: mc * tn + c0 + cn],
                 b2b.ap()[:, b2c + mc: b2c + mc + 1], xsl(mc, cn),
                 ALU.add, ALU.add, rd=(xacc, b2b, xb), wr=(xnew,))
        x_out(tt, xnew, cn)


XS_STR = 544


def emit_inner_layer(k, pp, Bf, C, li, x_cur, x_nxt, wq_src, wo_src, w1_src,
                     w2_src, tab_hbm, bm_hbm, b1t, attn_lo, ffn_lo,
                     gflag_col, lbuf=L_TOK, x_wr=None):
    if x_wr is None:
        x_wr = x_cur
    """One inner layer on the local window. x_cur: DBuf [128, NCH*lbuf]
    fp32 (input; ffn writes back on [ffn_lo, lbuf)); x_nxt: DBuf scratch
    (post-attn). tab_hbm: DBuf [128, 4*lbuf] rotary base tables.
    bm_hbm: DBuf [128, 6*512] this layer's boundary-block mask (per-core
    content; equals the standard band mask on heavy cores)."""
    sc = C["scalecols"]

    def x_tile_load(xsrc, lo, ln_, which):
        xt = Bf["xf"][which % 2]
        k.dma("sp", xt[:, : NCH * ln_].rearrange("p (kc t) -> p kc t",
                                                 kc=NCH),
              xsrc.ap().rearrange("p (kc t) -> p kc t", kc=NCH)[
                  :, :, lo: lo + ln_],
              key="xf" + str(which % 2), rd=(xsrc,), wr=(xt,))
        return xt

    def load_tables(r0, rn):
        tabs = Bf["tabs"]
        k.dma("pool", tabs.ap()[:, 0: 4 * 512].rearrange(
                  "p (ti t) -> p ti t", ti=4)[:, :, 0:rn],
              tab_hbm.ap().rearrange("p (ti t) -> p ti t", ti=4)[
                  :, :, r0: r0 + rn],
              key="tab0", rd=(tab_hbm,), wr=(tabs,))
        for ti in range(4):
            k.op("dve", "tensor_scalar_mul", tabs[:, ti * 512: ti * 512 + rn],
                 tabs[:, ti * 512: ti * 512 + rn],
                 sc.ap()[:, 4 * li + ti: 4 * li + ti + 1], rd=(tabs, sc),
                 wr=(tabs,))

    def shift_cast_range(xsrc, t0, tn):
        xs_bf = Bf["xs"]
        for j in range(ceil_div(tn, 256)):
            c0, cn = j * 256, min(256, tn - j * 256)
            lo = max(0, t0 + c0 - 1)
            ln_ = t0 + c0 + cn - lo
            xt = x_tile_load(xsrc, lo, ln_, j)
            off = t0 + c0 - lo
            fcol = None
            if gflag_col is not None and c0 <= gflag_col - t0 < c0 + cn:
                fcol = gflag_col - t0 - c0
            emit_shift_cast(k, xs_bf, XS_STR, c0, xt, ln_, off, cn,
                            C["flag"] if fcol is not None else None, fcol)
        return xs_bf

    k.dma("sp", C["bmask"][:, :], bm_hbm.ap(), key="bml",
          rd=(bm_hbm,), wr=(C["bmask"],))

    w_lo = attn_lo // WIN
    nwin = (lbuf - w_lo * WIN) // WIN
    blocks = []
    w = w_lo
    if nwin % 2 == 1:
        blocks.append((w, 1))
        w += 1
    while w < lbuf // WIN:
        blocks.append((w, 2))
        w += 2
    for bi, (wq0, nw) in enumerate(blocks):
        first = bi == 0
        if first:
            lb0 = (wq0 - 1) * WIN
            load_tables(lb0, WIN)
            xs = shift_cast_range(x_cur, lb0, WIN)
            emit_qkv(k, pp, Bf, C, xs, XS_STR, WIN, lb0, wq_src)
        t0 = wq0 * WIN
        tn = nw * WIN
        load_tables(t0, tn)
        xs = shift_cast_range(x_cur, t0, tn)
        emit_qkv(k, pp, Bf, C, xs, XS_STR, tn, t0, wq_src)
        bm = (wq0 <= W_BOUND_DYN[0] < wq0 + nw)
        mask = C["bmask"] if bm else C["masks"]
        emit_block_attn(k, pp, Bf, C, wq0, nw, wq0 * WIN, (wq0 + nw) * WIN,
                        mask, 0)
        wo_t = Bf["wv"]
        k.dma("sp", wo_t[:, 0:NCH * 1024].rearrange(
                  "p (kc m) -> p kc m", kc=NCH),
              wo_src.rearrange("(kc p) m -> p kc m", p=128), key="wv",
              wr=(wo_t,))
        q_lo = wq0 * WIN

        def x_load(c0, cn, q_lo=q_lo):
            xt = x_tile_load(x_cur, q_lo + c0, cn, c0 // 256)

            def sl(mc, cnn, xt=xt, cn=cn):
                return xt[:, mc * cn: mc * cn + cnn]
            return xt, sl

        def x_out(wob, gn, q_lo=q_lo):
            k.dma("sp", x_nxt.ap().rearrange("p (kc t) -> p kc t", kc=NCH)[
                      :, :, q_lo: q_lo + gn],
                  wob.ap()[:, 0: NCH * 512].rearrange(
                      "p (kc t) -> p kc t", kc=NCH)[:, :, 0:gn],
                  key="xw", rd=(wob,), wr=(x_nxt,))

        emit_wo_residual(k, pp, Bf, tn, wo_t, x_load, x_out)
    total = lbuf - ffn_lo
    npieces = ceil_div(total, FFNP)
    pieces = [(ffn_lo + pi * FFNP, min(FFNP, lbuf - ffn_lo - pi * FFNP))
              for pi in range(npieces)]

    def shift_ln(p0, pn):
        xs2, xn = Bf["xs"], Bf["qb"]
        for j in range(ceil_div(pn, 256)):
            c0, cn = j * 256, min(256, pn - j * 256)
            xt = x_tile_load(x_nxt, p0 + c0 - 1, cn + 1, j)
            fcol = None
            if gflag_col is not None and c0 <= gflag_col - p0 < c0 + cn:
                fcol = gflag_col - p0 - c0
            emit_shift_cast(k, xs2, FFNP, c0, xt, cn + 1, 1, cn,
                            C["flag"] if fcol is not None else None, fcol)
        emit_ln(k, pp, Bf["lnst"], xs2, FFNP, 0, pn, xn, C,
                gcol=(C["lncols"], 16 * li), bcol=(C["lncols"], 16 * li + 8))

    shift_ln(*pieces[0])
    for pi in range(npieces):
        p0, pn = pieces[pi]
        xn = Bf["qb"]
        mid = None
        if pi + 1 < npieces:
            mid = (lambda nxt=pieces[pi + 1]: shift_ln(*nxt))

        def fx_src(tt, cn, p0=p0):
            xt = x_tile_load(x_nxt, p0 + tt * 256, cn, tt)

            def sl(mc, cnn, xt=xt, cn=cn):
                return xt[:, mc * cn: mc * cn + cnn]
            return xt, sl

        def fx_out(tt, xnew, cn, p0=p0):
            k.dma("sp", x_wr.ap().rearrange("p (kc t) -> p kc t", kc=NCH)[
                      :, :, p0 + tt * 256: p0 + tt * 256 + cn],
                  xnew.ap()[:, 0: NCH * 256].rearrange(
                      "p (mc t) -> p mc t", mc=NCH)[:, :, 0:cn],
                  key="xw", rd=(xnew,), wr=(x_wr,))

        emit_ffn(k, pp, Bf, C, xn, FFNP, pn, w1_src, w2_src, b1t,
                 (C["lncols"], 96 + 8 * li), fx_src, fx_out, mid_cb=mid)


W_BOUND_DYN = [W_BOUND]


def layer_blocks(attn_lo, lbuf=L_TOK):
    w_lo = attn_lo // WIN
    nwin = (lbuf - w_lo * WIN) // WIN
    blocks = []
    w = w_lo
    if nwin % 2 == 1:
        blocks.append((w, 1))
        w += 1
    while w < lbuf // WIN:
        blocks.append((w, 2))
        w += 2
    return blocks


def boundary_masks_np(attn_lo, light, d_l=D_L, w_bound=None, lbuf=L_TOK):
    wb = w_bound if w_bound is not None else (d_l // WIN)
    blk = [b for b in layer_blocks(attn_lo, lbuf) if b[0] <= wb < b[0] + b[1]]
    if not blk or not light:
        return band_masks_np()
    wq0, nw = blk[0]
    return band_masks_np(key_g0=0, blk_key0_g=(wq0 - 1) * WIN - d_l,
                         blk_q0_g=wq0 * WIN - d_l)


def make_bufs(k):
    Bf = {
        "xs": k.sbuf("xs", (128, NCH * FFNP), BF16),
        "qb": k.sbuf("qb", (128, NCH * FFNP), BF16),
        "kb": k.sbuf("kb", (128, 11 * FFNP), BF16),
        "vt": k.sbuf("vt", (128, 11 * 1024), BF16),
        "at": k.sbuf("at", (128, NCH * 512), BF16),
        "tabs": k.sbuf("tabs", (128, 4 * 512), BF16),
        "stg0": k.sbuf("stg0", (128, 512), F32),
        "rt1": k.sbuf("rt1", (128, 1024), F32),
        "rt2": k.sbuf("rt2", (128, 512), F32),
        "sq": k.sbuf("sqx", (128, 512), BF16),
        "srow": k.sbuf("srow", (2, 1024 + 2 * FFNP), F32),
        "wt0": k.sbuf("wt0", (128, 1024), BF16),
        "wt1": k.sbuf("wt1", (128, 1024), BF16),
        "wg0": k.sbuf("wg0", (128, 1024), BF16),
        "wg1": k.sbuf("wg1", (128, 1024), BF16),
        "wv": k.sbuf("wv", (128, NCH * 1024), BF16),
        "xnew": k.sbuf("xnew", (128, NCH * 256), F32),
        "p0": k.sbuf("p0", (128, 6 * 512), BF16),
        "rec0": k.sbuf("rec0", (128, 512), F32),
        "xacc": k.sbuf("xacc", (128, NCH * FFNP), F32),
        "gbf": k.sbuf("gbf", (128, 512), BF16),
        "xf": [k.sbuf("xf0", (128, NCH * 257), F32),
               k.sbuf("xf1", (128, NCH * 257), F32)],
        "lnst": None,
    }
    Bf["lnst"] = {
        "sq0": Bf["sq"],
        "sq1": Bf["gbf"],
        "row": RowView(Bf["srow"], 0, 1024),
        "tmp": Bf["rt1"],
    }
    return Bf


def make_consts(k, cbf_hbm, csel_hbm, conesf_hbm, cmisc_hbm, mask_hbm,
                sc_hbm, ln_hbm):
    C = {}
    C["ones2"] = k.sbuf("ones2", (128, 2), BF16)
    C["ones64"] = k.sbuf("ones64", (128, 64), BF16)
    C["onesf"] = k.sbuf("onesf", (1, 128), F32)
    C["sel2"] = k.sbuf("sel2", (2, 128), F32)
    C["m8"] = k.sbuf("m8", (128, 8), F32)
    C["masks"] = k.sbuf("masks", (128, 6 * 512), BF16)
    C["bmask"] = k.sbuf("bmask", (128, 6 * 512), BF16)
    C["scalecols"] = k.sbuf("sccols", (128, 24), F32)
    C["lncols"] = k.sbuf("lncols", (128, 96 + 48), F32)
    C["flag"] = RowColView(C["m8"], 2)
    cbf = k.sbuf("cbf", (128, 66), BF16)
    k.dma("pool", cbf[:, :], cbf_hbm, key="cst", wr=(cbf,))
    k.op("dve", "tensor_copy", C["ones2"][:, :], cbf[:, 0:2], rd=(cbf,),
         wr=(C["ones2"],))
    k.op("dve", "tensor_copy", C["ones64"][:, :], cbf[:, 2:66], rd=(cbf,),
         wr=(C["ones64"],))
    k.dma("pool", C["sel2"][:, :], csel_hbm, key="cst", wr=(C["sel2"],))
    k.dma("pool", C["onesf"][:, :], conesf_hbm, key="cst", wr=(C["onesf"],))
    k.dma("pool", C["m8"][:, :], cmisc_hbm, key="cst", wr=(C["m8"],))
    k.dma("pool", C["masks"][:, :], mask_hbm, key="cst", wr=(C["masks"],))
    k.dma("pool", C["scalecols"][:, :], sc_hbm, key="cst",
          wr=(C["scalecols"],))
    k.dma("pool", C["lncols"][:, :], ln_hbm, key="cst", wr=(C["lncols"],))
    return C


class RowView:
    def __init__(self, buf, row, col0=0):
        self.b = buf
        self.row = row
        self.col0 = col0

    def __getitem__(self, idx):
        p, c = idx
        c = slice((c.start or 0) + self.col0, (c.stop or 0) + self.col0)
        return self.b.ap()[self.row: self.row + 1, c]

    def ap(self):
        return self.b.ap()[self.row: self.row + 1, self.col0:]

    @property
    def w(self):
        return self.b.w

    @w.setter
    def w(self, v):
        self.b.w = v

    @property
    def r(self):
        return self.b.r

    @r.setter
    def r(self, v):
        self.b.r = v


class RowColView:
    def __init__(self, buf, col0):
        self.b = buf
        self.col0 = col0

    def ap(self):
        return self.b.ap()[:, self.col0:]

    @property
    def w(self):
        return self.b.w

    @w.setter
    def w(self, v):
        self.b.w = v

    @property
    def r(self):
        return self.b.r

    @r.setter
    def r(self, v):
        self.b.r = v


def host_const_arrays():
    cbf = np.zeros((128, 66), BF)
    cbf[:, 0] = 1
    cbf[64:, 0] = 0
    cbf[64:, 1] = 1
    cbf[:, 2:66] = 1
    csel = np.zeros((2, 128), np.float32)
    csel[0, 0:64] = 1
    csel[1, 64:128] = 1
    conesf = np.ones((1, 128), np.float32)
    return cbf, csel, conesf


OCOL_BOUT, OCOL_G, OCOL_B, OCOL_B2 = 0, 8, 16, 24


def emit_ln_stream(k, pp, Bf, C, x_src, src_lbuf, t0, tn, out_f32=None,
                   out_hbm=None, gcol=None, bcol=None):
    xs = Bf["xs"]
    for j in range(ceil_div(tn, 256)):
        c0, cn = j * 256, min(256, tn - j * 256)
        xt = Bf["xf"][j % 2]
        k.dma("sp", xt.h.ap()[:, : NCH * cn].rearrange(
                  "p (kc t) -> p kc t", kc=NCH),
              x_src.ap().rearrange("p (kc t) -> p kc t", kc=NCH)[
                  :, :, t0 + c0: t0 + c0 + cn],
              key="xf" + str(j % 2), rd=(x_src,), wr=(xt,))
        k.op("act", "copy",
             xs.ap()[:, 0: NCH * FFNP].rearrange(
                 "p (kc t) -> p kc t", kc=NCH)[:, :, c0: c0 + cn],
             xt.ap()[:, 0: NCH * cn].rearrange(
                 "p (kc t) -> p kc t", kc=NCH),
             rd=(xt,), wr=(xs,))
    emit_ln(k, pp, Bf["lnst"], xs, FFNP, 0, tn, xs, C, gcol=gcol, bcol=bcol)
    xs3 = xs.ap()[:, 0: NCH * FFNP].rearrange(
        "p (kc t) -> p kc t", kc=NCH)[:, :, 0:tn]
    if out_f32 is not None:
        k.op("dve", "tensor_copy",
             out_f32.ap()[:, 0: NCH * FFNP].rearrange(
                 "p (kc t) -> p kc t", kc=NCH)[:, :, 0:tn],
             xs3, rd=(xs,), wr=(out_f32,))
    if out_hbm is not None:
        k.dma("sp", out_hbm.ap().rearrange("p (kc t) -> p kc t", kc=NCH)[
                  :, :, t0: t0 + tn],
              xs3, key="lnw", rd=(xs,), wr=(out_hbm,))


def emit_perf_kv(k, pp, Bf, C, tg_hbm, wq_src, projT, kv_sb, n_tok):
    wt = [Bf["wt0"], Bf["wt1"]]
    kb, vt, kf = Bf["kb"], Bf["vt"], Bf["qb"]
    xsb = Bf["xs"]
    wv = Bf["wv"]
    k.dma("sp", wv[:, 0:NCH * 1024].rearrange("p (kc m) -> p kc m", kc=NCH),
          wq_src.rearrange("(kc p) m -> p kc m", p=128)[:, :, 2 * D: 3 * D],
          key="wv", wr=(wv,))
    for t0 in range(0, n_tok, 512):
        cn = min(512, n_tok - t0)
        first = t0 == 0
        k.dma("sp", xsb.h.ap()[:, : NCH * cn].rearrange(
                  "p (kc t) -> p kc t", kc=NCH),
              tg_hbm.ap().rearrange("p (kc t) -> p kc t", kc=NCH)[
                  :, :, t0: t0 + cn],
              key="xf0", rd=(tg_hbm,), wr=(xsb,))
        load_w_ff(k, wt[0], wq_src, D, NCH, 128, key="wq0")
        for c in range(NCH):
            if c + 1 < NCH:
                load_w_ff(k, wt[(c + 1) % 2], wq_src, D, NCH + c + 1, 128,
                          key="wq" + str((c + 1) % 2))
            wtile = wt[c % 2]
            pq = pp.take()
            for kc in range(NCH):
                mm(k, pq, pq[:, :cn], wtile[:, kc * 128: kc * 128 + 128],
                   xsb[:, kc * cn: kc * cn + cn], rd=(wtile, xsb),
                   start=(kc == 0), stop=(kc == NCH - 1))
            k.op("act", "copy", kb[:, c * 512: c * 512 + cn], pq[:, :cn],
                 rd=(pq,), wr=(kb,))
        for tc in range(ceil_div(cn, 128)):
            tn_ = min(128, cn - tc * 128)
            for half in range(2):
                pv = pp.take()
                for kc in range(NCH):
                    mm(k, pv, pv[:tn_, :512],
                       xsb[:, kc * cn + tc * 128: kc * cn + tc * 128 + tn_],
                       wv[:, kc * 1024 + half * 512:
                          kc * 1024 + half * 512 + 512],
                       rd=(xsb, wv), start=(kc == 0), stop=(kc == NCH - 1))
                k.op("act", "copy",
                     vt.h.ap()[0:tn_, tc * 1040 + half * 8 * 65:
                               tc * 1040 + half * 8 * 65 + 520].rearrange(
                         "p (h e) -> p h e", h=8)[:, :, 0:64],
                     pv.ap()[0:tn_, 0:512].rearrange(
                         "p (h e) -> p h e", h=8),
                     rd=(pv,), wr=(vt,))
            k.op("dve", "memset",
                 vt.h.ap()[0:tn_, tc * 1040: tc * 1040 + 1040].rearrange(
                     "p (h e) -> p h e", h=16)[:, :, 64:65],
                 1.0, wr=(vt,))
        for tc in range(ceil_div(cn, 128)):
            tn_ = min(128, cn - tc * 128)
            for grp in range(2):
                pfs = pp.take_contig(4)
                for cc in range(4):
                    c = grp * 4 + cc
                    pf = pfs[cc]
                    for par in range(2):
                        plo, phi = (0, 64) if par == 0 else (64, 128)
                        mm(k, pf, pf[:tn_, par * 256: par * 256 + 256],
                           kb[plo:phi, c * 512 + tc * 128:
                              c * 512 + tc * 128 + tn_],
                           projT[plo:phi, c * 256: c * 256 + 256],
                           rd=(kb, projT), start=True, stop=True)
                span = pp.big.h.ap()[0:tn_, pfs[0].col0: pfs[0].col0 + 2048]
                eb = Bf["at"]
                k.op("act", "activation", eb[:tn_, :2048], span,
                     AF.Exp, rd=tuple(pfs), wr=(eb,))
                k.op("dve", "tensor_scalar_min", eb[:tn_, :2048],
                     eb[:tn_, :2048], 1.0, rd=(eb,), wr=(eb,))
                k.op("dve", "scalar_tensor_tensor",
                     kf[:tn_, grp * 2048: grp * 2048 + 2048],
                     span, 0.0, eb[:tn_, :2048],
                     ALU.max, ALU.add, rd=tuple(pfs) + (eb,), wr=(kf,))
            for h in range(H):
                pks = pp.take_contig(2)
                for fc in range(2):
                    mm(k, pks[fc], pks[fc][:, :65],
                       kf[0:tn_, h * 256 + fc * 128:
                          h * 256 + fc * 128 + 128],
                       vt[0:tn_, tc * 1040 + h * 65: tc * 1040 + h * 65 + 65],
                       rd=(kf, vt), start=True, stop=True)
                dst3 = kv_sb.ap()[:, h * 130: h * 130 + 130].rearrange(
                    "p (f e) -> p f e", f=2)
                src3 = pp.big.h.ap()[:, pks[0].col0: pks[0].col0
                                     + 1024].rearrange(
                    "p (f e) -> p f e", f=2)[:, :, 0:65]
                if first and tc == 0:
                    k.op("dve", "tensor_copy", dst3, src3,
                         rd=tuple(pks), wr=(kv_sb,))
                else:
                    k.op("dve", "tensor_add", dst3, dst3, src3,
                         rd=tuple(pks) + (kv_sb,), wr=(kv_sb,))


def emit_outer_block(k, pp, Bf, C, x0w, x0g, x_nxt, x1_hbm, x_cur, tg_hbm,
                     wq_src, projT_hbm, wo_src, w1_src, w2_src, b1o, ocols,
                     lbuf=L_TOK, n_tok=N, out_lo=OUT0):
    projT = k.sbuf("projT", (128, 2048), BF16)
    k.dma("pool", projT[:, :], projT_hbm.ap(), key="cst", wr=(projT,))
    k.op("dve", "memset", Bf["xnew"][:, 0: NCH * 244], 0.0,
         wr=(Bf["xnew"],))
    k.dma("sp", x_cur.ap().rearrange("p (kc t) -> p kc t", kc=NCH)[
              :, :, 0: out_lo],
          Bf["xnew"].ap()[:, 0: NCH * out_lo].rearrange(
              "p (kc t) -> p kc t", kc=NCH),
          key="xw", rd=(Bf["xnew"],), wr=(x_cur,))
    kv_sb = Bf["xacc"]
    kvb = Bf["at"]

    for p0 in range(0, n_tok, FFNP):
        pn = min(FFNP, n_tok - p0)
        emit_ln_stream(k, pp, Bf, C, x0g, n_tok, p0, pn, out_hbm=tg_hbm)
    emit_perf_kv(k, pp, Bf, C, tg_hbm, wq_src, projT, kv_sb, n_tok)
    k.op("dve", "tensor_copy",
         kvb.ap()[:, 0:2048].rearrange("p (g e) -> p g e", g=2 * H),
         kv_sb.ap()[:, 0: 65 * 2 * H].rearrange(
             "p (g e) -> p g e", g=2 * H)[:, :, 0:64],
         rd=(kv_sb,), wr=(kvb,))
    for h2 in range(2 * H):
        k.op("dve", "tensor_scalar_mul", kvb[:, 2048 + h2 * 64:
                                             2048 + h2 * 64 + 64],
             C["ones64"][:, 0:64], kv_sb.ap()[:, h2 * 65 + 64: h2 * 65 + 65],
             rd=(C["ones64"], kv_sb), wr=(kvb,))

    wv = Bf["wv"]
    k.dma("sp", wv[:, 0:NCH * 1024].rearrange("p (kc m) -> p kc m", kc=NCH),
          wo_src.rearrange("(kc p) m -> p kc m", p=128), key="wv", wr=(wv,))
    wt = [Bf["wt0"], Bf["wt1"]]
    xs, q_fm, qf, rc = Bf["xs"], Bf["kb"], Bf["p0"], Bf["rec0"]
    attn_o, xnew = Bf["qb"], Bf["xnew"]
    for p0 in range(out_lo, lbuf, FFNP):
        pn = min(FFNP, lbuf - p0)
        emit_ln_stream(k, pp, Bf, C, x0w, lbuf, p0, pn)
        load_w_ff(k, wt[0], wq_src, D, 0, 128, key="wq0")
        for c in range(NCH):
            if c + 1 < NCH:
                load_w_ff(k, wt[(c + 1) % 2], wq_src, D, c + 1, 128,
                          key="wq" + str((c + 1) % 2))
            wtile = wt[c % 2]
            for j in range(ceil_div(pn, 512)):
                c0, cn = j * 512, min(512, pn - j * 512)
                pq = pp.take()
                for kc in range(NCH):
                    mm(k, pq, pq[:, :cn], wtile[:, kc * 128: kc * 128 + 128],
                       xs[:, kc * FFNP + c0: kc * FFNP + c0 + cn],
                       rd=(wtile, xs), start=(kc == 0), stop=(kc == NCH - 1))
                k.op("act", "copy", q_fm[:, c * FFNP + c0: c * FFNP + c0 + cn],
                     pq[:, :cn], rd=(pq,), wr=(q_fm,))
        for j in range(ceil_div(pn, 512)):
            c0, cn = j * 512, min(512, pn - j * 512)
            for h in range(H):
                c, par = h // 2, h % 2
                plo, phi = (0, 64) if par == 0 else (64, 128)
                pfab = pp.take_contig(2)
                for fc in range(2):
                    mm(k, pfab[fc], pfab[fc][:, :cn],
                       projT[plo:phi, c * 256 + fc * 128:
                             c * 256 + fc * 128 + 128],
                       q_fm[plo:phi, c * FFNP + c0: c * FFNP + c0 + cn],
                       rd=(projT, q_fm), start=True, stop=True)
                span = pp.span_ap(pfab[0], 2, (0, 128), cn)
                e1 = Bf["rt1"]
                eview = e1.ap()[:, 0:1024].rearrange(
                    "p (f e) -> p f e", f=2)[:, :, 0:cn]
                k.op("act", "activation", eview, span,
                     AF.Exp, rd=tuple(pfab), wr=(e1,))
                k.op("dve", "tensor_scalar_min", e1[:, :1024],
                     e1[:, :1024], 1.0, rd=(e1,), wr=(e1,))
                k.op("dve", "scalar_tensor_tensor",
                     qf.ap()[:, 0:1024].rearrange(
                         "p (f e) -> p f e", f=2)[:, :, 0:cn],
                     span, 0.0, eview,
                     ALU.max, ALU.add, rd=tuple(pfab) + (e1,), wr=(qf,))
                pa = pp.take()
                pd = pp.take()
                for fc in range(2):
                    h2 = h * 2 + fc
                    mm(k, pa, pa[plo:phi, :cn], kvb[:, h2 * 64: h2 * 64 + 64],
                       qf[:, fc * 512: fc * 512 + cn], rd=(kvb, qf),
                       start=(fc == 0), stop=(fc == 1))
                    mm(k, pd, pd[plo:phi, :cn],
                       kvb[:, 2048 + h2 * 64: 2048 + h2 * 64 + 64],
                       qf[:, fc * 512: fc * 512 + cn], rd=(kvb, qf),
                       start=(fc == 0), stop=(fc == 1))
                m8o = C["m8"]
                k.op("act", "activation", rc[plo:phi, :cn], pd[plo:phi, :cn],
                     AF.Ln, bias=m8o.ap()[plo:phi, 6:7], rd=(pd, m8o),
                     wr=(rc,))
                k.op("act", "activation", rc[plo:phi, :cn], rc[plo:phi, :cn],
                     AF.Exp, scale=m8o.ap()[plo:phi, 3:4], rd=(rc, m8o),
                     wr=(rc,))
                k.op("dve", "tensor_mul",
                     attn_o[plo:phi, c * FFNP + c0: c * FFNP + c0 + cn],
                     pa[plo:phi, :cn], rc[plo:phi, :cn], rd=(pa, rc),
                     wr=(attn_o,))
        ub = Bf["kb"]
        for g0 in range(0, pn, 512):
            gn = min(512, pn - g0)
            for mc in range(NCH):
                pb = pp.take()
                for kc in range(NCH):
                    mm(k, pb, pb[:, :gn],
                       wv[:, kc * 1024 + mc * 128: kc * 1024 + mc * 128 + 128],
                       attn_o[:, kc * FFNP + g0: kc * FFNP + g0 + gn],
                       rd=(wv, attn_o), start=(kc == 0), stop=(kc == NCH - 1))
                k.op("dve", "scalar_tensor_tensor",
                     ub[:, mc * FFNP + g0: mc * FFNP + g0 + gn], pb[:, :gn],
                     ocols.ap()[:, OCOL_BOUT + mc: OCOL_BOUT + mc + 1],
                     xs[:, mc * FFNP + g0: mc * FFNP + g0 + gn],
                     ALU.add, ALU.add, rd=(pb, ocols, xs), wr=(ub,))
        emit_ln(k, pp, Bf["lnst"], ub, FFNP, 0, pn, xs, C,
                gcol=(ocols, OCOL_G), bcol=(ocols, OCOL_B))
        emit_ffn_plain(k, pp, Bf, C, pn, w1_src, w2_src, b1o, ocols)
        for mc in range(NCH):
            k.op("dve", "scalar_tensor_tensor",
                 ub[:, mc * FFNP: mc * FFNP + pn],
                 Bf["xacc"][:, mc * FFNP: mc * FFNP + pn],
                 ocols.ap()[:, OCOL_B2 + mc: OCOL_B2 + mc + 1],
                 xs[:, mc * FFNP: mc * FFNP + pn],
                 ALU.add, ALU.add, rd=(Bf["xacc"], ocols, xs), wr=(ub,))
        emit_ln(k, pp, Bf["lnst"], ub, FFNP, 0, pn, xs, C,
                gcol=(ocols, OCOL_G), bcol=(ocols, OCOL_B))
        emit_ln(k, pp, Bf["lnst"], xs, FFNP, 0, pn, xs, C)
        for g0 in range(0, pn, 256):
            cn = min(256, pn - g0)
            xt = Bf["xf"][(g0 // 256) % 2]
            k.dma("sp", xt.h.ap()[:, : NCH * cn].rearrange(
                      "p (kc t) -> p kc t", kc=NCH),
                  x0w.ap().rearrange("p (kc t) -> p kc t", kc=NCH)[
                      :, :, p0 + g0: p0 + g0 + cn],
                  key="xf" + str((g0 // 256) % 2), rd=(x0w,), wr=(xt,))
            k.op("dve", "tensor_add",
                 xnew.ap()[:, 0: NCH * 256].rearrange(
                     "p (mc t) -> p mc t", mc=NCH)[:, :, 0:cn],
                 xs.ap()[:, 0: NCH * FFNP].rearrange(
                     "p (mc t) -> p mc t", mc=NCH)[:, :, g0: g0 + cn],
                 xt.ap()[:, 0: NCH * cn].rearrange(
                     "p (mc t) -> p mc t", mc=NCH),
                 rd=(xs, xt), wr=(xnew,))
            k.dma("sp", x1_hbm.ap().rearrange("p (kc t) -> p kc t", kc=NCH)[
                      :, :, p0 + g0: p0 + g0 + cn],
                  xnew.ap()[:, 0: NCH * 256].rearrange(
                      "p (mc t) -> p mc t", mc=NCH)[:, :, 0:cn],
                  key="xw", rd=(xnew,), wr=(x1_hbm,))
            k.op("act", "copy",
                 ub.ap()[:, 0: NCH * FFNP].rearrange(
                     "p (mc t) -> p mc t", mc=NCH)[:, :, g0: g0 + cn],
                 xnew.ap()[:, 0: NCH * 256].rearrange(
                     "p (mc t) -> p mc t", mc=NCH)[:, :, 0:cn],
                 rd=(xnew,), wr=(ub,))
        emit_ln(k, pp, Bf["lnst"], ub, FFNP, 0, pn, Bf["xacc"], C)
        k.dma("sp", x_cur.ap().rearrange("p (kc t) -> p kc t", kc=NCH)[
                  :, :, p0: p0 + pn],
              Bf["xacc"].ap()[:, 0: NCH * FFNP].rearrange(
                  "p (kc t) -> p kc t", kc=NCH)[:, :, 0:pn],
              key="xw", rd=(Bf["xacc"],), wr=(x_cur,))


def emit_ffn_plain(k, pp, Bf, C, tn, w1src, w2src, b1o, ocols):
    w1t = [Bf["wt0"], Bf["wt1"]]
    w2t, hg, xacc = Bf["vt"], Bf["kb"], Bf["xacc"]
    groups = [list(range(0, 11)), list(range(11, 22)), list(range(22, 32))]
    xs = Bf["xs"]
    for gi, grp in enumerate(groups):
        for jj, j in enumerate(grp):
            wa = w1t[jj % 2]
            load_w_ff(k, wa, w1src, D, j, 128, key="w1a" + str(jj % 2))
            for j2 in range(ceil_div(tn, 512)):
                c0, cn = j2 * 512, min(512, tn - j2 * 512)
                pa = pp.take()
                for kc in range(NCH):
                    mm(k, pa, pa[:, :cn], wa[:, kc * 128: kc * 128 + 128],
                       xs[:, kc * FFNP + c0: kc * FFNP + c0 + cn],
                       rd=(wa, xs), start=(kc == 0), stop=(kc == NCH - 1))
                k.op("act", "activation",
                     hg[:, jj * tn + c0: jj * tn + c0 + cn], pa[:, :cn],
                     AF.Gelu, bias=b1o[:, j: j + 1], rd=(pa, b1o), wr=(hg,))
        for jj, j in enumerate(grp):
            k.dma("pool", w2t[:, jj * 1024: (jj + 1) * 1024],
                  w2src[128 * j: 128 * j + 128, :], key="w2", wr=(w2t,))
        for tt in range(ceil_div(tn, 512)):
            c0, cn = tt * 512, min(512, tn - tt * 512)
            for mh in range(2):
                pxs = pp.take_contig(4)
                for mi in range(4):
                    mc = mh * 4 + mi
                    px = pxs[mi]
                    for jj, j in enumerate(grp):
                        mm(k, px, px[:, :cn],
                           w2t[:, jj * 1024 + mc * 128:
                               jj * 1024 + mc * 128 + 128],
                           hg[:, jj * tn + c0: jj * tn + c0 + cn],
                           rd=(w2t, hg), start=(jj == 0),
                           stop=(jj == len(grp) - 1))
                dst3 = xacc.ap()[:, mh * 4 * FFNP: mh * 4 * FFNP
                                 + 4 * FFNP].rearrange(
                    "p (m t) -> p m t", m=4)[:, :, c0: c0 + cn]
                src3 = pp.span_ap(pxs[0], 4, (0, 128), cn)
                if gi == 0:
                    k.op("dve", "tensor_copy", dst3, src3,
                         rd=tuple(pxs), wr=(xacc,))
                else:
                    k.op("dve", "tensor_add", dst3, dst3, src3,
                         rd=tuple(pxs) + (xacc,), wr=(xacc,))


def emit_final(k, pp, Bf, C, x_cur, x1_hbm, out_hbm, lo=D_L, lbuf=L_TOK):
    xs, xnew = Bf["xs"], Bf["xnew"]
    span = lbuf - lo
    for p0 in range(lo, lbuf, FFNP):
        pn = min(FFNP, lbuf - p0)
        emit_ln_stream(k, pp, Bf, C, x_cur, lbuf, p0, pn)
        for g0 in range(0, pn, 256):
            cn = min(256, pn - g0)
            xt = Bf["xf"][(g0 // 256) % 2]
            k.dma("sp", xt.h.ap()[:, : NCH * cn].rearrange(
                      "p (kc t) -> p kc t", kc=NCH),
                  x1_hbm.ap().rearrange("p (kc t) -> p kc t", kc=NCH)[
                      :, :, p0 + g0: p0 + g0 + cn],
                  key="xf" + str((g0 // 256) % 2), rd=(x1_hbm,), wr=(xt,))
            k.op("dve", "tensor_add",
                 xnew.ap()[:, 0: NCH * 256].rearrange(
                     "p (mc t) -> p mc t", mc=NCH)[:, :, 0:cn],
                 xs.ap()[:, 0: NCH * FFNP].rearrange(
                     "p (mc t) -> p mc t", mc=NCH)[:, :, g0: g0 + cn],
                 xt.ap()[:, 0: NCH * cn].rearrange(
                     "p (mc t) -> p mc t", mc=NCH),
                 rd=(xs, xt), wr=(xnew,))
            k.dma("sp", out_hbm.ap().rearrange("p (mc t) -> p mc t", mc=NCH)[
                      :, :, p0 - lo + g0: p0 - lo + g0 + cn],
                  xnew.ap()[:, 0: NCH * 256].rearrange(
                      "p (mc t) -> p mc t", mc=NCH)[:, :, 0:cn],
                  key="ow", rd=(xnew,), wr=(out_hbm,))


def _fm(x):
    T = x.shape[0]
    return np.ascontiguousarray(
        x.reshape(T, NCH, 128).transpose(2, 1, 0).reshape(128, NCH * T))


def _defm(a, T):
    return a.reshape(128, NCH, T).transpose(2, 1, 0).reshape(T, 1024)


def _const_drams(nc):
    dt = nc.dram_tensor
    return {
        "hcbf": dt("hcbf", [128, 66], BF16, kind="ExternalInput").ap(),
        "hcsel": dt("hcsel", [2, 128], F32, kind="ExternalInput").ap(),
        "hconesf": dt("hconesf", [1, 128], F32, kind="ExternalInput").ap(),
        "hcmisc": dt("hcmisc", [128, 8], F32, kind="ExternalInput").ap(),
        "hcmask": dt("hcmask", [128, 3072], BF16, kind="ExternalInput").ap(),
        "hcsc": dt("hcsc", [128, 24], F32, kind="ExternalInput").ap(),
        "hcln": dt("hcln", [128, 144], F32, kind="ExternalInput").ap(),
    }


def build_outer():
    nc = bass.Bass("TRN2", target_bir_lowering=False)
    dt = nc.dram_tensor
    x0w = DBuf(dt("x0w", [128, NCH * L_TOK], F32, kind="ExternalInput").ap())
    x0g = DBuf(dt("x0g", [128, NCH * N], F32, kind="ExternalInput").ap())
    wq = dt("wq", [1024, 3072], F32, kind="ExternalInput").ap()
    pjh = DBuf(dt("hprojT", [128, 2048], F32, kind="ExternalInput").ap())
    wo = dt("wo", [1024, 1024], F32, kind="ExternalInput").ap()
    w1 = dt("w1", [1024, 4096], F32, kind="ExternalInput").ap()
    w2 = dt("w2", [4096, 1024], F32, kind="ExternalInput").ap()
    b1h = dt("b1h", [128, 32], F32, kind="ExternalInput").ap()
    och = dt("och", [128, 32], F32, kind="ExternalInput").ap()
    cd = _const_drams(nc)
    tg = DBuf(dt("tg", [128, NCH * N], BF16, kind="Internal").ap())
    xnx = DBuf(dt("xnx", [128, NCH * L_TOK], F32, kind="Internal").ap())
    x1 = DBuf(dt("x1o", [128, NCH * L_TOK], F32, kind="ExternalOutput").ap())
    xco = DBuf(dt("xco", [128, NCH * L_TOK], F32, kind="ExternalOutput").ap())
    k = Ker(nc)
    with nc.Block() as block:
        pp = PsumPool(k)
        Bf = make_bufs(k)
        b1o = k.sbuf("b1o", (128, 32), F32)
        oco = k.sbuf("oco", (128, 32), F32)

        @block.tensor
        def _(tensor):
            C = make_consts(k, cd["hcbf"], cd["hcsel"], cd["hconesf"],
                            cd["hcmisc"], cd["hcmask"], cd["hcsc"],
                            cd["hcln"])
            k.dma("pool", b1o[:, :], b1h, key="cst", wr=(b1o,))
            k.dma("pool", oco[:, :], och, key="cst", wr=(oco,))
            emit_outer_block(k, pp, Bf, C, x0w, x0g, xnx, x1, xco, tg,
                             wq, pjh, wo, w1, w2, b1o, oco)
            for t in k.barrier_tokens():
                k.wait("pool", t)
    return nc


def build_layer(attn_lo, ffn_lo):
    nc = bass.Bass("TRN2", target_bir_lowering=False)
    dt = nc.dram_tensor
    xin = DBuf(dt("xin", [128, NCH * L_TOK], F32, kind="ExternalInput").ap())
    xout = DBuf(dt("xout", [128, NCH * L_TOK], F32, kind="ExternalOutput").ap())
    xnx = DBuf(dt("xnx", [128, NCH * L_TOK], F32, kind="Internal").ap())
    wq = dt("wq", [1024, 3072], F32, kind="ExternalInput").ap()
    wo = dt("wo", [1024, 1024], F32, kind="ExternalInput").ap()
    w1 = dt("w1", [1024, 5460], F32, kind="ExternalInput").ap()
    w2 = dt("w2", [DI, 1024], F32, kind="ExternalInput").ap()
    tabh = DBuf(dt("tabh", [128, 4 * L_TOK], F32, kind="ExternalInput").ap())
    bmh = DBuf(dt("bmh", [128, 3072], BF16, kind="ExternalInput").ap())
    b1h = dt("b1h", [128, 44], F32, kind="ExternalInput").ap()
    cd = _const_drams(nc)
    k = Ker(nc)
    with nc.Block() as block:
        pp = PsumPool(k)
        Bf = make_bufs(k)
        b1t = k.sbuf("b1t", (128, 44), F32)

        @block.tensor
        def _(tensor):
            C = make_consts(k, cd["hcbf"], cd["hcsel"], cd["hconesf"],
                            cd["hcmisc"], cd["hcmask"], cd["hcsc"],
                            cd["hcln"])
            k.dma("pool", b1t[:, :], b1h, key="cst", wr=(b1t,))
            for kc in range(NCH):
                k.dma("sp", xout.ap()[:, kc * L_TOK: (kc + 1) * L_TOK],
                      xin.ap()[:, kc * L_TOK: (kc + 1) * L_TOK], key="xw",
                      rd=(xin,), wr=(xout,))
            emit_inner_layer(k, pp, Bf, C, 0, xin, xnx, wq, wo, w1, w2,
                             tabh, bmh, b1t, attn_lo, ffn_lo, D_L,
                             x_wr=xout)
            for t in k.barrier_tokens():
                k.wait("pool", t)
    return nc


def build_final():
    nc = bass.Bass("TRN2", target_bir_lowering=False)
    dt = nc.dram_tensor
    xin = DBuf(dt("xin", [128, NCH * L_TOK], F32, kind="ExternalInput").ap())
    x1h = DBuf(dt("x1h", [128, NCH * L_TOK], F32, kind="ExternalInput").ap())
    out = DBuf(dt("out", [128, NCH * (L_TOK - D_L)], F32,
                  kind="ExternalOutput").ap())
    cd = _const_drams(nc)
    k = Ker(nc)
    with nc.Block() as block:
        pp = PsumPool(k)
        Bf = make_bufs(k)

        @block.tensor
        def _(tensor):
            C = make_consts(k, cd["hcbf"], cd["hcsel"], cd["hconesf"],
                            cd["hcmisc"], cd["hcmask"], cd["hcsc"],
                            cd["hcln"])
            emit_final(k, pp, Bf, C, xin, x1h, out)
            for t in k.barrier_tokens():
                k.wait("pool", t)
    return nc


def _b1t_inner(b1):
    a = np.pad(b1[:DI], (0, 22 * 128 - DI)).reshape(22, 128).T
    g = np.pad(b1[DI:], (0, 22 * 128 - DI)).reshape(22, 128).T
    return np.ascontiguousarray(np.concatenate([a, g], 1)).astype(np.float32)


def _lncols_layer(g, b, b2):
    out = np.zeros((128, 144), np.float32)
    out[:, 0:8] = chunk_cols_np(g)
    out[:, 8:16] = chunk_cols_np(b)
    out[:, 96:104] = chunk_cols_np(b2)
    return out


def kernel_multi(**inputs):
    f32 = lambda a: np.asarray(a, dtype=np.float32)
    x0 = f32(inputs["L0_attn_x"])
    proj = f32(inputs["L0_attn_proj"])
    wqkv_o = f32(inputs["L0_attn_w_qkv"])
    wout_o = f32(inputs["L0_attn_w_out"])
    bout_o = f32(inputs["L0_attn_b_out"])
    gam_o = f32(inputs["L0_attn_gamma"])
    bet_o = f32(inputs["L0_attn_beta"])
    w1_o = f32(inputs["L0_attn_w1"])
    b1_o = f32(inputs["L0_attn_b1"])
    w2_o = f32(inputs["L0_attn_w2"])
    b2_o = f32(inputs["L0_attn_b2"])
    wqkv_i = f32(inputs["L0_ffn_Wqkv"])
    qs_i = f32(inputs["L0_ffn_q_scale"])
    ks_i = f32(inputs["L0_ffn_k_scale"])
    wo_i = f32(inputs["L0_ffn_Wo"])
    lng_i = f32(inputs["L0_ffn_ln_g"])
    lnb_i = f32(inputs["L0_ffn_ln_b"])
    w1_i = f32(inputs["L0_ffn_W1"])
    b1_i = f32(inputs["L0_ffn_b1"])
    w2_i = f32(inputs["L0_ffn_W2"])
    b2_i = f32(inputs["L0_ffn_b2"])

    cbf, csel, conesf = host_const_arrays()
    band = band_masks_np().astype(BF)
    cmisc = []
    for r in range(2):
        cm = np.zeros((128, 8), np.float32)
        cm[:, 0], cm[:, 1], cm[:, 2] = -8.0, 8.0, float(r)
        cm[:, 3], cm[:, 4], cm[:, 5], cm[:, 6] = -1.0, -0.5, EPS_LN, EPS_ATTN
        cmisc.append(cm)
    base_c = {"hcbf": cbf, "hcsel": csel, "hconesf": conesf, "hcmask": band}

    pj = np.zeros((128, 2048), np.float32)
    for c in range(8):
        pj[0:64, 256 * c: 256 * c + 256] = proj[2 * c]
        pj[64:128, 256 * c: 256 * c + 256] = proj[2 * c + 1]
    wqT_o = np.ascontiguousarray(wqkv_o.T)
    woT_o = np.ascontiguousarray(wout_o.T)
    w1T_o = np.ascontiguousarray(w1_o.T)
    w2T_o = np.ascontiguousarray(w2_o.T)
    b1h_o = np.ascontiguousarray(b1_o.reshape(32, 128).T).astype(np.float32)
    och = np.ascontiguousarray(np.concatenate(
        [chunk_cols_np(bout_o), chunk_cols_np(gam_o), chunk_cols_np(bet_o),
         chunk_cols_np(b2_o)], 1)).astype(np.float32)
    zsc = np.zeros((128, 24), np.float32)
    zln = np.zeros((128, 144), np.float32)
    x0w_c, x0g_c = [], []
    for core in range(8):
        b, r = core // 2, core % 2
        if r == 0:
            xw = np.concatenate(
                [np.zeros((D_L, D), np.float32), x0[b, :S_L]], 0)
        else:
            xw = x0[b, ORG_H:]
        x0w_c.append(_fm(xw))
        x0g_c.append(_fm(x0[b]))
    nc_outer = build_outer()
    maps = []
    for core in range(8):
        m = dict(base_c)
        m.update({"x0w": x0w_c[core], "x0g": x0g_c[core], "wq": wqT_o,
                  "hprojT": pj, "wo": woT_o, "w1": w1T_o, "w2": w2T_o,
                  "b1h": b1h_o, "och": och, "hcmisc": cmisc[core % 2],
                  "hcsc": zsc, "hcln": zln})
        maps.append(m)
    res = run_bass_kernel_spmd(nc_outer, maps,
                               core_ids=list(range(8))).results
    x1_c = [res[c]["x1o"] for c in range(8)]
    xc_c = [res[c]["xco"] for c in range(8)]

    tabs_c = []
    for core in range(8):
        org = -D_L if core % 2 == 0 else ORG_H
        tabs_c.append(np.ascontiguousarray(
            np.concatenate(rotary_tables_np(org, L_TOK), 1)))
    ranges = layer_ranges()
    progs = {}
    for li in range(6):
        al, fl = ranges[li]
        if (al, fl) not in progs:
            progs[(al, fl)] = build_layer(al, fl)
        ncl = progs[(al, fl)]
        wq_l = perm_wqkv_np(wqkv_i[li])
        sc_l = np.zeros((128, 24), np.float32)
        sc_l[:, 0:4] = scale_cols_np(qs_i[li], ks_i[li])
        ln_l = _lncols_layer(lng_i[li], lnb_i[li], b2_i[li])
        b1t_l = _b1t_inner(b1_i[li])
        bm_l = [boundary_masks_np(al, light=(core % 2 == 0)).astype(BF)
                for core in range(2)]
        maps = []
        for core in range(8):
            m = dict(base_c)
            m.update({"xin": xc_c[core], "wq": wq_l, "wo": wo_i[li],
                      "w1": w1_i[li], "w2": w2_i[li], "tabh": tabs_c[core],
                      "bmh": bm_l[core % 2], "b1h": b1t_l,
                      "hcmisc": cmisc[core % 2], "hcsc": sc_l,
                      "hcln": ln_l})
            maps.append(m)
        res = run_bass_kernel_spmd(ncl, maps,
                                   core_ids=list(range(8))).results
        xc_c = [res[c]["xout"] for c in range(8)]

    nc_fin = build_final()
    maps = []
    for core in range(8):
        m = dict(base_c)
        m.update({"xin": xc_c[core], "x1h": x1_c[core],
                  "hcmisc": cmisc[core % 2], "hcsc": zsc, "hcln": zln})
        maps.append(m)
    res = run_bass_kernel_spmd(nc_fin, maps,
                               core_ids=list(range(8))).results
    span = L_TOK - D_L
    y = np.empty((B, N, D), np.float32)
    for b in range(B):
        y[b, :S_L] = _defm(np.asarray(res[2 * b]["out"]), span)
        y[b, S_L:] = _defm(np.asarray(res[2 * b + 1]["out"]), span)[512:]
    return y


def emit_w_prep(k, Bf, pairs):
    bufs = [Bf["wv"], Bf["kb"]]
    slot = 0
    for src, dst, R, C in pairs:
        for r0 in range(0, R, 128):
            rn = min(128, R - r0)
            for c0 in range(0, C, 4096):
                cn = min(4096, C - c0)
                t = bufs[slot % 2]
                slot += 1
                k.dma("pool", t[0:rn, 0:cn], src[r0: r0 + rn, c0: c0 + cn],
                      key="wpc" + str(slot % 2), wr=(t,))
                k.dma("sp", dst[r0: r0 + rn, c0: c0 + cn], t[0:rn, 0:cn],
                      key="wps" + str(slot % 2), rd=(t,))


def build_fused():
    nc = bass.Bass("TRN2", target_bir_lowering=False)
    dt = nc.dram_tensor
    x0w = DBuf(dt("x0w", [128, NCH * L_TOK], F32, kind="ExternalInput").ap())
    x0g = DBuf(dt("x0g", [128, NCH * N], F32, kind="ExternalInput").ap())
    wqo = dt("wqo", [1024, 3072], F32, kind="ExternalInput").ap()
    pjh = DBuf(dt("hprojT", [128, 2048], F32, kind="ExternalInput").ap())
    woo = dt("woo", [1024, 1024], F32, kind="ExternalInput").ap()
    w1o = dt("w1o", [1024, 4096], F32, kind="ExternalInput").ap()
    w2o = dt("w2o", [4096, 1024], F32, kind="ExternalInput").ap()
    b1ho = dt("b1ho", [128, 32], F32, kind="ExternalInput").ap()
    och = dt("och", [128, 32], F32, kind="ExternalInput").ap()
    wqi = dt("wqi", [6 * 1024, 3072], F32, kind="ExternalInput").ap()
    woi = dt("woi", [6 * 1024, 1024], F32, kind="ExternalInput").ap()
    w1i = dt("w1i", [6 * 1024, 5460], F32, kind="ExternalInput").ap()
    w2i = dt("w2i", [6 * DI, 1024], F32, kind="ExternalInput").ap()
    b1i = dt("b1i", [128, 6 * 44], F32, kind="ExternalInput").ap()
    tabh = DBuf(dt("tabh", [128, 4 * L_TOK], F32, kind="ExternalInput").ap())
    bmi = dt("bmi", [128, 6 * 3072], BF16, kind="ExternalInput").ap()
    cd = _const_drams(nc)
    tg = DBuf(dt("tg", [128, NCH * N], BF16, kind="Internal").ap())
    x1h = DBuf(dt("x1h", [128, NCH * L_TOK], F32, kind="Internal").ap())
    xc = DBuf(dt("xc", [128, NCH * L_TOK], F32, kind="Internal").ap())
    xn = DBuf(dt("xn_s", [128, NCH * L_TOK], F32, kind="Internal").ap())
    out = DBuf(dt("out", [128, NCH * (L_TOK - D_L)], F32,
                  kind="ExternalOutput").ap())
    wqo_b = dt("wqo_b", [1024, 3072], BF16, kind="Internal").ap()
    woo_b = dt("woo_b", [1024, 1024], BF16, kind="Internal").ap()
    w1o_b = dt("w1o_b", [1024, 4096], BF16, kind="Internal").ap()
    w2o_b = dt("w2o_b", [4096, 1024], BF16, kind="Internal").ap()
    wqi_b = dt("wqi_b", [6 * 1024, 3072], BF16, kind="Internal").ap()
    woi_b = dt("woi_b", [6 * 1024, 1024], BF16, kind="Internal").ap()
    w1i_b = dt("w1i_b", [6 * 1024, 5460], BF16, kind="Internal").ap()
    w2i_b = dt("w2i_b", [6 * DI, 1024], BF16, kind="Internal").ap()
    tab_b = dt("tab_b", [128, 4 * L_TOK], BF16, kind="Internal").ap()
    k = Ker(nc)
    with nc.Block() as block:
        pp = PsumPool(k)
        Bf = make_bufs(k)
        b1o = k.sbuf("b1o", (128, 32), F32)
        oco = k.sbuf("oco", (128, 32), F32)
        b1t = k.sbuf("b1t", (128, 44), F32)

        @block.tensor
        def _(tensor):
            C = make_consts(k, cd["hcbf"], cd["hcsel"], cd["hconesf"],
                            cd["hcmisc"], cd["hcmask"], cd["hcsc"],
                            cd["hcln"])
            k.dma("pool", b1o[:, :], b1ho, key="cst", wr=(b1o,))
            k.dma("pool", oco[:, :], och, key="cst", wr=(oco,))
            emit_w_prep(k, Bf, [
                (wqo, wqo_b, 1024, 3072), (woo, woo_b, 1024, 1024),
                (w1o, w1o_b, 1024, 4096), (w2o, w2o_b, 4096, 1024),
                (tabh.ap(), tab_b, 128, 4 * L_TOK),
                (wqi, wqi_b, 6 * 1024, 3072), (woi, woi_b, 6 * 1024, 1024),
                (w1i, w1i_b, 6 * 1024, 5460), (w2i, w2i_b, 6 * DI, 1024),
            ])
            tab_bd = DBuf(tab_b)
            emit_outer_block(k, pp, Bf, C, x0w, x0g, xn, x1h, xc, tg,
                             wqo_b, pjh, woo_b, w1o_b, w2o_b, b1o, oco)
            ranges = layer_ranges()
            for li in range(6):
                al, fl = ranges[li]
                k.dma("pool", b1t[:, :], b1i[:, 44 * li: 44 * li + 44],
                      key="cst", wr=(b1t,))
                bm_l = DBuf(bmi[:, 3072 * li: 3072 * li + 3072])
                emit_inner_layer(
                    k, pp, Bf, C, li, xc, xn,
                    wqi_b[1024 * li: 1024 * li + 1024, :],
                    woi_b[1024 * li: 1024 * li + 1024, :],
                    w1i_b[1024 * li: 1024 * li + 1024, :],
                    w2i_b[DI * li: DI * li + DI, :],
                    tab_bd, bm_l, b1t, al, fl, D_L)
            emit_final(k, pp, Bf, C, xc, x1h, out)
            for t in k.barrier_tokens():
                k.wait("pool", t)
    return nc


def _lncols_all(lng, lnb, b2):
    out = np.zeros((128, 144), np.float32)
    for li in range(6):
        out[:, 16 * li: 16 * li + 8] = chunk_cols_np(lng[li])
        out[:, 16 * li + 8: 16 * li + 16] = chunk_cols_np(lnb[li])
        out[:, 96 + 8 * li: 104 + 8 * li] = chunk_cols_np(b2[li])
    return out


def kernel_fused(inputs):
    f32 = lambda a: np.asarray(a, dtype=np.float32)
    x0 = f32(inputs["L0_attn_x"])
    proj = f32(inputs["L0_attn_proj"])
    wqkv_o = f32(inputs["L0_attn_w_qkv"])
    wqkv_i = f32(inputs["L0_ffn_Wqkv"])
    qs_i = f32(inputs["L0_ffn_q_scale"])
    ks_i = f32(inputs["L0_ffn_k_scale"])

    cbf, csel, conesf = host_const_arrays()
    band = band_masks_np().astype(BF)
    cmisc = []
    for r in range(2):
        cm = np.zeros((128, 8), np.float32)
        cm[:, 0], cm[:, 1], cm[:, 2] = -8.0, 8.0, float(r)
        cm[:, 3], cm[:, 4], cm[:, 5], cm[:, 6] = -1.0, -0.5, EPS_LN, EPS_ATTN
        cmisc.append(cm)

    pj = np.zeros((128, 2048), np.float32)
    for c in range(8):
        pj[0:64, 256 * c: 256 * c + 256] = proj[2 * c]
        pj[64:128, 256 * c: 256 * c + 256] = proj[2 * c + 1]
    sc_all = np.zeros((128, 24), np.float32)
    for li in range(6):
        sc_all[:, 4 * li: 4 * li + 4] = scale_cols_np(qs_i[li], ks_i[li])
    ln_all = _lncols_all(f32(inputs["L0_ffn_ln_g"]),
                         f32(inputs["L0_ffn_ln_b"]),
                         f32(inputs["L0_ffn_b2"]))
    b1all = np.concatenate(
        [_b1t_inner(f32(inputs["L0_ffn_b1"])[li]) for li in range(6)], 1)
    b1all = np.ascontiguousarray(b1all)
    ranges = layer_ranges()
    bmi_r = []
    for r in range(2):
        bmi_r.append(np.ascontiguousarray(np.concatenate(
            [boundary_masks_np(ranges[li][0], light=(r == 0))
             for li in range(6)], 1)).astype(BF))
    wqi = np.concatenate([perm_wqkv_np(wqkv_i[li]) for li in range(6)], 0)
    woi = f32(inputs["L0_ffn_Wo"]).reshape(6 * 1024, 1024)
    w1i = f32(inputs["L0_ffn_W1"]).reshape(6 * 1024, 5460)
    w2i = f32(inputs["L0_ffn_W2"]).reshape(6 * DI, 1024)

    base = {
        "hcbf": cbf, "hcsel": csel, "hconesf": conesf, "hcmask": band,
        "hcsc": sc_all, "hcln": ln_all,
        "wqo": np.ascontiguousarray(wqkv_o.T), "hprojT": pj,
        "woo": np.ascontiguousarray(f32(inputs["L0_attn_w_out"]).T),
        "w1o": np.ascontiguousarray(f32(inputs["L0_attn_w1"]).T),
        "w2o": np.ascontiguousarray(f32(inputs["L0_attn_w2"]).T),
        "b1ho": np.ascontiguousarray(
            f32(inputs["L0_attn_b1"]).reshape(32, 128).T),
        "och": np.ascontiguousarray(np.concatenate(
            [chunk_cols_np(f32(inputs["L0_attn_b_out"])),
             chunk_cols_np(f32(inputs["L0_attn_gamma"])),
             chunk_cols_np(f32(inputs["L0_attn_beta"])),
             chunk_cols_np(f32(inputs["L0_attn_b2"]))], 1)),
        "wqi": wqi, "woi": woi, "w1i": w1i, "w2i": w2i, "b1i": b1all,
    }
    maps = []
    for core in range(8):
        b, r = core // 2, core % 2
        xw = (np.concatenate([np.zeros((D_L, D), np.float32),
                              x0[b, :S_L]], 0)
              if r == 0 else x0[b, ORG_H:])
        org = -D_L if r == 0 else ORG_H
        m = dict(base)
        m.update({
            "x0w": _fm(xw), "x0g": _fm(x0[b]),
            "tabh": np.ascontiguousarray(
                np.concatenate(rotary_tables_np(org, L_TOK), 1)),
            "bmi": bmi_r[r], "hcmisc": cmisc[r],
        })
        maps.append(m)
    nc = build_fused()
    res = run_bass_kernel_spmd(nc, maps, core_ids=list(range(8))).results
    span = L_TOK - D_L
    y = np.empty((B, N, D), np.float32)
    for b in range(B):
        y[b, :S_L] = _defm(np.asarray(res[2 * b]["out"]), span)
        y[b, S_L:] = _defm(np.asarray(res[2 * b + 1]["out"]), span)[512:]
    return y


def kernel(**inputs):
    return kernel_fused(inputs)
